# version 41
# baseline (speedup 1.0000x reference)
import numpy as np
import ml_dtypes
from contextlib import ExitStack

import concourse.bass as bass
import concourse.mybir as mybir
from concourse.bass_utils import run_bass_kernel_spmd

F32 = mybir.dt.float32
BF16 = mybir.dt.bfloat16
AF = mybir.ActivationFunctionType
ALU = mybir.AluOpType
AX = mybir.AxisListType

ENG_NAMES = ("sync", "gpsimd", "vector", "scalar", "tensor")


def full_cfg():
    return dict(S=8192, NQ=2048, D=1024, H=16, F=4096, CKV=512, QTS=512,
                WTC=1024)


class Prog:

    def __init__(self, nc, es):
        self.nc = nc
        self.es = es
        self.sems = {}
        self.cnt = {}
        self.stream = {e: [] for e in ENG_NAMES}
        self.lastw = {e: {} for e in ENG_NAMES}

    def sem(self, name):
        if name not in self.sems:
            self.sems[name] = self.es.enter_context(self.nc.semaphore(name))
            self.cnt[name] = 0
        return self.sems[name]

    def tok(self, name):
        self.sem(name)
        return self.cnt[name]

    def wait(self, engname, semname, val):
        if val is None or val <= 0:
            return
        s = self.sem(semname)
        lw = self.lastw[engname]
        if lw.get(semname, 0) >= val:
            return
        lw[semname] = val
        self.stream[engname].append(lambda eng, s=s, v=val: eng.wait_ge(s, v))

    def op(self, engname, build, inc=None, waits=()):
        for (sn, v) in waits:
            self.wait(engname, sn, v)
        incs = []
        tok = None
        if inc is not None:
            sn, n = inc
            self.sem(sn)
            self.cnt[sn] += n
            tok = self.cnt[sn]
            incs.append((self.sems[sn], n))

        def emit(eng, build=build, incs=incs):
            ins = build(eng)
            for (s, n) in incs:
                ins = ins.then_inc(s, n)

        self.stream[engname].append(emit)
        return tok

    def fence(self, exclude=("g", "gwa0", "gwa1")):
        for sn in list(self.cnt.keys()):
            if sn in exclude:
                continue
            v = self.cnt[sn]
            for e in ENG_NAMES:
                self.wait(e, sn, v)

    def emit_all(self, blk):
        for name in ENG_NAMES:
            fns = self.stream[name]

            def runner(eng, fns=fns):
                for fn in fns:
                    fn(eng)

            getattr(blk, name)(runner)


def build_nc(cfg):
    S, NQ, D, H, F = cfg["S"], cfg["NQ"], cfg["D"], cfg["H"], cfg["F"]
    CKV, QTS, WTC = cfg["CKV"], cfg["QTS"], cfg["WTC"]
    G = D // 128
    FG = F // 128
    KV = S // 2
    KVG = KV // 128
    NCH = KV // CKV
    NQT = NQ // QTS
    QSUB = QTS // 128
    DH = D // H
    assert DH == 64
    SCALE = 1.0 / float(np.sqrt(DH))
    XT = S // 128
    XOT = NQ // 128
    CS = 2 * CKV
    CW = CS + 128
    NJ = CKV // 128
    VROW = H * 65

    nc = bass.Bass("TRN2", target_bir_lowering=False)

    def din(name, shape):
        return nc.dram_tensor(name, shape, F32, kind="ExternalInput")

    x_own = din("x_own", [NQ, D])
    x_ctx = din("x_ctx", [S, D])
    q_w = din("q_w", [D, D]); q_b = din("q_b", [D, 1])
    k_w = din("k_w", [D, D]); k_b = din("k_b", [D, 1])
    v_w = din("v_w", [D, D]); v_b = din("v_b", [D, 1])
    out_w = din("out_w", [D, D]); out_b = din("out_b", [1, D])
    conv_w = din("conv_w", [D, 3]); conv_b = din("conv_b", [D, 1])
    w_up = din("w_up", [F, D])
    w_down = din("w_down", [D, F])
    y = nc.dram_tensor("y", [NQ, D], F32, kind="ExternalOutput")

    ident_in = nc.dram_tensor("ident", [128, 128], BF16, kind="ExternalInput")

    xn_fm_sc = nc.dram_tensor("xn_fm_sc", [D, S], BF16)
    xn_own_fm_sc = nc.dram_tensor("xn_own_fm_sc", [D, NQ], BF16)
    r1_sc = nc.dram_tensor("r1_sc", [NQ, 1], F32)
    wq_bf = nc.dram_tensor("wq_bf", [D, D], BF16)
    wk_bf = nc.dram_tensor("wk_bf", [D, D], BF16)
    wv_bf = nc.dram_tensor("wv_bf", [D, D], BF16)
    wo_bf = nc.dram_tensor("wo_bf", [D, D], BF16)
    kt_sc = nc.dram_tensor("kt_sc", [D, KV], BF16)
    vp_sc = nc.dram_tensor("vp_sc", [KV, VROW], BF16)
    wup_t_sc = nc.dram_tensor("wup_t_sc", [F, D], BF16)
    wdn_t_sc = nc.dram_tensor("wdn_t_sc", [D, F], BF16)
    wupT_sc = nc.dram_tensor("wupT_sc", [D, F], BF16)
    wdnT_sc = nc.dram_tensor("wdnT_sc", [F, D], BF16)
    h_sc = nc.dram_tensor("h_sc", [F, NQ], BF16)

    es = ExitStack()
    P = Prog(nc, es)

    def sb(name, shape, dt, scope=es):
        return scope.enter_context(nc.sbuf_tensor(name, shape, dt))

    qb_sb = sb("qb_sb", [128, G], F32)
    kb_sb = sb("kb_sb", [128, G], F32)
    vb_sb = sb("vb_sb", [128, G], F32)
    cb_sb = sb("cb_sb", [128, G], F32)
    cw_sb = sb("cw_sb", [128, 3 * G], F32)
    ones_bf = sb("ones_bf", [128, 1], BF16)
    ones_f = sb("ones_f", [128, 1], F32)
    ones_row_f = sb("ones_row_f", [1, 128], F32)
    ones_row_b = sb("ones_row_b", [1, 128], BF16)
    ob_row = sb("ob_row", [1, D], F32)
    obc = sb("obc", [128, D], F32)
    den = sb("den", [1, 1024], F32)
    rec = sb("rec", [1, 1024], F32)
    recb = sb("recb", [1, 1024], BF16)
    bcf = sb("bcf", [128, QTS], F32)
    ubuf = [sb(f"ubuf{i}", [128, D], F32) for i in range(2)]
    ysb = [sb(f"ysb{i}", [128, D], F32) for i in range(2)]
    r2t = sb("r2t", [128, 2 * QSUB], F32)
    thr_t = [sb(f"thr{i}", [128, 1], F32) for i in range(2)]
    nthr_t = [sb(f"nthr{i}", [128, 1], F32) for i in range(2)]
    WTCA = 2048
    asum_p = [sb(f"asum_p{i}", [128, 1], F32) for i in range(2)]
    rsum = [sb(f"rsum{i}", [128, 1], F32) for i in range(2)]
    epsln = sb("epsln", [128, 1], F32)
    ident_sb = sb("ident_sb", [128, 128], BF16)
    scopeW = ExitStack()
    wbuf = [sb(f"wbuf{i}", [128, WTCA], F32, scopeW) for i in range(2)]
    absb2 = [sb(f"absb2{i}", [128, WTCA], BF16, scopeW) for i in range(3)]

    pW3 = es.enter_context(nc.psum_tensor("pw3", [128, 1024], F32))
    pG, pH = pW3[:, 0:512], pW3[:, 512:1024]
    pAL = [pG, pH]
    scopeT = ExitStack()
    pT = [scopeT.enter_context(nc.psum_tensor(f"pt{i}", [128, 1024], BF16))
          for i in range(2)]

    for (src, dst) in ((q_w, wq_bf), (k_w, wk_bf), (v_w, wv_bf), (out_w, wo_bf)):
        P.op("gpsimd", lambda g, s=src, d=dst:
             g.dma_start(out=d[:, :], in_=s[:, :]), inc=("gcast", 16))
    t_cast = P.tok("gcast")
    for (vec, tile) in ((q_b, qb_sb), (k_b, kb_sb), (v_b, vb_sb), (conv_b, cb_sb)):
        for g in range(G):
            P.op("sync", lambda e, v=vec, t=tile, g=g:
                 e.dma_start(out=t[:, g:g + 1], in_=v[g * 128:(g + 1) * 128, 0:1]),
                 inc=("dsm", 16))
    for g in range(G):
        P.op("sync", lambda e, g=g: e.dma_start(
            out=cw_sb[:, g * 3:(g + 1) * 3], in_=conv_w[g * 128:(g + 1) * 128, :]),
            inc=("dsm", 16))
    P.op("sync", lambda e: e.dma_start(out=ob_row[:, :], in_=out_b[0:1, :]),
         inc=("dsm", 16))
    P.op("sync", lambda e: e.dma_start(out=ident_sb[:, :], in_=ident_in[:, :]),
         inc=("dsm", 16))
    t_dsm = P.tok("dsm")

    P.op("vector", lambda v: v.memset(ones_bf[:, :], 1.0))
    P.op("vector", lambda v: v.memset(ones_f[:, :], 1.0))
    P.op("vector", lambda v: v.memset(ones_row_f[:, :], 1.0))
    P.op("vector", lambda v: v.memset(ones_row_b[:, :], 1.0))
    P.op("vector", lambda v: v.memset(epsln[:, :], 1e-5), inc=("v", 1))
    t_const = P.tok("v")

    t1 = P.op("tensor", lambda t: t.matmul(pG[:, 0:512], ones_row_f[:, :],
                                           ob_row[0:1, 0:512], start=True, stop=True),
              inc=("t", 1), waits=[("dsm", t_dsm), ("v", t_const)])
    t2 = P.op("tensor", lambda t: t.matmul(pH[:, 0:512], ones_row_f[:, :],
                                           ob_row[0:1, 512:1024], start=True, stop=True),
              inc=("t", 1))
    P.op("vector", lambda v: v.tensor_copy(obc[:, 0:512], pG[:, 0:512]),
         waits=[("t", t1)])
    t_obc = P.op("vector", lambda v: v.tensor_copy(obc[:, 512:1024], pH[:, 0:512]),
                 inc=("v", 1), waits=[("t", t2)])

    ga_sem = ["gwa0", "gwa1"]
    abs_tok = [0, 0]
    pe_alpha_tok = [0, 0, 0]
    alpha_jobs = []
    for wi, wsrc in enumerate((w_up, w_down)):
        nt = len(wtiles(wsrc, WTCA))
        for it in range(nt):
            alpha_jobs.append((wi, wsrc, it, it == 0, it == nt - 1))
    n_alpha = len(alpha_jobs)
    alpha_done_tok = []

    aq = []

    def emit_alpha_mm(item):
        j, ta = item
        wi, wsrc, it, first, last = alpha_jobs[j]
        ai = j % 3
        nsub = WTCA // 512
        for k in range(nsub):
            wm = [("s", ta)] if k == 0 else []
            if first and k == 0:
                wm.append(("v", t_obc))
            P.op("tensor", lambda t, a=ai, k=k, wi=wi: t.matmul(
                pAL[wi][0:1, 0:512], ones_bf[:, :],
                absb2[a][:, k * 512:(k + 1) * 512],
                start=(first and k == 0), stop=(last and k == nsub - 1)),
                waits=wm, inc=("t", 1) if k == nsub - 1 else None)
        pe_alpha_tok[ai] = P.tok("t")
        if last:
            P.op("vector", lambda v, wi=wi: v.reduce_sum(
                asum_p[wi][0:1, 0:1], pAL[wi][0:1, 0:512], axis=AX.X),
                inc=("v", 1), waits=[("t", pe_alpha_tok[ai])])
            alpha_done_tok.append(P.tok("v"))

    def emit_alpha_job(j):
        wi, wsrc, it, first, last = alpha_jobs[j]
        bi = j % 2
        ai = j % 3
        P.op("sync", lambda g, it=it, b=bi, w=wsrc:
             g.dma_start(out=bass.AP(wbuf[b], 0, wtiles(w, WTCA)[it][1]),
                         in_=wtiles(w, WTCA)[it][0]),
             inc=(ga_sem[bi], 16), waits=[("s", abs_tok[bi])])
        tld = P.tok(ga_sem[bi])
        abs_tok[bi] = P.op("scalar", lambda s, b=bi, a=ai: s.activation(
            absb2[a][:, :], wbuf[b][:, :], AF.Abs),
            inc=("s", 1), waits=[(ga_sem[bi], tld), ("t", pe_alpha_tok[ai])])
        aq.append((j, abs_tok[bi]))
        if len(aq) > 2:
            emit_alpha_mm(aq.pop(0))

    NB = 4
    with ExitStack() as scope1:
        xbuf4 = [sb(f"xbuf4{i}", [128, NB * D], F32, scope1) for i in range(3)]
        nbuf4 = [sb(f"nbuf4{i}", [128, NB * D], BF16, scope1) for i in range(3)]
        fmb = [sb(f"fmb{i}", [128, G * NB * 128], BF16, scope1) for i in range(2)]
        stat2 = [sb(f"stat{i}", [128, 12], F32, scope1) for i in range(3)]
        mv2 = [sb(f"mv{i}", [128, 2 * NB], F32, scope1) for i in range(3)]
        lnt2 = [sb(f"lnt{i}", [128, 3 * NB], F32, scope1) for i in range(3)]
        sqt2 = [sb(f"sqt{i}", [128, 3 * NB], F32, scope1) for i in range(3)]

        ln_jobs = [(x_ctx, xn_fm_sc, j * NB * 128) for j in range(XT // NB)] + \
                  [(x_own, xn_own_fm_sc, j * NB * 128) for j in range(XOT // NB)]
        aj = [0]
        norm_tok = [0, 0, 0]
        aggr2 = [0, 0, 0]
        tp_tok = [0, 0, 0]
        ev_tok = [0, 0]
        fst_tok = [0, 0]
        r1st_tok = [0, 0, 0]
        nmr_tok = 0
        xl_sem = ["dl0", "dl1", "dl2"]
        fm_sems = ["dfm0", "dfm1"]
        for i, (src_, dst, r0) in enumerate(ln_jobs):
            bi = i % 3
            fi = i % 2
            stat, mv, lnt = stat2[bi], mv2[bi], lnt2[bi]
            aggr_tok = aggr2[bi]
            wl = []
            if i >= 3:
                wl += [("s", norm_tok[bi])]
            P.op("sync", lambda e, s=src_, r0=r0, b=bi: e.dma_start(
                out=xbuf4[b][:, 0:NB * D],
                in_=s[r0:r0 + NB * 128, :].rearrange("(a p) c -> p a c", p=128)),
                inc=(xl_sem[bi], 16), waits=wl)
            tld = P.tok(xl_sem[bi])
            for sub in range(NB):
                P.op("vector", lambda v, b=bi, sub=sub, stat=stat: v.bn_stats(
                    stat[:, 0:6], xbuf4[b][:, sub * D: sub * D + 512]),
                    waits=[(xl_sem[bi], tld), ("v", aggr_tok),
                           ("s", norm_tok[bi])])
                P.op("vector", lambda v, b=bi, sub=sub, stat=stat: v.bn_stats(
                    stat[:, 6:12], xbuf4[b][:, sub * D + 512: sub * D + 1024]),
                    inc=("v", 1))
                ts_ = P.tok("v")
                aggr_tok = P.op("vector", lambda v, sub=sub, stat=stat, mv=mv:
                                v.bn_aggr(mv[:, sub * 2:sub * 2 + 2], stat[:, :]),
                                inc=("v", 1), waits=[("v", ts_)])
            aggr2[bi] = aggr_tok
            P.op("scalar", lambda s, mv=mv, lnt=lnt: s.activation(
                lnt[:, 0:NB], mv[:, 1:2 * NB:2], AF.Ln,
                bias=epsln[:, 0:1], scale=1.0),
                inc=("s", 1), waits=[("v", aggr_tok), ("v", t_const)])
            tl1 = P.tok("s")
            P.op("scalar", lambda s, lnt=lnt: s.activation(
                lnt[:, NB:2 * NB], lnt[:, 0:NB], AF.Exp, bias=0.0, scale=-0.5),
                inc=("s", 1), waits=[("s", tl1)])
            tl2 = P.tok("s")
            nmr_tok = P.op("vector", lambda v, mv=mv, lnt=lnt:
                           v.scalar_tensor_tensor(
                               lnt[:, 2 * NB:3 * NB], mv[:, 0:2 * NB:2], -1.0,
                               lnt[:, NB:2 * NB], ALU.mult, ALU.mult),
                           inc=("v", 1), waits=[("s", tl2)])
            if dst is xn_own_fm_sc:
                sq = sqt2[bi]
                P.op("scalar", lambda s, mv=mv, sq=sq: s.activation(
                    sq[:, 0:NB], mv[:, 1:2 * NB:2], AF.Sqrt),
                    inc=("s", 1), waits=[("s", tl2), ("dr1", r1st_tok[bi])])
                tsq = P.tok("s")
                P.op("vector", lambda v, lnt=lnt, sq=sq: v.tensor_mul(
                    sq[:, NB:2 * NB], sq[:, 0:NB], lnt[:, NB:2 * NB]),
                    waits=[("s", tsq)])
                P.op("vector", lambda v, sq=sq: v.tensor_scalar(
                    sq[:, NB:2 * NB], sq[:, NB:2 * NB], 1e-8, None, ALU.add))
                P.op("vector", lambda v, sq=sq: v.reciprocal(
                    sq[:, 2 * NB:3 * NB], sq[:, NB:2 * NB]), inc=("v", 1))
                tr1 = P.tok("v")
                P.op("gpsimd", lambda e, r0=r0, sq=sq: e.dma_start(
                    out=r1_sc[r0:r0 + NB * 128, 0:1].rearrange(
                        "(a p) c -> p a c", p=128),
                    in_=sq[:, 2 * NB:3 * NB],
                    allow_slow_non_contiguous=True),
                    inc=("dr1", 16), waits=[("v", tr1)])
                r1st_tok[bi] = P.tok("dr1")
            for sub in range(NB):
                P.op("scalar", lambda s, b=bi, sub=sub, lnt=lnt:
                     s.activation(
                         nbuf4[b][:, sub * D:(sub + 1) * D],
                         xbuf4[b][:, sub * D:(sub + 1) * D],
                         AF.Identity,
                         bias=lnt[:, 2 * NB + sub:2 * NB + sub + 1],
                         scale=lnt[:, NB + sub:NB + sub + 1]),
                     waits=[("v", nmr_tok), ("t", tp_tok[bi])],
                     inc=("s", 1) if sub == NB - 1 else None)
            norm_tok[bi] = P.tok("s")
            for sub in range(NB):
                tb = (i * NB + sub) % 2
                for g in range(G):
                    wt = []
                    if g == 0:
                        wt = [("s", norm_tok[bi]), ("v", ev_tok[tb]),
                              ("dsm", t_dsm)]
                    P.op("tensor", lambda t, b=bi, sub=sub, g=g, tb=tb:
                         t.transpose(
                             pT[tb][:, g * 128:(g + 1) * 128],
                             nbuf4[b][:, sub * D + g * 128: sub * D + g * 128 + 128],
                             ident_sb[:, :]),
                         waits=wt, inc=("t", 1) if g == G - 1 else None)
                ttp = P.tok("t")
                we = [("t", ttp)]
                if sub == 0:
                    we.append((fm_sems[fi], fst_tok[fi]))
                ev_tok[tb] = P.op("vector", lambda v, b=fi, sub=sub, tb=tb:
                                  v.tensor_copy(
                                      bass.AP(fmb[b], sub * 128,
                                              [[G * NB * 128, 128],
                                               [NB * 128, G], [1, 128]]),
                                      pT[tb][:, 0:G * 128]),
                                  inc=("v", 1), waits=we)
            tp_tok[bi] = P.tok("t")
            ev_all = P.tok("v")
            P.op("gpsimd", lambda e, d=dst, r0=r0, b=fi: e.dma_start(
                out=d[0:D, r0:r0 + NB * 128].rearrange("(g p) c -> p g c", p=128),
                in_=fmb[b][:, :]),
                inc=(fm_sems[fi], 16), waits=[("v", ev_all)])
            fst_tok[fi] = P.tok(fm_sems[fi])
            while aj[0] < min(n_alpha, (i + 1) * 2):
                emit_alpha_job(aj[0])
                aj[0] += 1
        while aj[0] < n_alpha:
            emit_alpha_job(aj[0])
            aj[0] += 1
        while aq:
            emit_alpha_mm(aq.pop(0))
        t_lnfm = [("dfm0", P.tok("dfm0")), ("dfm1", P.tok("dfm1"))]
        t_r1done = ("dr1", P.tok("dr1"))
    t_p1_s = P.tok("s")
    t_p1_v = P.tok("v")
    scopeT.close()
    pW = [es.enter_context(nc.psum_tensor(f"pw{i}", [128, 1024], F32))
          for i in range(3)] + [pW3]
    ps = [pW[i // 2][:, (i % 2) * 512:(i % 2) * 512 + 512] for i in range(8)]
    pE, pF = ps[4], ps[5]

    P.fence()
    with ExitStack() as scope3:
        kwT = sb("kwT", [128, G * D], BF16, scope3)
        vwT = sb("vwT", [128, G * D], BF16, scope3)
        xnc = [sb(f"xnc{i}", [128, G * CW], BF16, scope3) for i in range(2)]
        xdT = [sb(f"xdT{i}", [128, G * CKV], BF16, scope3) for i in range(2)]
        cacc = [sb(f"cacc{i}", [128, CKV], F32, scope3) for i in range(2)]
        ktb = [sb(f"ktb{i}", [128, CKV], BF16, scope3) for i in range(2)]
        vpb = [sb(f"vpb{i}", [128, VROW], BF16, scope3) for i in range(2)]

        P.wait("sync", "gcast", t_cast)
        for (wsc, tile) in ((wk_bf, kwT), (wv_bf, vwT)):
            for g in range(G):
                P.op("sync", lambda e, w=wsc, t=tile, g=g: e.dma_start(
                    out=t[:, g * D:(g + 1) * D],
                    in_=w[0:D, g * 128:(g + 1) * 128], transpose=True),
                    inc=("dwt", 16))
        t_kvw = P.tok("dwt")

        P.op("vector", lambda v: v.memset(
            bass.AP(xnc[0], 0, [[xnc[0].shape[1], 128], [CW, G], [1, 128]]), 0.0))
        P.op("vector", lambda v: v.memset(
            bass.AP(vpb[0], 64, [[vpb[0].shape[1], 128], [65, H], [1, 1]]), 1.0))
        t_zp = P.op("vector", lambda v: v.memset(
            bass.AP(vpb[1], 64, [[vpb[1].shape[1], 128], [65, H], [1, 1]]), 1.0),
            inc=("v", 1))

        conv_tok = [0, 0]
        kvmm_tok = [0, 0]
        kevac = [0] * 8
        kst_tok = [0, 0]
        vst_tok = [0, 0]
        tr_sem = ["dc0", "dc1"]
        def emit_loads_conv(c):
            bi = c % 2
            lo = c * CS - 128
            wl = list(t_lnfm)
            if c == 0:
                wl.append(("v", t_zp))
            if c >= 2:
                wl.append(("v", conv_tok[bi]))
            if c == 0:
                P.op("sync", lambda e: e.dma_start(
                    out=bass.AP(xnc[0], 128,
                                [[G * CW, 128], [CW, G], [1, CS]]),
                    in_=xn_fm_sc[0:D, 0:CS].rearrange(
                        "(g p) c -> p g c", p=128)),
                    inc=(tr_sem[bi], 16), waits=wl)
            else:
                P.op("sync", lambda e, lo=lo, b=bi: e.dma_start(
                    out=bass.AP(xnc[b], 0,
                                [[G * CW, 128], [CW, G], [1, CW]]),
                    in_=xn_fm_sc[0:D, lo:lo + CW].rearrange(
                        "(g p) c -> p g c", p=128)),
                    inc=(tr_sem[bi], 16), waits=wl)
            t_tr = P.tok(tr_sem[bi])
            wc = [(tr_sem[bi], t_tr), ("dsm", t_dsm)]
            if c >= 2:
                wc.append(("t", kvmm_tok[bi]))
            prev_g = P.tok("v")
            for g in range(G):
                def xg(off, g=g, b=bi):
                    return bass.AP(xnc[b], g * CW + off,
                                   [[xnc[b].shape[1], 128], [2, CKV]])
                prev_g = P.op("vector", lambda p, g=g, b=bi, xg=xg: p.tensor_scalar(
                    cacc[b][:, :], xg(127), cw_sb[:, g * 3:g * 3 + 1],
                    cb_sb[:, g:g + 1], ALU.mult, ALU.add),
                    inc=("v", 1),
                    waits=(wc if g == 0 else []) + [("v", prev_g)])
                prev_g = P.op("vector", lambda p, g=g, b=bi, xg=xg: p.scalar_tensor_tensor(
                    cacc[1 - b][:, :], xg(128), cw_sb[:, g * 3 + 1:g * 3 + 2],
                    cacc[b][:, :], ALU.mult, ALU.add),
                    inc=("v", 1), waits=[("v", prev_g)])
                prev_g = P.op("vector", lambda p, g=g, b=bi, xg=xg: p.scalar_tensor_tensor(
                    xdT[b][:, g * CKV:(g + 1) * CKV], xg(129),
                    cw_sb[:, g * 3 + 2:g * 3 + 3], cacc[1 - b][:, :],
                    ALU.mult, ALU.add),
                    inc=("v", 1), waits=[("v", prev_g)])
            conv_tok[bi] = P.tok("v")

        def emit_kv(c):
            bi = c % 2
            for og in range(G):
                pb = ps[og]
                wme = [("v", conv_tok[bi]), ("dwt", t_kvw), ("v", kevac[og])]
                for ic in range(G):
                    P.op("tensor", lambda t, og=og, ic=ic, b=bi, pb=pb: t.matmul(
                        pb[:, 0:CKV],
                        kwT[:, ic * D + og * 128: ic * D + og * 128 + 128],
                        xdT[b][:, ic * CKV:(ic + 1) * CKV],
                        start=(ic == 0), stop=(ic == G - 1)),
                        waits=wme if ic == 0 else (),
                        inc=("t", 1) if ic == G - 1 else None)
                tmm = P.tok("t")
                we = [("t", tmm), (["dkta", "dktb"][og % 2], kst_tok[og % 2])]
                tv = P.op("vector", lambda v, og=og, pb=pb: v.tensor_scalar(
                    ktb[og % 2][:, :], pb[:, 0:CKV], kb_sb[:, og:og + 1],
                    None, ALU.add), inc=("v", 1), waits=we)
                kevac[og] = tv
                ks = ["dkta", "dktb"][og % 2]
                P.op("gpsimd", lambda e, og=og, c=c: e.dma_start(
                    out=kt_sc[og * 128:(og + 1) * 128, c * CKV:(c + 1) * CKV],
                    in_=ktb[og % 2][:, :]), inc=(ks, 16), waits=[("v", tv)])
                kst_tok[og % 2] = P.tok(ks)

            for jg in range(NJ):
                for half in range(2):
                    pb = ps[jg * 2 + half]
                    wme = [("v", kevac[jg * 2 + half])]
                    for ic in range(G):
                        P.op("tensor", lambda t, jg=jg, half=half, ic=ic, b=bi, pb=pb:
                             t.matmul(
                                 pb[:, 0:512],
                                 xdT[b][:, ic * CKV + jg * 128: ic * CKV + jg * 128 + 128],
                                 vwT[:, ic * D + half * 512: ic * D + half * 512 + 512],
                                 start=(ic == 0), stop=(ic == G - 1)),
                             waits=wme if ic == 0 else (),
                             inc=("t", 1) if ic == G - 1 else None)
                    tmm = P.tok("t")
                    if jg == NJ - 1 and half == 1:
                        kvmm_tok[bi] = tmm
                    kvg = c * NJ + jg
                    we = [("t", tmm)]
                    if half == 0:
                        we.append((["dvpa", "dvpb"][kvg % 2], vst_tok[kvg % 2]))
                    tv = P.op("vector", lambda v, kvg=kvg, half=half, pb=pb:
                              v.tensor_copy(
                                  bass.AP(vpb[kvg % 2], half * 8 * 65,
                                          [[vpb[0].shape[1], 128], [65, 8], [1, 64]]),
                                  bass.AP(pb.tensor, pb.offset,
                                          [[pb.tensor.shape[1], 128], [64, 8], [1, 64]])),
                              inc=("v", 1), waits=we)
                    kevac[jg * 2 + half] = tv
                    if half == 1:
                        vs = ["dvpa", "dvpb"][kvg % 2]
                        P.op("gpsimd", lambda e, kvg=kvg: e.dma_start(
                            out=vp_sc[kvg * 128:(kvg + 1) * 128, :],
                            in_=vpb[kvg % 2][:, :]), inc=(vs, 16),
                            waits=[("v", tv)])
                        vst_tok[kvg % 2] = P.tok(vs)

        for c in range(NCH):
            emit_loads_conv(c)
            if c >= 1:
                emit_kv(c - 1)
        emit_kv(NCH - 1)
        t_kt_done = [("dkta", P.tok("dkta")), ("dktb", P.tok("dktb"))]
        t_vp_done = [("dvpa", P.tok("dvpa")), ("dvpb", P.tok("dvpb"))]
        t_p3_t = P.tok("t")
        t_p3_v = P.tok("v")
        t_p3_g = P.tok("g")

    thr_toks = []
    tv2_prev = 0
    for wi in range(2):
        tv1 = P.op("vector", lambda v, wi=wi: v.tensor_scalar(
            den[0:1, 0:1], asum_p[wi][0:1, 0:1], 0.5 / (F * D), None, ALU.mult),
            inc=("v", 1), waits=[("v", alpha_done_tok[wi]), ("t", t_p3_t),
                                 ("v", t_obc), ("v", tv2_prev)])
        tm2 = P.op("tensor", lambda t: t.matmul(
            pH[:, 0:1], ones_row_f[:, :], den[0:1, 0:1], start=True, stop=True),
            inc=("t", 1), waits=[("v", tv1)])
        tcp = P.op("vector", lambda v, wi=wi: v.tensor_copy(thr_t[wi][:, :], pH[:, 0:1]),
             inc=("v", 1), waits=[("t", tm2)])
        tv2 = P.op("vector", lambda v, wi=wi: v.tensor_scalar(
            nthr_t[wi][:, :], thr_t[wi][:, :], -1.0, None, ALU.mult),
            inc=("v", 1), waits=[("v", tcp)])
        thr_toks.append(tv2)
        tv2_prev = tv2
    scopeW.close()

    P.fence()
    scopeB = ExitStack()
    attnT = sb("attnT", [128, G * NQ], BF16, scopeB)
    bcf2a = sb("bcf2a", [128, NQ], BF16, scopeB)
    scopeQ = ExitStack()
    qTall = sb("qTall", [128, G * NQ], BF16, scopeQ)
    scope4 = ExitStack()
    qwT = sb("qwT", [128, G * D], BF16, scope4)
    xnT = sb("xnT", [128, G * QTS], BF16, scope4)
    P.wait("sync", "t", t_p3_t)
    P.wait("sync", "g", t_p3_g)
    for g in range(G):
        P.op("sync", lambda e, g=g: e.dma_start(
            out=qwT[:, g * D:(g + 1) * D],
            in_=wq_bf[0:D, g * 128:(g + 1) * 128], transpose=True),
            inc=("dwt", 16), waits=[("gcast", t_cast)])
    t_qw = P.tok("dwt")

    xq_sem = ["dq0", "dq1"]
    qp_done = [0, 0]
    qevac = [0, 0, 0, 0]
    for qt in range(NQT):
        q0 = qt * QTS
        wl = list(t_lnfm)
        if qt >= 1:
            wl.append(("t", qp_done[(qt - 1) % 2]))
        P.op("sync", lambda e, q0=q0: e.dma_start(
            out=bass.AP(xnT, 0, [[G * QTS, 128], [QTS, G], [1, QTS]]),
            in_=xn_own_fm_sc[0:D, q0:q0 + QTS].rearrange(
                "(g p) c -> p g c", p=128)),
            inc=(xq_sem[qt % 2], 16), waits=wl)
        t_xq = P.tok(xq_sem[qt % 2])
        for og in range(G):
            pb = ps[og % 4]
            wme = [(xq_sem[qt % 2], t_xq), ("dwt", t_qw), ("s", qevac[og % 4]),
                   ("v", t_p3_v)]
            for ic in range(G):
                P.op("tensor", lambda t, og=og, ic=ic, pb=pb: t.matmul(
                    pb[:, 0:QTS],
                    qwT[:, ic * D + og * 128: ic * D + og * 128 + 128],
                    xnT[:, ic * QTS:(ic + 1) * QTS],
                    start=(ic == 0), stop=(ic == G - 1)),
                    waits=wme if ic == 0 else (),
                    inc=("t", 1) if ic == G - 1 else None)
            tmm = P.tok("t")
            qevac[og % 4] = P.op("scalar", lambda s, og=og, q0=q0, pb=pb: s.activation(
                qT_all_slice(qTall, og, q0, QTS, NQ), pb[:, 0:QTS], AF.Identity,
                bias=qb_sb[:, og:og + 1], scale=1.0),
                inc=("s", 1), waits=[("t", tmm), ("s", t_p1_s)])
        qp_done[qt % 2] = P.tok("t")
    t_qproj_s = P.tok("s")
    t_qproj_t = P.tok("t")
    scope4.close()

    P.fence()
    scope5 = ExitStack()
    tmpn = sb("tmpn", [64, 1024], F32, scope5)
    PTW = [sb(f"PTW{i}", [128, 1024], BF16, scope5) for i in range(2)]
    rawb = sb("rawb", [65, 1024], F32, scope5)
    bcsb = sb("bcsb", [64, 1024], F32, scope5)
    ktg = [sb(f"ktg{i}", [128, KV], BF16, scope5) for i in range(2)]
    vg = [sb(f"vg{i}", [128, KVG * 130], BF16, scope5) for i in range(2)]
    twb = [sb(f"twb{i}", [128, WTC], F32, scope5) for i in range(2)]
    tb = [sb(f"tb{i}", [128, WTC], BF16, scope5) for i in range(2)]
    tbf = [sb(f"tbf{i}", [128, WTC], BF16, scope5) for i in range(2)]

    tern_ld_sem = ["dwb0", "dwb1"]
    tst_tok = [0, 0]
    tern_jobs = []
    for wi, (wsrc, wtsc) in enumerate(((w_up, wup_t_sc), (w_down, wdn_t_sc))):
        for it in range(len(wtiles(wsrc, WTC))):
            tern_jobs.append((wi, wsrc, wtsc, it))

    def emit_tern_job(j):
        wi, wsrc, wtsc, it = tern_jobs[j]
        bi = j % 2
        wl = [("v", thr_toks[1])]
        if j >= 2:
            wl.append((["dwo0", "dwo1"][bi], tst_tok[bi]))
        P.op("sync", lambda e, it=it, b=bi, w=wsrc:
             e.dma_start(out=bass.AP(twb[b], 0, wtiles(w, WTC)[it][1]),
                         in_=wtiles(w, WTC)[it][0]),
             inc=(tern_ld_sem[bi], 16), waits=wl)
        tld = P.tok(tern_ld_sem[bi])
        P.op("vector", lambda p, b=bi, wi=wi: p.tensor_scalar(
            tb[b][:, :], twb[b][:, :], thr_t[wi][:, 0:1], None, ALU.is_gt),
            waits=[(tern_ld_sem[bi], tld)])
        P.op("vector", lambda p, b=bi, wi=wi: p.tensor_scalar(
            tbf[b][:, :], twb[b][:, :], nthr_t[wi][:, 0:1], None, ALU.is_lt))
        tsb_ = P.op("vector", lambda p, b=bi: p.tensor_sub(
            tbf[b][:, :], tb[b][:, :], tbf[b][:, :]), inc=("v", 1))
        P.op("gpsimd", lambda e, it=it, b=bi, wt=wtsc:
             e.dma_start(out=wtiles(wt, WTC)[it][0],
                         in_=bass.AP(tbf[b], 0, wtiles(wt, WTC)[it][1])),
             inc=(["dwo0", "dwo1"][bi], 16), waits=[("v", tsb_)])
        tst_tok[bi] = P.tok(["dwo0", "dwo1"][bi])

    rt_jobs = []
    for g in range(G):
        for fc in range(F // WTC):
            rt_jobs.append((wup_t_sc, wupT_sc, fc * WTC, g))
    for fg in range(FG):
        rt_jobs.append((wdn_t_sc, wdnT_sc, 0, fg))
    rtb = [sb(f"rtb{i}", [128, WTC], BF16, scope5) for i in range(2)]
    rt_ld_sem = ["drb0", "drb1"]
    rt_st_tok = [0, 0]
    t_tern_all = [None]

    def emit_rt_job(j):
        src_sc, dst_sc, r0, g = rt_jobs[j]
        bi = j % 2
        if t_tern_all[0] is None:
            t_tern_all[0] = [("dwo0", P.tok("dwo0")), ("dwo1", P.tok("dwo1"))]
        wl = list(t_tern_all[0])
        if j >= 2:
            wl.append((["dro0", "dro1"][bi], rt_st_tok[bi]))
        P.op("sync", lambda e, b=bi, s=src_sc, r0=r0, g=g:
             e.dma_start(out=rtb[b][:, :],
                         in_=s[r0:r0 + WTC, g * 128:(g + 1) * 128],
                         transpose=True),
             inc=(rt_ld_sem[bi], 16), waits=wl)
        tld = P.tok(rt_ld_sem[bi])
        P.op("gpsimd", lambda e, b=bi, d=dst_sc, r0=r0, g=g:
             e.dma_start(out=d[g * 128:(g + 1) * 128, r0:r0 + WTC],
                         in_=rtb[b][:, :]),
             inc=(["dro0", "dro1"][bi], 16), waits=[(rt_ld_sem[bi], tld)])
        rt_st_tok[bi] = P.tok(["dro0", "dro1"][bi])

    n_tern = len(tern_jobs)
    n_rt = len(rt_jobs)

    xnT5 = sb("xnT5", [128, G * QTS], BF16, scope5)
    wupb = [sb(f"wupb{i}", [128, G * 128], BF16, scope5) for i in range(3)]
    hTb = [sb(f"hTb{i}", [128, QTS], BF16, scope5) for i in range(3)]
    h2s5 = [sb(f"h2s5{i}", [128, QTS], BF16, scope5) for i in range(2)]
    bc1s = sb("bc1s", [128, QTS], BF16, scope5)
    r1row5 = sb("r1row5", [1, NQ], BF16, scope5)
    r2row = sb("r2row", [1, 1024], BF16, scope5)
    pUA = pW[1][:, 0:512]
    pUB = pW[1][:, 512:1024]
    wupb_sem = ["du0", "du1", "du2"]
    h_sems = ["dh0", "dh1", "dh2"]
    U = {"ld": [0, 0, 0], "mmA": 0, "gel": [0, 0, 0], "gelLast": 0,
         "sq": [0, 0], "h2m": 0, "hst": [0, 0, 0], "xn5": 0, "bcp": 0,
         "l2": 0, "rtup": None, "scale": 0}
    hq_done = {}

    def up_load(qt, fc):
        bi = (qt * FG + fc) % 3
        wl = list(U["rtup"])
        wl.append(("t", U["ld"][bi]))
        P.op("sync", lambda e, fc=fc, b=bi: e.dma_start(
            out=wupb[b][:, :],
            in_=wupT_sc[0:D, fc * 128:(fc + 1) * 128].rearrange(
                "(dc p) f -> p dc f", p=128)),
            inc=(wupb_sem[bi], 16), waits=wl)

    def emit_u0(qt):
        q0 = qt * QTS
        P.op("sync", lambda e, q0=q0: e.dma_start(
            out=bass.AP(xnT5, 0, [[G * QTS, 128], [QTS, G], [1, QTS]]),
            in_=xn_own_fm_sc[0:D, q0:q0 + QTS].rearrange(
                "(g p) c -> p g c", p=128)),
            inc=("dx5", 16), waits=list(t_lnfm) + [("t", U["xn5"])])
        t_x5 = P.tok("dx5")
        if qt == 0:
            P.op("gpsimd", lambda e: e.dma_start(
                out=r1row5[0:1, :],
                in_=r1_sc[0:NQ, 0:1].rearrange("a c -> c a")),
                inc=("dr1r", 16), waits=[t_r1done])
        t_r1r = P.tok("dr1r")
        tbm = P.op("tensor", lambda t, q0=q0: t.matmul(
            pUA[:, 0:QTS], ones_row_b[0:1, :], r1row5[0:1, q0:q0 + QTS],
            start=True, stop=True),
            inc=("t", 1), waits=[("dr1r", t_r1r), ("s", U["gelLast"]),
                                 ("v", U["bcp"])])
        U["mmA"] = P.tok("t")
        U["bcp"] = P.op("vector", lambda v: v.tensor_copy(
            bc1s[:, :], pUA[:, 0:QTS]), inc=("v", 1), waits=[("t", U["mmA"])])
        for g2 in range(G):
            P.op("vector", lambda v, g2=g2: v.tensor_mul(
                xnT5[:, g2 * QTS:(g2 + 1) * QTS],
                xnT5[:, g2 * QTS:(g2 + 1) * QTS], bc1s[:, :]),
                waits=[("dx5", t_x5), ("v", U["bcp"])] if g2 == 0 else (),
                inc=("v", 1) if g2 == G - 1 else None)
        U["scale"] = P.tok("v")
        up_load(qt, 0)
        up_load(qt, 1)

    def emit_u1(qt, fc):
        q0 = qt * QTS
        bi = (qt * FG + fc) % 3
        hb = fc % 3
        if fc + 2 < FG:
            up_load(qt, fc + 2)
        t_ld = P.tok(wupb_sem[bi])
        wme = [(wupb_sem[bi], t_ld), ("s", U["gelLast"]),
               ("v", U["scale"]), ("v", U["bcp"])]
        for dc in range(G):
            P.op("tensor", lambda t, dc=dc, b=bi: t.matmul(
                pUA[:, 0:QTS], wupb[b][:, dc * 128:(dc + 1) * 128],
                xnT5[:, dc * QTS:(dc + 1) * QTS],
                start=(dc == 0), stop=(dc == G - 1)),
                waits=wme if dc == 0 else (),
                inc=("t", 1) if dc == G - 1 else None)
        tmm = P.tok("t")
        U["ld"][bi] = tmm
        U["gelLast"] = P.op("scalar", lambda s, hb=hb: s.activation(
            hTb[hb][:, :], pUA[:, 0:QTS], AF.Gelu),
            inc=("s", 1), waits=[("t", tmm), (h_sems[hb], U["hst"][hb]),
                                 ("v", U["sq"][fc % 2])])
        tg = U["gelLast"]
        P.op("vector", lambda v, hb=hb, fc=fc: v.tensor_mul(
            h2s5[fc % 2][:, :], hTb[hb][:, :], hTb[hb][:, :]),
            inc=("v", 1), waits=[("s", tg), ("t", U["h2m"])])
        U["sq"][fc % 2] = P.tok("v")
        wm = [("v", U["sq"][fc % 2])]
        if fc == 0:
            wm.append(("s", U["l2"]))
        U["h2m"] = P.op("tensor", lambda t, fc=fc: t.matmul(
            pUB[0:1, 0:QTS], ones_bf[:, :], h2s5[fc % 2][:, :],
            start=(fc == 0), stop=(fc == FG - 1)),
            inc=("t", 1), waits=wm)
        P.op("gpsimd", lambda e, fc=fc, q0=q0, hb=hb: e.dma_start(
            out=h_sc[fc * 128:(fc + 1) * 128, q0:q0 + QTS],
            in_=hTb[hb][:, :]),
            inc=(h_sems[hb], 16), waits=[("s", tg)])
        U["hst"][hb] = P.tok(h_sems[hb])
        if fc == FG - 1:
            U["xn5"] = tmm

    def emit_u2(qt):
        P.op("scalar", lambda s: s.activation(
            den[0:1, 0:QTS], pUB[0:1, 0:QTS], AF.Ln, bias=0.0, scale=1.0),
            inc=("s", 1), waits=[("t", U["h2m"])])
        U["l2"] = P.tok("s")
        P.op("scalar", lambda s: s.activation(
            den[0:1, 512:512 + QTS], den[0:1, 0:QTS], AF.Exp,
            bias=0.0, scale=0.5), inc=("s", 1))
        t_e2 = P.tok("s")
        tq1 = P.op("vector", lambda v: v.tensor_scalar(
            den[0:1, 0:QTS], den[0:1, 512:512 + QTS],
            1.0 / float(np.sqrt(F)), 1e-8, ALU.mult, ALU.add),
            inc=("v", 1), waits=[("s", t_e2)])
        P.op("vector", lambda v: v.reciprocal(
            rec[0:1, 0:QTS], den[0:1, 0:QTS]), waits=[("v", tq1)])
        tq2 = P.op("vector", lambda v: v.tensor_copy(
            r2row[0:1, 0:QTS], rec[0:1, 0:QTS]), inc=("v", 1))
        tbm2 = P.op("tensor", lambda t: t.matmul(
            pUA[:, 0:QTS], ones_row_b[0:1, :], r2row[0:1, 0:QTS],
            start=True, stop=True), inc=("t", 1),
            waits=[("v", tq2), ("s", U["gelLast"]), ("v", U["bcp"])])
        tmm2 = P.tok("t")
        U["mmA"] = tmm2
        U["bcp"] = P.op("vector", lambda v, qt=qt: v.tensor_copy(
            bcf2a[:, qt * QTS:(qt + 1) * QTS], pUA[:, 0:QTS]),
            inc=("v", 1), waits=[("t", tmm2)])
        hq_done[qt] = [(sname, P.tok(sname)) for sname in h_sems]

    bg_jobs = [("tern", j) for j in range(n_tern)]
    bg_jobs += [("rt", j) for j in range(n_rt)]
    for qt in range(NQT):
        bg_jobs.append(("u0", qt))
        for fc in range(FG):
            bg_jobs.append(("u1", (qt, fc)))
        bg_jobs.append(("u2", qt))
    n_bg = len(bg_jobs)
    bg_per_slot = (n_bg + G * NQT - 1) // (G * NQT)

    def emit_bg(j):
        kind, arg = bg_jobs[j]
        if kind == "tern":
            emit_tern_job(arg)
        elif kind == "rt":
            emit_rt_job(arg)
            if arg == 31:
                U["rtup"] = [("dro0", P.tok("dro0")), ("dro1", P.tok("dro1"))]
        elif kind == "u0":
            emit_u0(arg)
        elif kind == "u1":
            emit_u1(*arg)
        else:
            emit_u2(arg)

    P.wait("sync", "t", t_qproj_t)
    kv_sem = ["dk0", "dk1"]
    exp_tok = [0, 0]
    av_tok = [0, 0]
    lastexp = 0
    lastnorm = 0
    kvload_tok = [0, 0]
    pending_norm = [None]
    pEF, pGH = pW[2], pW[3]
    P.rawcopy_tok = 0
    P.lastnorm_new = 0
    ti = 0
    for g in range(G):
        gb = g % 2
        wl = t_kt_done + t_vp_done
        if g >= 2:
            wl.append(("t", kvload_tok[gb]))
        P.op("sync", lambda e, g=g, b=gb: e.dma_start(
            out=ktg[b][:, :], in_=kt_sc[g * 128:(g + 1) * 128, :]),
            inc=(kv_sem[gb], 16), waits=wl)
        P.op("sync", lambda e, g=g, b=gb: e.dma_start(
            out=vg[b][:, :],
            in_=bass.AP(vp_sc, 2 * g * 65,
                        [[VROW, 128], [128 * VROW, KVG], [1, 130]])),
            inc=(kv_sem[gb], 16))
        t_kv = P.tok(kv_sem[gb])

        for qt in range(NQT):
            q0 = qt * QTS

            def emit_qk(kc, g=g, qt=qt, q0=q0, gb=gb, t_kv=t_kv):
                pq = pW[0]
                wme = [("s", t_qproj_s), (kv_sem[gb], t_kv)]
                if kc >= 1:
                    wme.append(("s", exp_tok[(kc - 1) % 2]))
                else:
                    wme.append(("s", lastexp_l[0]))
                P.op("tensor", lambda t, kc=kc, pq=pq: t.matmul(
                    pq[:, 0:QTS], ktg[gb][0:64, kc * 128:(kc + 1) * 128],
                    qTall[0:64, g * NQ + q0: g * NQ + q0 + QTS],
                    start=True, stop=True),
                    waits=wme)
                P.op("tensor", lambda t, kc=kc, pq=pq: t.matmul(
                    pq[:, 512:512 + QTS], ktg[gb][64:128, kc * 128:(kc + 1) * 128],
                    qTall[64:128, g * NQ + q0: g * NQ + q0 + QTS],
                    start=True, stop=True),
                    inc=("t", 1))
                tqk_l[kc % 2] = P.tok("t")

            def emit_exp_av(kc, g=g, gb=gb):
                pq = pW[0]
                we = [("t", tqk_l[kc % 2])]
                if kc >= 2:
                    we.append(("t", av_tok[kc % 2]))
                if QTS == 512:
                    te = P.op("scalar", lambda s, kc=kc, pq=pq: s.activation(
                        PTW[kc % 2][:, :], pq[:, :], AF.Exp,
                        bias=0.0, scale=SCALE), inc=("s", 1), waits=we)
                else:
                    P.op("scalar", lambda s, kc=kc, pq=pq: s.activation(
                        PTW[kc % 2][:, 0:QTS], pq[:, 0:QTS], AF.Exp,
                        bias=0.0, scale=SCALE), waits=we)
                    te = P.op("scalar", lambda s, kc=kc, pq=pq: s.activation(
                        PTW[kc % 2][:, 512:512 + QTS], pq[:, 512:512 + QTS],
                        AF.Exp, bias=0.0, scale=SCALE), inc=("s", 1))
                exp_tok[kc % 2] = te
                wa = [("s", te)]
                if kc == 0:
                    wa.append(("v", P.rawcopy_tok))
                P.op("tensor", lambda t, kc=kc: t.matmul(
                    pE[0:65, 0:QTS],
                    vg[gb][:, kc * 130: kc * 130 + 65],
                    PTW[kc % 2][:, 0:QTS],
                    start=(kc == 0), stop=(kc == KVG - 1)),
                    waits=wa)
                av_tok[kc % 2] = P.op("tensor", lambda t, kc=kc: t.matmul(
                    pF[0:65, 0:QTS],
                    vg[gb][:, kc * 130 + 65: kc * 130 + 130],
                    PTW[kc % 2][:, 512:512 + QTS],
                    start=(kc == 0), stop=(kc == KVG - 1)),
                    inc=("t", 1))

            tqk_l = [0, 0]
            lastexp_l = [lastexp]
            slot_base = (g * NQT + qt) * bg_per_slot
            for kc in range(KVG):
                if kc >= 1:
                    emit_exp_av(kc - 1)
                emit_qk(kc)
                if kc == 1 and pending_norm[0] is not None:
                    pending_norm[0]()
                    pending_norm[0] = None
                if kc % 4 == 3:
                    tgt = slot_base + (bg_per_slot * (kc + 1)) // KVG
                    while ti < min(n_bg, tgt):
                        emit_bg(ti)
                        ti += 1
            emit_exp_av(KVG - 1)
            lastexp = P.tok("s")
            t_av = P.tok("t")
            if qt == NQT - 1:
                kvload_tok[gb] = t_av

            def norm_block(g=g, q0=q0, t_av=t_av):
                def dual(x, p0, np_):
                    tn = x.tensor if isinstance(x, bass.AP) else x
                    off = x.offset if isinstance(x, bass.AP) else 0
                    pitch = tn.shape[1]
                    return bass.AP(tn, off + p0 * pitch,
                                   [[pitch, np_], [512, 2], [1, QTS]])
                trw = P.op("vector", lambda v: v.tensor_copy(
                    dual(rawb, 0, 65), dual(pEF, 0, 65)), inc=("v", 1),
                    waits=[("t", t_av), ("v", P.lastnorm_new)])
                P.rawcopy_tok = trw
                trec = P.op("vector", lambda v: v.tensor_copy(
                    dual(recb, 0, 1), dual(rawb, 64, 1)),
                    inc=("v", 1), waits=[("v", trw)])
                tb0 = P.op("tensor", lambda t: t.matmul(
                    pG[0:64, 0:QTS], ones_row_b[0:1, 0:64], recb[0:1, 0:QTS],
                    start=True, stop=True), waits=[("v", trec)])
                tb1 = P.op("tensor", lambda t: t.matmul(
                    pH[0:64, 0:QTS], ones_row_b[0:1, 0:64],
                    recb[0:1, 512:512 + QTS],
                    start=True, stop=True), inc=("t", 1))
                tbc = P.tok("t")
                tcp = P.op("vector", lambda v: v.tensor_copy(
                    dual(bcsb, 0, 64), dual(pGH, 0, 64)), inc=("v", 1),
                    waits=[("t", tbc)])
                trc2 = P.op("vector", lambda v: v.reciprocal(
                    dual(tmpn, 0, 64), dual(bcsb, 0, 64)),
                    inc=("v", 1), waits=[("v", tcp)])
                tml = P.op("vector", lambda v: v.tensor_tensor(
                    dual(bcsb, 0, 64), dual(rawb, 0, 64), dual(tmpn, 0, 64),
                    ALU.mult),
                    inc=("v", 1), waits=[("v", trc2)])
                P.op("vector", lambda v: v.tensor_scalar(
                    attnT[0:64, g * NQ + q0: g * NQ + q0 + QTS],
                    bcsb[0:64, 0:QTS], vb_sb[0:64, g:g + 1], None, ALU.add),
                    waits=[("v", tml)])
                tnm = P.op("vector", lambda v: v.tensor_scalar(
                    attnT[64:128, g * NQ + q0: g * NQ + q0 + QTS],
                    bcsb[0:64, 512:512 + QTS], vb_sb[64:128, g:g + 1],
                    None, ALU.add), inc=("v", 1))
                P.lastnorm_new = tnm

            pending_norm[0] = norm_block
    if pending_norm[0] is not None:
        pending_norm[0]()
        pending_norm[0] = None
    while ti < n_bg:
        emit_bg(ti)
        ti += 1
    t_attn_v = P.tok("v")
    t_attn_t = P.tok("t")
    t_attn_s = P.tok("s")
    t_tern_done = [("dro0", P.tok("dro0")), ("dro1", P.tok("dro1"))]
    scope5.close()

    scopeQ.close()
    P.fence()
    scope7 = ExitStack()
    owT = sb("owT", [128, G * D], BF16, scope7)
    wdb = [sb(f"wdb{i}", [128, D], BF16, scope7) for i in range(3)]
    hld = [sb(f"hld{i}", [128, QTS], BF16, scope7) for i in range(3)]
    hsc = [sb(f"hsc{i}", [128, QTS], BF16, scope7) for i in range(2)]

    P.wait("sync", "t", t_attn_t)
    for g in range(G):
        P.op("sync", lambda e, g=g: e.dma_start(
            out=owT[:, g * D:(g + 1) * D],
            in_=wo_bf[0:D, g * 128:(g + 1) * 128], transpose=True),
            inc=("dft", 16), waits=[("gcast", t_cast)])
    t_fw = P.tok("dft")

    wd_sem = ["dd0", "dd1", "dd2"]
    hl_sem = ["dg0", "dg1", "dg2"]
    xo_sem = ["do0", "do1"]
    fevac = [0, 0]
    fyst_tok = [0, 0]
    wdld_tok = [0, 0, 0]
    hscl_tok = [0, 0, 0]
    dnmm_tok = [0, 0]
    for qt in range(NQT):
        q0 = qt * QTS
        wme = [("v", t_attn_v), ("dft", t_fw)]
        if qt > 0:
            wme += [("v", fevac[0]), ("v", fevac[1])]
        for qs in range(QSUB):
            for dc in range(G):
                P.op("tensor", lambda t, dc=dc, q0=q0, qs=qs: t.matmul(
                    ps[qs * 2][:, 0:512],
                    attnT[:, dc * NQ + q0 + qs * 128: dc * NQ + q0 + qs * 128 + 128],
                    owT[:, dc * D: dc * D + 512],
                    start=(dc == 0), stop=False),
                    waits=wme if (qs == 0 and dc == 0) else ())
                P.op("tensor", lambda t, dc=dc, q0=q0, qs=qs: t.matmul(
                    ps[qs * 2 + 1][:, 0:512],
                    attnT[:, dc * NQ + q0 + qs * 128: dc * NQ + q0 + qs * 128 + 128],
                    owT[:, dc * D + 512: dc * D + 1024],
                    start=(dc == 0), stop=False),
                    inc=("t", 1) if dc == G - 1 else None)
        for fc in range(FG):
            bi = fc % 3
            wl = list(t_tern_done)
            if fc >= 3 or qt > 0:
                wl.append(("t", wdld_tok[bi]))
            P.op("sync", lambda e, fc=fc, b=bi: e.dma_start(
                out=wdb[b][:, :],
                in_=wdnT_sc[fc * 128:(fc + 1) * 128, 0:D]),
                inc=(wd_sem[bi], 16), waits=wl)
            t_wd = P.tok(wd_sem[bi])
            wl2 = list(hq_done[qt])
            if fc >= 3 or qt > 0:
                wl2.append(("v", hscl_tok[bi]))
            P.op("sync", lambda e, fc=fc, q0=q0, b=bi: e.dma_start(
                out=hld[b][:, :],
                in_=h_sc[fc * 128:(fc + 1) * 128, q0:q0 + QTS]),
                inc=(hl_sem[bi], 16), waits=wl2)
            t_hl = P.tok(hl_sem[bi])
            ths = P.op("vector", lambda v, fc=fc, q0=q0, b=bi: v.tensor_mul(
                hsc[fc % 2][:, :], hld[b][:, :], bcf2a[:, q0:q0 + QTS]),
                inc=("v", 1),
                waits=[(hl_sem[bi], t_hl), ("t", dnmm_tok[fc % 2])])
            hscl_tok[bi] = ths
            wme2 = [(wd_sem[bi], t_wd), ("v", ths)]
            for qs in range(QSUB):
                P.op("tensor", lambda t, fc=fc, qs=qs, b=bi: t.matmul(
                    ps[qs * 2][:, 0:512],
                    hsc[fc % 2][:, qs * 128: qs * 128 + 128],
                    wdb[b][:, 0:512],
                    start=False, stop=(fc == FG - 1)),
                    waits=wme2 if qs == 0 else ())
                P.op("tensor", lambda t, fc=fc, qs=qs, b=bi: t.matmul(
                    ps[qs * 2 + 1][:, 0:512],
                    hsc[fc % 2][:, qs * 128: qs * 128 + 128],
                    wdb[b][:, 512:1024],
                    start=False, stop=(fc == FG - 1)),
                    inc=("t", 1) if qs == QSUB - 1 else None)
            dnmm_tok[fc % 2] = P.tok("t")
            wdld_tok[bi] = dnmm_tok[fc % 2]
        t_dn = P.tok("t")
        for qs in range(QSUB):
            bi = qs % 2
            P.op("sync", lambda e, q0=q0, qs=qs, b=bi: e.dma_start(
                out=ubuf[b][:, :],
                in_=x_own[q0 + qs * 128: q0 + qs * 128 + 128, :]),
                inc=(xo_sem[bi], 16),
                waits=[("v", fevac[bi])] if (qt > 0 or qs > 1) else [])
            t_yl = P.tok(xo_sem[bi])
            P.op("vector", lambda v, qs=qs, b=bi: v.tensor_add(
                ysb[b][:, 0:512], ps[qs * 2][:, 0:512], obc[:, 0:512]),
                waits=[("t", t_dn), (xo_sem[bi], t_yl), ("dy", fyst_tok[bi])])
            P.op("vector", lambda v, qs=qs, b=bi: v.tensor_add(
                ysb[b][:, 512:1024], ps[qs * 2 + 1][:, 0:512], obc[:, 512:1024]),
                inc=("v", 1))
            ta2 = P.tok("v")
            fevac[bi] = P.op("vector", lambda v, b=bi: v.tensor_add(
                ysb[b][:, :], ysb[b][:, :], ubuf[b][:, :]), inc=("v", 1),
                waits=[("v", ta2)])
            P.op("gpsimd", lambda e, q0=q0, qs=qs, b=bi: e.dma_start(
                out=y[q0 + qs * 128: q0 + qs * 128 + 128, :], in_=ysb[b][:, :]),
                inc=("dy", 16), waits=[("v", fevac[bi])])
            fyst_tok[bi] = P.tok("dy")
    scope7.close()
    scopeB.close()

    P.wait("sync", "dy", P.tok("dy"))
    P.wait("sync", "gcast", P.tok("gcast"))

    with nc.Block() as blk:
        P.emit_all(blk)
    es.close()
    return nc


def wtiles(w, WTC):
    R, C = w.shape
    out = []
    if C >= WTC:
        assert C % WTC == 0 and R % 128 == 0
        for r in range(R // 128):
            for cs in range(C // WTC):
                out.append((w[r * 128:(r + 1) * 128, cs * WTC:(cs + 1) * WTC],
                            [[WTC, 128], [1, WTC]]))
    else:
        rpt = 128 * (WTC // C)
        assert R % rpt == 0
        na = rpt // 128
        for r in range(R // rpt):
            out.append((w[r * rpt:(r + 1) * rpt, :].rearrange(
                "(a p) c -> p a c", p=128),
                [[WTC, 128], [C, na], [1, C]]))
    return out


def qT_all_slice(qTall, og, q0, QTS, NQ):
    return qTall[:, og * NQ + q0: og * NQ + q0 + QTS]


def kernel(**inputs):
    cfg = full_cfg()
    S, NQ, D = cfg["S"], cfg["NQ"], cfg["D"]
    x = np.ascontiguousarray(np.asarray(inputs["L0_attn_x"], dtype=np.float32))
    B = x.shape[0]
    ncores = 8
    cpb = ncores // B

    f32c = lambda a: np.ascontiguousarray(np.asarray(a, dtype=np.float32))
    common = dict(
        q_w=f32c(inputs["L0_attn_q_w"]),
        q_b=f32c(inputs["L0_attn_q_b"]).reshape(D, 1),
        k_w=f32c(inputs["L0_attn_k_w"]),
        k_b=f32c(inputs["L0_attn_k_b"]).reshape(D, 1),
        v_w=f32c(inputs["L0_attn_v_w"]),
        v_b=f32c(inputs["L0_attn_v_b"]).reshape(D, 1),
        out_w=f32c(inputs["L0_attn_out_w"]),
        out_b=f32c(inputs["L0_attn_out_b"]).reshape(1, D),
        conv_w=f32c(inputs["L0_attn_conv_w"]).reshape(D, 3),
        conv_b=f32c(inputs["L0_attn_conv_b"]).reshape(D, 1),
        w_up=f32c(inputs["L0_ffn_w_up"]),
        w_down=f32c(inputs["L0_ffn_w_down"]),
        ident=np.eye(128, dtype=ml_dtypes.bfloat16),
    )
    in_maps = []
    for c in range(ncores):
        b, qc = divmod(c, cpb)
        m = dict(common)
        m["x_ctx"] = np.ascontiguousarray(x[b])
        m["x_own"] = np.ascontiguousarray(x[b, qc * NQ:(qc + 1) * NQ])
        in_maps.append(m)

    nc = build_nc(cfg)
    res = run_bass_kernel_spmd(nc, in_maps, core_ids=list(range(ncores)))
    out = np.empty((B, S, D), dtype=np.float32)
    for c in range(ncores):
        b, qc = divmod(c, cpb)
        out[b, qc * NQ:(qc + 1) * NQ] = res.results[c]["y"]
    return out


# version 42
# speedup vs baseline: 1.3698x; 1.3698x over previous
import numpy as np
import ml_dtypes
from contextlib import ExitStack

import concourse.bass as bass
import concourse.mybir as mybir
from concourse.bass_utils import run_bass_kernel_spmd

F32 = mybir.dt.float32
BF16 = mybir.dt.bfloat16
AF = mybir.ActivationFunctionType
ALU = mybir.AluOpType
AX = mybir.AxisListType

ENG_NAMES = ("sync", "gpsimd", "vector", "scalar", "tensor")


def full_cfg():
    return dict(S=8192, NQ=2048, D=1024, H=16, F=4096, CKV=512, QTS=512,
                WTC=1024)


class Prog:

    def __init__(self, nc, es):
        self.nc = nc
        self.es = es
        self.sems = {}
        self.cnt = {}
        self.stream = {e: [] for e in ENG_NAMES}
        self.lastw = {e: {} for e in ENG_NAMES}

    def sem(self, name):
        if name not in self.sems:
            self.sems[name] = self.es.enter_context(self.nc.semaphore(name))
            self.cnt[name] = 0
        return self.sems[name]

    def tok(self, name):
        self.sem(name)
        return self.cnt[name]

    def wait(self, engname, semname, val):
        if val is None or val <= 0:
            return
        s = self.sem(semname)
        lw = self.lastw[engname]
        if lw.get(semname, 0) >= val:
            return
        lw[semname] = val
        self.stream[engname].append(lambda eng, s=s, v=val: eng.wait_ge(s, v))

    def op(self, engname, build, inc=None, waits=()):
        for (sn, v) in waits:
            self.wait(engname, sn, v)
        incs = []
        tok = None
        if inc is not None:
            sn, n = inc
            self.sem(sn)
            self.cnt[sn] += n
            tok = self.cnt[sn]
            incs.append((self.sems[sn], n))

        def emit(eng, build=build, incs=incs):
            ins = build(eng)
            for (s, n) in incs:
                ins = ins.then_inc(s, n)

        self.stream[engname].append(emit)
        return tok

    def fence(self, exclude=("g", "gwa0", "gwa1")):
        for sn in list(self.cnt.keys()):
            if sn in exclude:
                continue
            v = self.cnt[sn]
            for e in ENG_NAMES:
                self.wait(e, sn, v)

    def emit_all(self, blk):
        for name in ENG_NAMES:
            fns = self.stream[name]

            def runner(eng, fns=fns):
                for fn in fns:
                    fn(eng)

            getattr(blk, name)(runner)


def build_nc(cfg):
    S, NQ, D, H, F = cfg["S"], cfg["NQ"], cfg["D"], cfg["H"], cfg["F"]
    CKV, QTS, WTC = cfg["CKV"], cfg["QTS"], cfg["WTC"]
    G = D // 128
    FG = F // 128
    KV = S // 2
    KVG = KV // 128
    NCH = KV // CKV
    NQT = NQ // QTS
    QSUB = QTS // 128
    DH = D // H
    assert DH == 64
    SCALE = 1.0 / float(np.sqrt(DH))
    XT = S // 128
    XOT = NQ // 128
    CS = 2 * CKV
    CW = CS + 128
    NJ = CKV // 128
    VROW = H * 65

    nc = bass.Bass("TRN2", target_bir_lowering=False)

    def din(name, shape):
        return nc.dram_tensor(name, shape, F32, kind="ExternalInput")

    x_own = din("x_own", [NQ, D])
    x_ctx = din("x_ctx", [S, D])
    q_w = din("q_w", [D, D]); q_b = din("q_b", [D, 1])
    k_w = din("k_w", [D, D]); k_b = din("k_b", [D, 1])
    v_w = din("v_w", [D, D]); v_b = din("v_b", [D, 1])
    out_w = din("out_w", [D, D]); out_b = din("out_b", [1, D])
    conv_w = din("conv_w", [D, 3]); conv_b = din("conv_b", [D, 1])
    w_up = din("w_up", [F, D])
    w_down = din("w_down", [D, F])
    y = nc.dram_tensor("y", [NQ, D], F32, kind="ExternalOutput")

    ident_in = nc.dram_tensor("ident", [128, 128], BF16, kind="ExternalInput")

    xn_fm_sc = nc.dram_tensor("xn_fm_sc", [D, S], BF16)
    xn_own_fm_sc = nc.dram_tensor("xn_own_fm_sc", [D, NQ], BF16)
    r1_sc = nc.dram_tensor("r1_sc", [NQ, 1], F32)
    wq_bf = nc.dram_tensor("wq_bf", [D, D], BF16)
    wk_bf = nc.dram_tensor("wk_bf", [D, D], BF16)
    wv_bf = nc.dram_tensor("wv_bf", [D, D], BF16)
    wo_bf = nc.dram_tensor("wo_bf", [D, D], BF16)
    kt_sc = nc.dram_tensor("kt_sc", [D, KV], BF16)
    vp_sc = nc.dram_tensor("vp_sc", [KV, VROW], BF16)
    wup_t_sc = nc.dram_tensor("wup_t_sc", [F, D], BF16)
    wdn_t_sc = nc.dram_tensor("wdn_t_sc", [D, F], BF16)
    wupT_sc = nc.dram_tensor("wupT_sc", [D, F], BF16)
    wdnT_sc = nc.dram_tensor("wdnT_sc", [F, D], BF16)
    h_sc = nc.dram_tensor("h_sc", [F, NQ], BF16)

    es = ExitStack()
    P = Prog(nc, es)

    def sb(name, shape, dt, scope=es):
        return scope.enter_context(nc.sbuf_tensor(name, shape, dt))

    qb_sb = sb("qb_sb", [128, G], F32)
    kb_sb = sb("kb_sb", [128, G], F32)
    vb_sb = sb("vb_sb", [128, G], F32)
    cb_sb = sb("cb_sb", [128, G], F32)
    cw_sb = sb("cw_sb", [128, 3 * G], F32)
    ones_bf = sb("ones_bf", [128, 1], BF16)
    ones_f = sb("ones_f", [128, 1], F32)
    ones_row_f = sb("ones_row_f", [1, 128], F32)
    ones_row_b = sb("ones_row_b", [1, 128], BF16)
    ob_row = sb("ob_row", [1, D], F32)
    obc = sb("obc", [128, D], F32)
    den = sb("den", [1, 1024], F32)
    rec = sb("rec", [1, 1024], F32)
    recb = sb("recb", [1, 1024], BF16)
    bcf = sb("bcf", [128, QTS], F32)
    ubuf = [sb(f"ubuf{i}", [128, D], F32) for i in range(2)]
    ysb = [sb(f"ysb{i}", [128, D], F32) for i in range(2)]
    r2t = sb("r2t", [128, 2 * QSUB], F32)
    thr_t = [sb(f"thr{i}", [128, 1], F32) for i in range(2)]
    nthr_t = [sb(f"nthr{i}", [128, 1], F32) for i in range(2)]
    WTCA = 2048
    asum_p = [sb(f"asum_p{i}", [128, 1], F32) for i in range(2)]
    rsum = [sb(f"rsum{i}", [128, 1], F32) for i in range(2)]
    epsln = sb("epsln", [128, 1], F32)
    ident_sb = sb("ident_sb", [128, 128], BF16)
    scopeW = ExitStack()
    wbuf = [sb(f"wbuf{i}", [128, WTCA], F32, scopeW) for i in range(2)]
    absb2 = [sb(f"absb2{i}", [128, WTCA], BF16, scopeW) for i in range(3)]

    pW3 = es.enter_context(nc.psum_tensor("pw3", [128, 1024], F32))
    pG, pH = pW3[:, 0:512], pW3[:, 512:1024]
    pAL = [pG, pH]
    scopeT = ExitStack()
    pT = [scopeT.enter_context(nc.psum_tensor(f"pt{i}", [128, 1024], BF16))
          for i in range(2)]

    for (src, dst) in ((q_w, wq_bf), (k_w, wk_bf), (v_w, wv_bf), (out_w, wo_bf)):
        P.op("gpsimd", lambda g, s=src, d=dst:
             g.dma_start(out=d[:, :], in_=s[:, :]), inc=("gcast", 16))
    t_cast = P.tok("gcast")
    for (vec, tile) in ((q_b, qb_sb), (k_b, kb_sb), (v_b, vb_sb), (conv_b, cb_sb)):
        for g in range(G):
            P.op("sync", lambda e, v=vec, t=tile, g=g:
                 e.dma_start(out=t[:, g:g + 1], in_=v[g * 128:(g + 1) * 128, 0:1]),
                 inc=("dsm", 16))
    for g in range(G):
        P.op("sync", lambda e, g=g: e.dma_start(
            out=cw_sb[:, g * 3:(g + 1) * 3], in_=conv_w[g * 128:(g + 1) * 128, :]),
            inc=("dsm", 16))
    P.op("sync", lambda e: e.dma_start(out=ob_row[:, :], in_=out_b[0:1, :]),
         inc=("dsm", 16))
    P.op("sync", lambda e: e.dma_start(out=ident_sb[:, :], in_=ident_in[:, :]),
         inc=("dsm", 16))
    t_dsm = P.tok("dsm")

    P.op("vector", lambda v: v.memset(ones_bf[:, :], 1.0))
    P.op("vector", lambda v: v.memset(ones_f[:, :], 1.0))
    P.op("vector", lambda v: v.memset(ones_row_f[:, :], 1.0))
    P.op("vector", lambda v: v.memset(ones_row_b[:, :], 1.0))
    P.op("vector", lambda v: v.memset(epsln[:, :], 1e-5), inc=("v", 1))
    t_const = P.tok("v")

    t1 = P.op("tensor", lambda t: t.matmul(pG[:, 0:512], ones_row_f[:, :],
                                           ob_row[0:1, 0:512], start=True, stop=True),
              inc=("t", 1), waits=[("dsm", t_dsm), ("v", t_const)])
    t2 = P.op("tensor", lambda t: t.matmul(pH[:, 0:512], ones_row_f[:, :],
                                           ob_row[0:1, 512:1024], start=True, stop=True),
              inc=("t", 1))
    P.op("vector", lambda v: v.tensor_copy(obc[:, 0:512], pG[:, 0:512]),
         waits=[("t", t1)])
    t_obc = P.op("vector", lambda v: v.tensor_copy(obc[:, 512:1024], pH[:, 0:512]),
                 inc=("v", 1), waits=[("t", t2)])

    ga_sem = ["gwa0", "gwa1"]
    abs_tok = [0, 0]
    pe_alpha_tok = [0, 0, 0]
    alpha_jobs = []
    for wi, wsrc in enumerate((w_up, w_down)):
        nt = len(wtiles(wsrc, WTCA))
        for it in range(nt):
            alpha_jobs.append((wi, wsrc, it, it == 0, it == nt - 1))
    n_alpha = len(alpha_jobs)
    alpha_done_tok = []

    aq = []

    def emit_alpha_mm(item):
        j, ta = item
        wi, wsrc, it, first, last = alpha_jobs[j]
        ai = j % 3
        nsub = WTCA // 512
        for k in range(nsub):
            wm = [("s", ta)] if k == 0 else []
            if first and k == 0:
                wm.append(("v", t_obc))
            P.op("tensor", lambda t, a=ai, k=k, wi=wi: t.matmul(
                pAL[wi][0:1, 0:512], ones_bf[:, :],
                absb2[a][:, k * 512:(k + 1) * 512],
                start=(first and k == 0), stop=(last and k == nsub - 1)),
                waits=wm, inc=("t", 1) if k == nsub - 1 else None)
        pe_alpha_tok[ai] = P.tok("t")
        if last:
            P.op("vector", lambda v, wi=wi: v.reduce_sum(
                asum_p[wi][0:1, 0:1], pAL[wi][0:1, 0:512], axis=AX.X),
                inc=("v", 1), waits=[("t", pe_alpha_tok[ai])])
            alpha_done_tok.append(P.tok("v"))

    def emit_alpha_job(j):
        wi, wsrc, it, first, last = alpha_jobs[j]
        bi = j % 2
        ai = j % 3
        P.op("sync", lambda g, it=it, b=bi, w=wsrc:
             g.dma_start(out=bass.AP(wbuf[b], 0, wtiles(w, WTCA)[it][1]),
                         in_=wtiles(w, WTCA)[it][0]),
             inc=(ga_sem[bi], 16), waits=[("s", abs_tok[bi])])
        tld = P.tok(ga_sem[bi])
        abs_tok[bi] = P.op("scalar", lambda s, b=bi, a=ai: s.activation(
            absb2[a][:, :], wbuf[b][:, :], AF.Abs),
            inc=("s", 1), waits=[(ga_sem[bi], tld), ("t", pe_alpha_tok[ai])])
        aq.append((j, abs_tok[bi]))
        if len(aq) > 2:
            emit_alpha_mm(aq.pop(0))

    NB = 4
    with ExitStack() as scope1:
        xbuf4 = [sb(f"xbuf4{i}", [128, NB * D], F32, scope1) for i in range(3)]
        nbuf4 = [sb(f"nbuf4{i}", [128, NB * D], BF16, scope1) for i in range(3)]
        fmb = [sb(f"fmb{i}", [128, G * NB * 128], BF16, scope1) for i in range(2)]
        stat2 = [sb(f"stat{i}", [128, 12], F32, scope1) for i in range(3)]
        mv2 = [sb(f"mv{i}", [128, 2 * NB], F32, scope1) for i in range(3)]
        lnt2 = [sb(f"lnt{i}", [128, 3 * NB], F32, scope1) for i in range(3)]
        sqt2 = [sb(f"sqt{i}", [128, 3 * NB], F32, scope1) for i in range(3)]

        ln_jobs = [(x_ctx, xn_fm_sc, j * NB * 128) for j in range(XT // NB)] + \
                  [(x_own, xn_own_fm_sc, j * NB * 128) for j in range(XOT // NB)]
        aj = [0]
        norm_tok = [0, 0, 0]
        aggr2 = [0, 0, 0]
        tp_tok = [0, 0, 0]
        ev_tok = [0, 0]
        fst_tok = [0, 0]
        r1st_tok = [0, 0, 0]
        nmr_tok = 0
        xl_sem = ["dl0", "dl1", "dl2"]
        fm_sems = ["dfm0", "dfm1"]
        for i, (src_, dst, r0) in enumerate(ln_jobs):
            bi = i % 3
            fi = i % 2
            stat, mv, lnt = stat2[bi], mv2[bi], lnt2[bi]
            aggr_tok = aggr2[bi]
            wl = []
            if i >= 3:
                wl += [("s", norm_tok[bi])]
            P.op("sync", lambda e, s=src_, r0=r0, b=bi: e.dma_start(
                out=xbuf4[b][:, 0:NB * D],
                in_=s[r0:r0 + NB * 128, :].rearrange("(a p) c -> p a c", p=128)),
                inc=(xl_sem[bi], 16), waits=wl)
            tld = P.tok(xl_sem[bi])
            for sub in range(NB):
                P.op("vector", lambda v, b=bi, sub=sub, stat=stat: v.bn_stats(
                    stat[:, 0:6], xbuf4[b][:, sub * D: sub * D + 512]),
                    waits=[(xl_sem[bi], tld), ("v", aggr_tok),
                           ("s", norm_tok[bi])])
                P.op("vector", lambda v, b=bi, sub=sub, stat=stat: v.bn_stats(
                    stat[:, 6:12], xbuf4[b][:, sub * D + 512: sub * D + 1024]),
                    inc=("v", 1))
                ts_ = P.tok("v")
                aggr_tok = P.op("vector", lambda v, sub=sub, stat=stat, mv=mv:
                                v.bn_aggr(mv[:, sub * 2:sub * 2 + 2], stat[:, :]),
                                inc=("v", 1), waits=[("v", ts_)])
            aggr2[bi] = aggr_tok
            P.op("scalar", lambda s, mv=mv, lnt=lnt: s.activation(
                lnt[:, 0:NB], mv[:, 1:2 * NB:2], AF.Ln,
                bias=epsln[:, 0:1], scale=1.0),
                inc=("s", 1), waits=[("v", aggr_tok), ("v", t_const)])
            tl1 = P.tok("s")
            P.op("scalar", lambda s, lnt=lnt: s.activation(
                lnt[:, NB:2 * NB], lnt[:, 0:NB], AF.Exp, bias=0.0, scale=-0.5),
                inc=("s", 1), waits=[("s", tl1)])
            tl2 = P.tok("s")
            nmr_tok = P.op("vector", lambda v, mv=mv, lnt=lnt:
                           v.scalar_tensor_tensor(
                               lnt[:, 2 * NB:3 * NB], mv[:, 0:2 * NB:2], -1.0,
                               lnt[:, NB:2 * NB], ALU.mult, ALU.mult),
                           inc=("v", 1), waits=[("s", tl2)])
            if dst is xn_own_fm_sc:
                sq = sqt2[bi]
                P.op("scalar", lambda s, mv=mv, sq=sq: s.activation(
                    sq[:, 0:NB], mv[:, 1:2 * NB:2], AF.Sqrt),
                    inc=("s", 1), waits=[("s", tl2), ("dr1", r1st_tok[bi])])
                tsq = P.tok("s")
                P.op("vector", lambda v, lnt=lnt, sq=sq: v.tensor_mul(
                    sq[:, NB:2 * NB], sq[:, 0:NB], lnt[:, NB:2 * NB]),
                    waits=[("s", tsq)])
                P.op("vector", lambda v, sq=sq: v.tensor_scalar(
                    sq[:, NB:2 * NB], sq[:, NB:2 * NB], 1e-8, None, ALU.add))
                P.op("vector", lambda v, sq=sq: v.reciprocal(
                    sq[:, 2 * NB:3 * NB], sq[:, NB:2 * NB]), inc=("v", 1))
                tr1 = P.tok("v")
                P.op("gpsimd", lambda e, r0=r0, sq=sq: e.dma_start(
                    out=r1_sc[r0:r0 + NB * 128, 0:1].rearrange(
                        "(a p) c -> p a c", p=128),
                    in_=sq[:, 2 * NB:3 * NB],
                    allow_slow_non_contiguous=True),
                    inc=("dr1", 16), waits=[("v", tr1)])
                r1st_tok[bi] = P.tok("dr1")
            for sub in range(NB):
                P.op("scalar", lambda s, b=bi, sub=sub, lnt=lnt:
                     s.activation(
                         nbuf4[b][:, sub * D:(sub + 1) * D],
                         xbuf4[b][:, sub * D:(sub + 1) * D],
                         AF.Identity,
                         bias=lnt[:, 2 * NB + sub:2 * NB + sub + 1],
                         scale=lnt[:, NB + sub:NB + sub + 1]),
                     waits=[("v", nmr_tok), ("t", tp_tok[bi])],
                     inc=("s", 1) if sub == NB - 1 else None)
            norm_tok[bi] = P.tok("s")
            for sub in range(NB):
                tb = (i * NB + sub) % 2
                for g in range(G):
                    wt = []
                    if g == 0:
                        wt = [("s", norm_tok[bi]), ("v", ev_tok[tb]),
                              ("dsm", t_dsm)]
                    P.op("tensor", lambda t, b=bi, sub=sub, g=g, tb=tb:
                         t.transpose(
                             pT[tb][:, g * 128:(g + 1) * 128],
                             nbuf4[b][:, sub * D + g * 128: sub * D + g * 128 + 128],
                             ident_sb[:, :]),
                         waits=wt, inc=("t", 1) if g == G - 1 else None)
                ttp = P.tok("t")
                we = [("t", ttp)]
                if sub == 0:
                    we.append((fm_sems[fi], fst_tok[fi]))
                ev_tok[tb] = P.op("vector", lambda v, b=fi, sub=sub, tb=tb:
                                  v.tensor_copy(
                                      bass.AP(fmb[b], sub * 128,
                                              [[G * NB * 128, 128],
                                               [NB * 128, G], [1, 128]]),
                                      pT[tb][:, 0:G * 128]),
                                  inc=("v", 1), waits=we)
            tp_tok[bi] = P.tok("t")
            ev_all = P.tok("v")
            P.op("gpsimd", lambda e, d=dst, r0=r0, b=fi: e.dma_start(
                out=d[0:D, r0:r0 + NB * 128].rearrange("(g p) c -> p g c", p=128),
                in_=fmb[b][:, :]),
                inc=(fm_sems[fi], 16), waits=[("v", ev_all)])
            fst_tok[fi] = P.tok(fm_sems[fi])
            while aj[0] < min(n_alpha, (i + 1) * 2):
                emit_alpha_job(aj[0])
                aj[0] += 1
        while aj[0] < n_alpha:
            emit_alpha_job(aj[0])
            aj[0] += 1
        while aq:
            emit_alpha_mm(aq.pop(0))
        t_lnfm = [("dfm0", P.tok("dfm0")), ("dfm1", P.tok("dfm1"))]
        t_r1done = ("dr1", P.tok("dr1"))
    t_p1_s = P.tok("s")
    t_p1_v = P.tok("v")
    scopeT.close()
    pW = [es.enter_context(nc.psum_tensor(f"pw{i}", [128, 1024], F32))
          for i in range(3)] + [pW3]
    ps = [pW[i // 2][:, (i % 2) * 512:(i % 2) * 512 + 512] for i in range(8)]
    pE, pF = ps[4], ps[5]

    P.fence()
    with ExitStack() as scope3:
        kwT = sb("kwT", [128, G * D], BF16, scope3)
        vwT = sb("vwT", [128, G * D], BF16, scope3)
        xnc = [sb(f"xnc{i}", [128, G * CW], BF16, scope3) for i in range(2)]
        xdT = [sb(f"xdT{i}", [128, G * CKV], BF16, scope3) for i in range(2)]
        cacc = [sb(f"cacc{i}", [128, CKV], F32, scope3) for i in range(2)]
        ktb = [sb(f"ktb{i}", [128, CKV], BF16, scope3) for i in range(2)]
        vpb = [sb(f"vpb{i}", [128, VROW], BF16, scope3) for i in range(2)]

        P.wait("sync", "gcast", t_cast)
        for (wsc, tile) in ((wk_bf, kwT), (wv_bf, vwT)):
            for g in range(G):
                P.op("sync", lambda e, w=wsc, t=tile, g=g: e.dma_start(
                    out=t[:, g * D:(g + 1) * D],
                    in_=w[0:D, g * 128:(g + 1) * 128], transpose=True),
                    inc=("dwt", 16))
        t_kvw = P.tok("dwt")

        P.op("vector", lambda v: v.memset(
            bass.AP(xnc[0], 0, [[xnc[0].shape[1], 128], [CW, G], [1, 128]]), 0.0))
        P.op("vector", lambda v: v.memset(
            bass.AP(vpb[0], 64, [[vpb[0].shape[1], 128], [65, H], [1, 1]]), 1.0))
        t_zp = P.op("vector", lambda v: v.memset(
            bass.AP(vpb[1], 64, [[vpb[1].shape[1], 128], [65, H], [1, 1]]), 1.0),
            inc=("v", 1))

        conv_tok = [0, 0]
        kvmm_tok = [0, 0]
        kevac = [0] * 8
        kst_tok = [0, 0]
        vst_tok = [0, 0]
        tr_sem = ["dc0", "dc1"]
        def emit_loads_conv(c):
            bi = c % 2
            lo = c * CS - 128
            wl = list(t_lnfm)
            if c == 0:
                wl.append(("v", t_zp))
            if c >= 2:
                wl.append(("v", conv_tok[bi]))
            if c == 0:
                P.op("sync", lambda e: e.dma_start(
                    out=bass.AP(xnc[0], 128,
                                [[G * CW, 128], [CW, G], [1, CS]]),
                    in_=xn_fm_sc[0:D, 0:CS].rearrange(
                        "(g p) c -> p g c", p=128)),
                    inc=(tr_sem[bi], 16), waits=wl)
            else:
                P.op("sync", lambda e, lo=lo, b=bi: e.dma_start(
                    out=bass.AP(xnc[b], 0,
                                [[G * CW, 128], [CW, G], [1, CW]]),
                    in_=xn_fm_sc[0:D, lo:lo + CW].rearrange(
                        "(g p) c -> p g c", p=128)),
                    inc=(tr_sem[bi], 16), waits=wl)
            t_tr = P.tok(tr_sem[bi])
            wc = [(tr_sem[bi], t_tr), ("dsm", t_dsm)]
            if c >= 2:
                wc.append(("t", kvmm_tok[bi]))
            prev_g = P.tok("v")
            for g in range(G):
                def xg(off, g=g, b=bi):
                    return bass.AP(xnc[b], g * CW + off,
                                   [[xnc[b].shape[1], 128], [2, CKV]])
                prev_g = P.op("vector", lambda p, g=g, b=bi, xg=xg: p.tensor_scalar(
                    cacc[b][:, :], xg(127), cw_sb[:, g * 3:g * 3 + 1],
                    cb_sb[:, g:g + 1], ALU.mult, ALU.add),
                    inc=("v", 1),
                    waits=(wc if g == 0 else []) + [("v", prev_g)])
                prev_g = P.op("vector", lambda p, g=g, b=bi, xg=xg: p.scalar_tensor_tensor(
                    cacc[1 - b][:, :], xg(128), cw_sb[:, g * 3 + 1:g * 3 + 2],
                    cacc[b][:, :], ALU.mult, ALU.add),
                    inc=("v", 1), waits=[("v", prev_g)])
                prev_g = P.op("vector", lambda p, g=g, b=bi, xg=xg: p.scalar_tensor_tensor(
                    xdT[b][:, g * CKV:(g + 1) * CKV], xg(129),
                    cw_sb[:, g * 3 + 2:g * 3 + 3], cacc[1 - b][:, :],
                    ALU.mult, ALU.add),
                    inc=("v", 1), waits=[("v", prev_g)])
            conv_tok[bi] = P.tok("v")

        def emit_kv(c):
            bi = c % 2
            for og in range(G):
                pb = ps[og]
                wme = [("v", conv_tok[bi]), ("dwt", t_kvw), ("v", kevac[og])]
                for ic in range(G):
                    P.op("tensor", lambda t, og=og, ic=ic, b=bi, pb=pb: t.matmul(
                        pb[:, 0:CKV],
                        kwT[:, ic * D + og * 128: ic * D + og * 128 + 128],
                        xdT[b][:, ic * CKV:(ic + 1) * CKV],
                        start=(ic == 0), stop=(ic == G - 1)),
                        waits=wme if ic == 0 else (),
                        inc=("t", 1) if ic == G - 1 else None)
                tmm = P.tok("t")
                we = [("t", tmm), (["dkta", "dktb"][og % 2], kst_tok[og % 2])]
                tv = P.op("vector", lambda v, og=og, pb=pb: v.tensor_scalar(
                    ktb[og % 2][:, :], pb[:, 0:CKV], kb_sb[:, og:og + 1],
                    None, ALU.add), inc=("v", 1), waits=we)
                kevac[og] = tv
                ks = ["dkta", "dktb"][og % 2]
                P.op("gpsimd", lambda e, og=og, c=c: e.dma_start(
                    out=kt_sc[og * 128:(og + 1) * 128, c * CKV:(c + 1) * CKV],
                    in_=ktb[og % 2][:, :]), inc=(ks, 16), waits=[("v", tv)])
                kst_tok[og % 2] = P.tok(ks)

            for jg in range(NJ):
                for half in range(2):
                    pb = ps[jg * 2 + half]
                    wme = [("v", kevac[jg * 2 + half])]
                    for ic in range(G):
                        P.op("tensor", lambda t, jg=jg, half=half, ic=ic, b=bi, pb=pb:
                             t.matmul(
                                 pb[:, 0:512],
                                 xdT[b][:, ic * CKV + jg * 128: ic * CKV + jg * 128 + 128],
                                 vwT[:, ic * D + half * 512: ic * D + half * 512 + 512],
                                 start=(ic == 0), stop=(ic == G - 1)),
                             waits=wme if ic == 0 else (),
                             inc=("t", 1) if ic == G - 1 else None)
                    tmm = P.tok("t")
                    if jg == NJ - 1 and half == 1:
                        kvmm_tok[bi] = tmm
                    kvg = c * NJ + jg
                    we = [("t", tmm)]
                    if half == 0:
                        we.append((["dvpa", "dvpb"][kvg % 2], vst_tok[kvg % 2]))
                    tv = P.op("vector", lambda v, kvg=kvg, half=half, pb=pb:
                              v.tensor_copy(
                                  bass.AP(vpb[kvg % 2], half * 8 * 65,
                                          [[vpb[0].shape[1], 128], [65, 8], [1, 64]]),
                                  bass.AP(pb.tensor, pb.offset,
                                          [[pb.tensor.shape[1], 128], [64, 8], [1, 64]])),
                              inc=("v", 1), waits=we)
                    kevac[jg * 2 + half] = tv
                    if half == 1:
                        vs = ["dvpa", "dvpb"][kvg % 2]
                        P.op("gpsimd", lambda e, kvg=kvg: e.dma_start(
                            out=vp_sc[kvg * 128:(kvg + 1) * 128, :],
                            in_=vpb[kvg % 2][:, :]), inc=(vs, 16),
                            waits=[("v", tv)])
                        vst_tok[kvg % 2] = P.tok(vs)

        for c in range(NCH):
            emit_loads_conv(c)
            if c >= 1:
                emit_kv(c - 1)
        emit_kv(NCH - 1)
        t_kt_done = [("dkta", P.tok("dkta")), ("dktb", P.tok("dktb"))]
        t_vp_done = [("dvpa", P.tok("dvpa")), ("dvpb", P.tok("dvpb"))]
        t_p3_t = P.tok("t")
        t_p3_v = P.tok("v")
        t_p3_g = P.tok("g")

    thr_toks = []
    tv2_prev = 0
    for wi in range(2):
        tv1 = P.op("vector", lambda v, wi=wi: v.tensor_scalar(
            den[0:1, 0:1], asum_p[wi][0:1, 0:1], 0.5 / (F * D), None, ALU.mult),
            inc=("v", 1), waits=[("v", alpha_done_tok[wi]), ("t", t_p3_t),
                                 ("v", t_obc), ("v", tv2_prev)])
        tm2 = P.op("tensor", lambda t: t.matmul(
            pH[:, 0:1], ones_row_f[:, :], den[0:1, 0:1], start=True, stop=True),
            inc=("t", 1), waits=[("v", tv1)])
        tcp = P.op("vector", lambda v, wi=wi: v.tensor_copy(thr_t[wi][:, :], pH[:, 0:1]),
             inc=("v", 1), waits=[("t", tm2)])
        tv2 = P.op("vector", lambda v, wi=wi: v.tensor_scalar(
            nthr_t[wi][:, :], thr_t[wi][:, :], -1.0, None, ALU.mult),
            inc=("v", 1), waits=[("v", tcp)])
        thr_toks.append(tv2)
        tv2_prev = tv2
    scopeW.close()

    P.fence()
    scopeB = ExitStack()
    attnT = sb("attnT", [128, G * NQ], BF16, scopeB)
    bcf2a = sb("bcf2a", [128, NQ], BF16, scopeB)
    scopeQ = ExitStack()
    qTall = sb("qTall", [128, G * NQ], BF16, scopeQ)
    scope4 = ExitStack()
    qwT = sb("qwT", [128, G * D], BF16, scope4)
    xnT = sb("xnT", [128, G * QTS], BF16, scope4)
    P.wait("sync", "t", t_p3_t)
    P.wait("sync", "g", t_p3_g)
    for g in range(G):
        P.op("sync", lambda e, g=g: e.dma_start(
            out=qwT[:, g * D:(g + 1) * D],
            in_=wq_bf[0:D, g * 128:(g + 1) * 128], transpose=True),
            inc=("dwt", 16), waits=[("gcast", t_cast)])
    t_qw = P.tok("dwt")

    xq_sem = ["dq0", "dq1"]
    qp_done = [0, 0]
    qevac = [0, 0, 0, 0]
    for qt in range(NQT):
        q0 = qt * QTS
        wl = list(t_lnfm)
        if qt >= 1:
            wl.append(("t", qp_done[(qt - 1) % 2]))
        P.op("sync", lambda e, q0=q0: e.dma_start(
            out=bass.AP(xnT, 0, [[G * QTS, 128], [QTS, G], [1, QTS]]),
            in_=xn_own_fm_sc[0:D, q0:q0 + QTS].rearrange(
                "(g p) c -> p g c", p=128)),
            inc=(xq_sem[qt % 2], 16), waits=wl)
        t_xq = P.tok(xq_sem[qt % 2])
        for og in range(G):
            pb = ps[og % 4]
            wme = [(xq_sem[qt % 2], t_xq), ("dwt", t_qw), ("s", qevac[og % 4]),
                   ("v", t_p3_v)]
            for ic in range(G):
                P.op("tensor", lambda t, og=og, ic=ic, pb=pb: t.matmul(
                    pb[:, 0:QTS],
                    qwT[:, ic * D + og * 128: ic * D + og * 128 + 128],
                    xnT[:, ic * QTS:(ic + 1) * QTS],
                    start=(ic == 0), stop=(ic == G - 1)),
                    waits=wme if ic == 0 else (),
                    inc=("t", 1) if ic == G - 1 else None)
            tmm = P.tok("t")
            qevac[og % 4] = P.op("scalar", lambda s, og=og, q0=q0, pb=pb: s.activation(
                qT_all_slice(qTall, og, q0, QTS, NQ), pb[:, 0:QTS], AF.Identity,
                bias=qb_sb[:, og:og + 1], scale=1.0),
                inc=("s", 1), waits=[("t", tmm), ("s", t_p1_s)])
        qp_done[qt % 2] = P.tok("t")
    t_qproj_s = P.tok("s")
    t_qproj_t = P.tok("t")
    scope4.close()

    P.fence()
    scope5 = ExitStack()
    tmpn = sb("tmpn", [64, 1024], F32, scope5)
    PTW = [sb(f"PTW{i}", [128, 1024], BF16, scope5) for i in range(2)]
    rawb = sb("rawb", [65, 1024], F32, scope5)
    bcsb = sb("bcsb", [64, 1024], F32, scope5)
    ktg = [sb(f"ktg{i}", [128, KV], BF16, scope5) for i in range(2)]
    vg = [sb(f"vg{i}", [128, KVG * 130], BF16, scope5) for i in range(2)]
    twb = [sb(f"twb{i}", [128, WTC], F32, scope5) for i in range(2)]
    tb = [sb(f"tb{i}", [128, WTC], BF16, scope5) for i in range(2)]
    tbf = [sb(f"tbf{i}", [128, WTC], BF16, scope5) for i in range(2)]

    tern_ld_sem = ["dwb0", "dwb1"]
    tst_tok = [0, 0]
    tern_jobs = []
    for wi, (wsrc, wtsc) in enumerate(((w_up, wup_t_sc), (w_down, wdn_t_sc))):
        for it in range(len(wtiles(wsrc, WTC))):
            tern_jobs.append((wi, wsrc, wtsc, it))

    def emit_tern_job(j):
        wi, wsrc, wtsc, it = tern_jobs[j]
        bi = j % 2
        wl = [("v", thr_toks[1])]
        if j >= 2:
            wl.append((["dwo0", "dwo1"][bi], tst_tok[bi]))
        P.op("sync", lambda e, it=it, b=bi, w=wsrc:
             e.dma_start(out=bass.AP(twb[b], 0, wtiles(w, WTC)[it][1]),
                         in_=wtiles(w, WTC)[it][0]),
             inc=(tern_ld_sem[bi], 16), waits=wl)
        tld = P.tok(tern_ld_sem[bi])
        P.op("vector", lambda p, b=bi, wi=wi: p.tensor_scalar(
            tb[b][:, :], twb[b][:, :], thr_t[wi][:, 0:1], None, ALU.is_gt),
            waits=[(tern_ld_sem[bi], tld)])
        P.op("vector", lambda p, b=bi, wi=wi: p.tensor_scalar(
            tbf[b][:, :], twb[b][:, :], nthr_t[wi][:, 0:1], None, ALU.is_lt))
        tsb_ = P.op("vector", lambda p, b=bi: p.tensor_sub(
            tbf[b][:, :], tb[b][:, :], tbf[b][:, :]), inc=("v", 1))
        P.op("gpsimd", lambda e, it=it, b=bi, wt=wtsc:
             e.dma_start(out=wtiles(wt, WTC)[it][0],
                         in_=bass.AP(tbf[b], 0, wtiles(wt, WTC)[it][1])),
             inc=(["dwo0", "dwo1"][bi], 16), waits=[("v", tsb_)])
        tst_tok[bi] = P.tok(["dwo0", "dwo1"][bi])

    rt_jobs = []
    for g in range(G):
        for fc in range(F // WTC):
            rt_jobs.append((wup_t_sc, wupT_sc, fc * WTC, g))
    for fg in range(FG):
        rt_jobs.append((wdn_t_sc, wdnT_sc, 0, fg))
    rtb = [sb(f"rtb{i}", [128, WTC], BF16, scope5) for i in range(2)]
    rt_ld_sem = ["drb0", "drb1"]
    rt_st_tok = [0, 0]
    t_tern_all = [None]

    def emit_rt_job(j):
        src_sc, dst_sc, r0, g = rt_jobs[j]
        bi = j % 2
        if t_tern_all[0] is None:
            t_tern_all[0] = [("dwo0", P.tok("dwo0")), ("dwo1", P.tok("dwo1"))]
        wl = list(t_tern_all[0])
        if j >= 2:
            wl.append((["dro0", "dro1"][bi], rt_st_tok[bi]))
        P.op("sync", lambda e, b=bi, s=src_sc, r0=r0, g=g:
             e.dma_start(out=rtb[b][:, :],
                         in_=s[r0:r0 + WTC, g * 128:(g + 1) * 128],
                         transpose=True),
             inc=(rt_ld_sem[bi], 16), waits=wl)
        tld = P.tok(rt_ld_sem[bi])
        P.op("gpsimd", lambda e, b=bi, d=dst_sc, r0=r0, g=g:
             e.dma_start(out=d[g * 128:(g + 1) * 128, r0:r0 + WTC],
                         in_=rtb[b][:, :]),
             inc=(["dro0", "dro1"][bi], 16), waits=[(rt_ld_sem[bi], tld)])
        rt_st_tok[bi] = P.tok(["dro0", "dro1"][bi])

    n_tern = len(tern_jobs)
    n_rt = len(rt_jobs)

    xnT5 = sb("xnT5", [128, G * QTS], BF16, scope5)
    wupb = [sb(f"wupb{i}", [128, G * 128], BF16, scope5) for i in range(3)]
    hTb = [sb(f"hTb{i}", [128, QTS], BF16, scope5) for i in range(3)]
    h2s5 = [sb(f"h2s5{i}", [128, QTS], BF16, scope5) for i in range(2)]
    bc1s = sb("bc1s", [128, QTS], BF16, scope5)
    r1row5 = sb("r1row5", [1, NQ], BF16, scope5)
    r2row = sb("r2row", [1, 1024], BF16, scope5)
    h2p = sb("h2p", [128, QTS], F32, scope5)
    pUA = pW[3][:, 0:512]
    wupb_sem = ["du0", "du1", "du2"]
    h_sems = ["dh0", "dh1", "dh2"]
    U = {"ld": [0, 0, 0], "mmA": 0, "gel": [0, 0, 0], "gelLast": 0,
         "sq": [0, 0], "h2m": 0, "h2tot": 0, "hst": [0, 0, 0], "xn5": 0,
         "bcp": 0, "l2": 0, "rtup": None, "scale": 0}
    P.norm_tcp = 0
    hq_done = {}

    def up_load(qt, fc):
        bi = (qt * FG + fc) % 3
        wl = list(U["rtup"])
        wl.append(("t", U["ld"][bi]))
        P.op("sync", lambda e, fc=fc, b=bi: e.dma_start(
            out=wupb[b][:, :],
            in_=wupT_sc[0:D, fc * 128:(fc + 1) * 128].rearrange(
                "(dc p) f -> p dc f", p=128)),
            inc=(wupb_sem[bi], 16), waits=wl)

    def emit_u0(qt):
        q0 = qt * QTS
        P.op("sync", lambda e, q0=q0: e.dma_start(
            out=bass.AP(xnT5, 0, [[G * QTS, 128], [QTS, G], [1, QTS]]),
            in_=xn_own_fm_sc[0:D, q0:q0 + QTS].rearrange(
                "(g p) c -> p g c", p=128)),
            inc=("dx5", 16), waits=list(t_lnfm) + [("t", U["xn5"])])
        t_x5 = P.tok("dx5")
        if qt == 0:
            P.op("gpsimd", lambda e: e.dma_start(
                out=r1row5[0:1, :],
                in_=r1_sc[0:NQ, 0:1].rearrange("a c -> c a")),
                inc=("dr1r", 16), waits=[t_r1done])
        t_r1r = P.tok("dr1r")
        tbm = P.op("tensor", lambda t, q0=q0: t.matmul(
            pUA[:, 0:QTS], ones_row_b[0:1, :], r1row5[0:1, q0:q0 + QTS],
            start=True, stop=True),
            inc=("t", 1), waits=[("dr1r", t_r1r), ("s", U["gelLast"]),
                                 ("v", U["bcp"]), ("v", P.norm_tcp)])
        U["mmA"] = P.tok("t")
        U["bcp"] = P.op("vector", lambda v: v.tensor_copy(
            bc1s[:, :], pUA[:, 0:QTS]), inc=("v", 1), waits=[("t", U["mmA"])])
        for g2 in range(G):
            P.op("vector", lambda v, g2=g2: v.tensor_mul(
                xnT5[:, g2 * QTS:(g2 + 1) * QTS],
                xnT5[:, g2 * QTS:(g2 + 1) * QTS], bc1s[:, :]),
                waits=[("dx5", t_x5), ("v", U["bcp"])] if g2 == 0 else (),
                inc=("v", 1) if g2 == G - 1 else None)
        U["scale"] = P.tok("v")
        up_load(qt, 0)
        up_load(qt, 1)

    def emit_u1(qt, fc):
        q0 = qt * QTS
        bi = (qt * FG + fc) % 3
        hb = fc % 3
        if fc + 2 < FG:
            up_load(qt, fc + 2)
        t_ld = P.tok(wupb_sem[bi])
        wme = [(wupb_sem[bi], t_ld), ("s", U["gelLast"]),
               ("v", U["scale"]), ("v", U["bcp"]), ("v", P.norm_tcp)]
        for dc in range(G):
            P.op("tensor", lambda t, dc=dc, b=bi: t.matmul(
                pUA[:, 0:QTS], wupb[b][:, dc * 128:(dc + 1) * 128],
                xnT5[:, dc * QTS:(dc + 1) * QTS],
                start=(dc == 0), stop=(dc == G - 1)),
                waits=wme if dc == 0 else (),
                inc=("t", 1) if dc == G - 1 else None)
        tmm = P.tok("t")
        U["ld"][bi] = tmm
        U["gelLast"] = P.op("scalar", lambda s, hb=hb: s.activation(
            hTb[hb][:, :], pUA[:, 0:QTS], AF.Gelu),
            inc=("s", 1), waits=[("t", tmm), (h_sems[hb], U["hst"][hb]),
                                 ("v", U["sq"][fc % 2])])
        tg = U["gelLast"]
        P.op("vector", lambda v, hb=hb, fc=fc: v.tensor_mul(
            h2s5[fc % 2][:, :], hTb[hb][:, :], hTb[hb][:, :]),
            inc=("v", 1), waits=[("s", tg)])
        U["sq"][fc % 2] = P.tok("v")
        if fc == 0:
            U["h2m"] = P.op("vector", lambda v, fc=fc: v.tensor_copy(
                h2p[:, :], h2s5[fc % 2][:, :]), inc=("v", 1),
                waits=[("t", U["h2tot"])])
        else:
            U["h2m"] = P.op("vector", lambda v, fc=fc: v.tensor_add(
                h2p[:, :], h2p[:, :], h2s5[fc % 2][:, :]), inc=("v", 1))
        P.op("gpsimd", lambda e, fc=fc, q0=q0, hb=hb: e.dma_start(
            out=h_sc[fc * 128:(fc + 1) * 128, q0:q0 + QTS],
            in_=hTb[hb][:, :]),
            inc=(h_sems[hb], 16), waits=[("s", tg)])
        U["hst"][hb] = P.tok(h_sems[hb])
        if fc == FG - 1:
            U["xn5"] = tmm

    def emit_u2(qt):
        U["h2tot"] = P.op("tensor", lambda t: t.matmul(
            pUA[0:1, 0:QTS], ones_f[:, :], h2p[:, :],
            start=True, stop=True), inc=("t", 1),
            waits=[("v", U["h2m"]), ("s", U["gelLast"]), ("v", U["bcp"]),
                   ("v", P.norm_tcp)])
        P.op("scalar", lambda s: s.activation(
            den[0:1, 0:QTS], pUA[0:1, 0:QTS], AF.Ln, bias=0.0, scale=1.0),
            inc=("s", 1), waits=[("t", U["h2tot"])])
        U["l2"] = P.tok("s")
        P.op("scalar", lambda s: s.activation(
            den[0:1, 512:512 + QTS], den[0:1, 0:QTS], AF.Exp,
            bias=0.0, scale=0.5), inc=("s", 1))
        t_e2 = P.tok("s")
        tq1 = P.op("vector", lambda v: v.tensor_scalar(
            den[0:1, 0:QTS], den[0:1, 512:512 + QTS],
            1.0 / float(np.sqrt(F)), 1e-8, ALU.mult, ALU.add),
            inc=("v", 1), waits=[("s", t_e2)])
        P.op("vector", lambda v: v.reciprocal(
            rec[0:1, 0:QTS], den[0:1, 0:QTS]), waits=[("v", tq1)])
        tq2 = P.op("vector", lambda v: v.tensor_copy(
            r2row[0:1, 0:QTS], rec[0:1, 0:QTS]), inc=("v", 1))
        tbm2 = P.op("tensor", lambda t: t.matmul(
            pUA[:, 0:QTS], ones_row_b[0:1, :], r2row[0:1, 0:QTS],
            start=True, stop=True), inc=("t", 1),
            waits=[("v", tq2), ("s", U["gelLast"]), ("v", U["bcp"]),
                   ("v", P.norm_tcp), ("s", U["l2"])])
        tmm2 = P.tok("t")
        U["mmA"] = tmm2
        U["bcp"] = P.op("vector", lambda v, qt=qt: v.tensor_copy(
            bcf2a[:, qt * QTS:(qt + 1) * QTS], pUA[:, 0:QTS]),
            inc=("v", 1), waits=[("t", tmm2)])
        hq_done[qt] = [(sname, P.tok(sname)) for sname in h_sems]

    bg_jobs = [("tern", j) for j in range(n_tern)]
    bg_jobs += [("rt", j) for j in range(n_rt)]
    for qt in range(NQT):
        bg_jobs.append(("u0", qt))
        for fc in range(FG):
            bg_jobs.append(("u1", (qt, fc)))
        bg_jobs.append(("u2", qt))
    n_bg = len(bg_jobs)
    bg_per_slot = (n_bg + G * NQT - 1) // (G * NQT)

    def emit_bg(j):
        kind, arg = bg_jobs[j]
        if kind == "tern":
            emit_tern_job(arg)
        elif kind == "rt":
            emit_rt_job(arg)
            if arg == 31:
                U["rtup"] = [("dro0", P.tok("dro0")), ("dro1", P.tok("dro1"))]
        elif kind == "u0":
            emit_u0(arg)
        elif kind == "u1":
            emit_u1(*arg)
        else:
            emit_u2(arg)

    P.wait("sync", "t", t_qproj_t)
    kv_sem = ["dk0", "dk1"]
    exp_tok = [0, 0]
    av_tok = [0, 0]
    lastexp = 0
    lastnorm = 0
    kvload_tok = [0, 0]
    pending_norm = [None]
    pEF, pGH = pW[2], pW[3]
    P.rawcopy_tok = 0
    P.lastnorm_new = 0
    ti = 0
    for g in range(G):
        gb = g % 2
        wl = t_kt_done + t_vp_done
        if g >= 2:
            wl.append(("t", kvload_tok[gb]))
        P.op("sync", lambda e, g=g, b=gb: e.dma_start(
            out=ktg[b][:, :], in_=kt_sc[g * 128:(g + 1) * 128, :]),
            inc=(kv_sem[gb], 16), waits=wl)
        P.op("sync", lambda e, g=g, b=gb: e.dma_start(
            out=vg[b][:, :],
            in_=bass.AP(vp_sc, 2 * g * 65,
                        [[VROW, 128], [128 * VROW, KVG], [1, 130]])),
            inc=(kv_sem[gb], 16))
        t_kv = P.tok(kv_sem[gb])

        for qt in range(NQT):
            q0 = qt * QTS

            def emit_qk(kc, g=g, qt=qt, q0=q0, gb=gb, t_kv=t_kv):
                pq = pW[kc % 2]
                wme = [("s", t_qproj_s), (kv_sem[gb], t_kv)]
                if kc >= 2:
                    wme.append(("s", exp_tok[kc % 2]))
                else:
                    wme.append(("s", lastexp_l[0]))
                P.op("tensor", lambda t, kc=kc, pq=pq: t.matmul(
                    pq[:, 0:QTS], ktg[gb][0:64, kc * 128:(kc + 1) * 128],
                    qTall[0:64, g * NQ + q0: g * NQ + q0 + QTS],
                    start=True, stop=True),
                    waits=wme)
                P.op("tensor", lambda t, kc=kc, pq=pq: t.matmul(
                    pq[:, 512:512 + QTS], ktg[gb][64:128, kc * 128:(kc + 1) * 128],
                    qTall[64:128, g * NQ + q0: g * NQ + q0 + QTS],
                    start=True, stop=True),
                    inc=("t", 1))
                tqk_l[kc % 2] = P.tok("t")

            def emit_exp_av(kc, g=g, gb=gb):
                pq = pW[kc % 2]
                we = [("t", tqk_l[kc % 2])]
                if kc >= 2:
                    we.append(("t", av_tok[kc % 2]))
                if QTS == 512:
                    te = P.op("scalar", lambda s, kc=kc, pq=pq: s.activation(
                        PTW[kc % 2][:, :], pq[:, :], AF.Exp,
                        bias=0.0, scale=SCALE), inc=("s", 1), waits=we)
                else:
                    P.op("scalar", lambda s, kc=kc, pq=pq: s.activation(
                        PTW[kc % 2][:, 0:QTS], pq[:, 0:QTS], AF.Exp,
                        bias=0.0, scale=SCALE), waits=we)
                    te = P.op("scalar", lambda s, kc=kc, pq=pq: s.activation(
                        PTW[kc % 2][:, 512:512 + QTS], pq[:, 512:512 + QTS],
                        AF.Exp, bias=0.0, scale=SCALE), inc=("s", 1))
                exp_tok[kc % 2] = te
                wa = [("s", te)]
                if kc == 0:
                    wa.append(("v", P.rawcopy_tok))
                P.op("tensor", lambda t, kc=kc: t.matmul(
                    pE[0:65, 0:QTS],
                    vg[gb][:, kc * 130: kc * 130 + 65],
                    PTW[kc % 2][:, 0:QTS],
                    start=(kc == 0), stop=(kc == KVG - 1)),
                    waits=wa)
                av_tok[kc % 2] = P.op("tensor", lambda t, kc=kc: t.matmul(
                    pF[0:65, 0:QTS],
                    vg[gb][:, kc * 130 + 65: kc * 130 + 130],
                    PTW[kc % 2][:, 512:512 + QTS],
                    start=(kc == 0), stop=(kc == KVG - 1)),
                    inc=("t", 1))

            tqk_l = [0, 0]
            lastexp_l = [lastexp]
            slot_base = (g * NQT + qt) * bg_per_slot
            for kc in range(KVG):
                emit_qk(kc)
                if kc == 1 and pending_norm[0] is not None:
                    pending_norm[0]()
                    pending_norm[0] = None
                if kc % 4 == 3:
                    tgt = slot_base + (bg_per_slot * (kc + 1)) // KVG
                    while ti < min(n_bg, tgt):
                        emit_bg(ti)
                        ti += 1
                if kc >= 1:
                    emit_exp_av(kc - 1)
            emit_exp_av(KVG - 1)
            lastexp = P.tok("s")
            t_av = P.tok("t")
            if qt == NQT - 1:
                kvload_tok[gb] = t_av

            def norm_block(g=g, q0=q0, t_av=t_av):
                def dual(x, p0, np_):
                    tn = x.tensor if isinstance(x, bass.AP) else x
                    off = x.offset if isinstance(x, bass.AP) else 0
                    pitch = tn.shape[1]
                    return bass.AP(tn, off + p0 * pitch,
                                   [[pitch, np_], [512, 2], [1, QTS]])
                trw = P.op("vector", lambda v: v.tensor_copy(
                    dual(rawb, 0, 65), dual(pEF, 0, 65)), inc=("v", 1),
                    waits=[("t", t_av), ("v", P.lastnorm_new)])
                P.rawcopy_tok = trw
                trec = P.op("vector", lambda v: v.tensor_copy(
                    dual(recb, 0, 1), dual(rawb, 64, 1)),
                    inc=("v", 1), waits=[("v", trw)])
                tb0 = P.op("tensor", lambda t: t.matmul(
                    pG[0:64, 0:QTS], ones_row_b[0:1, 0:64], recb[0:1, 0:QTS],
                    start=True, stop=True),
                    waits=[("v", trec), ("s", U["gelLast"]),
                           ("v", U["bcp"])])
                tb1 = P.op("tensor", lambda t: t.matmul(
                    pH[0:64, 0:QTS], ones_row_b[0:1, 0:64],
                    recb[0:1, 512:512 + QTS],
                    start=True, stop=True), inc=("t", 1))
                tbc = P.tok("t")
                tcp = P.op("vector", lambda v: v.tensor_copy(
                    dual(bcsb, 0, 64), dual(pGH, 0, 64)), inc=("v", 1),
                    waits=[("t", tbc)])
                P.norm_tcp = tcp
                trc2 = P.op("vector", lambda v: v.reciprocal(
                    dual(tmpn, 0, 64), dual(bcsb, 0, 64)),
                    inc=("v", 1), waits=[("v", tcp)])
                tml = P.op("vector", lambda v: v.tensor_tensor(
                    dual(bcsb, 0, 64), dual(rawb, 0, 64), dual(tmpn, 0, 64),
                    ALU.mult),
                    inc=("v", 1), waits=[("v", trc2)])
                P.op("vector", lambda v: v.tensor_scalar(
                    attnT[0:64, g * NQ + q0: g * NQ + q0 + QTS],
                    bcsb[0:64, 0:QTS], vb_sb[0:64, g:g + 1], None, ALU.add),
                    waits=[("v", tml)])
                tnm = P.op("vector", lambda v: v.tensor_scalar(
                    attnT[64:128, g * NQ + q0: g * NQ + q0 + QTS],
                    bcsb[0:64, 512:512 + QTS], vb_sb[64:128, g:g + 1],
                    None, ALU.add), inc=("v", 1))
                P.lastnorm_new = tnm

            pending_norm[0] = norm_block
    if pending_norm[0] is not None:
        pending_norm[0]()
        pending_norm[0] = None
    while ti < n_bg:
        emit_bg(ti)
        ti += 1
    t_attn_v = P.tok("v")
    t_attn_t = P.tok("t")
    t_attn_s = P.tok("s")
    t_tern_done = [("dro0", P.tok("dro0")), ("dro1", P.tok("dro1"))]
    scope5.close()

    scopeQ.close()
    P.fence()
    scope7 = ExitStack()
    owT = sb("owT", [128, G * D], BF16, scope7)
    wdb = [sb(f"wdb{i}", [128, D], BF16, scope7) for i in range(3)]
    hld = [sb(f"hld{i}", [128, QTS], BF16, scope7) for i in range(3)]
    hsc = [sb(f"hsc{i}", [128, QTS], BF16, scope7) for i in range(2)]

    P.wait("sync", "t", t_attn_t)
    for g in range(G):
        P.op("sync", lambda e, g=g: e.dma_start(
            out=owT[:, g * D:(g + 1) * D],
            in_=wo_bf[0:D, g * 128:(g + 1) * 128], transpose=True),
            inc=("dft", 16), waits=[("gcast", t_cast)])
    t_fw = P.tok("dft")

    wd_sem = ["dd0", "dd1", "dd2"]
    hl_sem = ["dg0", "dg1", "dg2"]
    xo_sem = ["do0", "do1"]
    fevac = [0, 0]
    fyst_tok = [0, 0]
    wdld_tok = [0, 0, 0]
    hscl_tok = [0, 0, 0]
    dnmm_tok = [0, 0]
    for qt in range(NQT):
        q0 = qt * QTS
        wme = [("v", t_attn_v), ("dft", t_fw)]
        if qt > 0:
            wme += [("v", fevac[0]), ("v", fevac[1])]
        for qs in range(QSUB):
            for dc in range(G):
                P.op("tensor", lambda t, dc=dc, q0=q0, qs=qs: t.matmul(
                    ps[qs * 2][:, 0:512],
                    attnT[:, dc * NQ + q0 + qs * 128: dc * NQ + q0 + qs * 128 + 128],
                    owT[:, dc * D: dc * D + 512],
                    start=(dc == 0), stop=False),
                    waits=wme if (qs == 0 and dc == 0) else ())
                P.op("tensor", lambda t, dc=dc, q0=q0, qs=qs: t.matmul(
                    ps[qs * 2 + 1][:, 0:512],
                    attnT[:, dc * NQ + q0 + qs * 128: dc * NQ + q0 + qs * 128 + 128],
                    owT[:, dc * D + 512: dc * D + 1024],
                    start=(dc == 0), stop=False),
                    inc=("t", 1) if dc == G - 1 else None)
        for fc in range(FG):
            bi = fc % 3
            wl = list(t_tern_done)
            if fc >= 3 or qt > 0:
                wl.append(("t", wdld_tok[bi]))
            P.op("sync", lambda e, fc=fc, b=bi: e.dma_start(
                out=wdb[b][:, :],
                in_=wdnT_sc[fc * 128:(fc + 1) * 128, 0:D]),
                inc=(wd_sem[bi], 16), waits=wl)
            t_wd = P.tok(wd_sem[bi])
            wl2 = list(hq_done[qt])
            if fc >= 3 or qt > 0:
                wl2.append(("v", hscl_tok[bi]))
            P.op("sync", lambda e, fc=fc, q0=q0, b=bi: e.dma_start(
                out=hld[b][:, :],
                in_=h_sc[fc * 128:(fc + 1) * 128, q0:q0 + QTS]),
                inc=(hl_sem[bi], 16), waits=wl2)
            t_hl = P.tok(hl_sem[bi])
            ths = P.op("vector", lambda v, fc=fc, q0=q0, b=bi: v.tensor_mul(
                hsc[fc % 2][:, :], hld[b][:, :], bcf2a[:, q0:q0 + QTS]),
                inc=("v", 1),
                waits=[(hl_sem[bi], t_hl), ("t", dnmm_tok[fc % 2])])
            hscl_tok[bi] = ths
            wme2 = [(wd_sem[bi], t_wd), ("v", ths)]
            for qs in range(QSUB):
                P.op("tensor", lambda t, fc=fc, qs=qs, b=bi: t.matmul(
                    ps[qs * 2][:, 0:512],
                    hsc[fc % 2][:, qs * 128: qs * 128 + 128],
                    wdb[b][:, 0:512],
                    start=False, stop=(fc == FG - 1)),
                    waits=wme2 if qs == 0 else ())
                P.op("tensor", lambda t, fc=fc, qs=qs, b=bi: t.matmul(
                    ps[qs * 2 + 1][:, 0:512],
                    hsc[fc % 2][:, qs * 128: qs * 128 + 128],
                    wdb[b][:, 512:1024],
                    start=False, stop=(fc == FG - 1)),
                    inc=("t", 1) if qs == QSUB - 1 else None)
            dnmm_tok[fc % 2] = P.tok("t")
            wdld_tok[bi] = dnmm_tok[fc % 2]
        t_dn = P.tok("t")
        for qs in range(QSUB):
            bi = qs % 2
            P.op("sync", lambda e, q0=q0, qs=qs, b=bi: e.dma_start(
                out=ubuf[b][:, :],
                in_=x_own[q0 + qs * 128: q0 + qs * 128 + 128, :]),
                inc=(xo_sem[bi], 16),
                waits=[("v", fevac[bi])] if (qt > 0 or qs > 1) else [])
            t_yl = P.tok(xo_sem[bi])
            P.op("vector", lambda v, qs=qs, b=bi: v.tensor_add(
                ysb[b][:, 0:512], ps[qs * 2][:, 0:512], obc[:, 0:512]),
                waits=[("t", t_dn), (xo_sem[bi], t_yl), ("dy", fyst_tok[bi])])
            P.op("vector", lambda v, qs=qs, b=bi: v.tensor_add(
                ysb[b][:, 512:1024], ps[qs * 2 + 1][:, 0:512], obc[:, 512:1024]),
                inc=("v", 1))
            ta2 = P.tok("v")
            fevac[bi] = P.op("vector", lambda v, b=bi: v.tensor_add(
                ysb[b][:, :], ysb[b][:, :], ubuf[b][:, :]), inc=("v", 1),
                waits=[("v", ta2)])
            P.op("gpsimd", lambda e, q0=q0, qs=qs, b=bi: e.dma_start(
                out=y[q0 + qs * 128: q0 + qs * 128 + 128, :], in_=ysb[b][:, :]),
                inc=("dy", 16), waits=[("v", fevac[bi])])
            fyst_tok[bi] = P.tok("dy")
    scope7.close()
    scopeB.close()

    P.wait("sync", "dy", P.tok("dy"))
    P.wait("sync", "gcast", P.tok("gcast"))

    with nc.Block() as blk:
        P.emit_all(blk)
    es.close()
    return nc


def wtiles(w, WTC):
    R, C = w.shape
    out = []
    if C >= WTC:
        assert C % WTC == 0 and R % 128 == 0
        for r in range(R // 128):
            for cs in range(C // WTC):
                out.append((w[r * 128:(r + 1) * 128, cs * WTC:(cs + 1) * WTC],
                            [[WTC, 128], [1, WTC]]))
    else:
        rpt = 128 * (WTC // C)
        assert R % rpt == 0
        na = rpt // 128
        for r in range(R // rpt):
            out.append((w[r * rpt:(r + 1) * rpt, :].rearrange(
                "(a p) c -> p a c", p=128),
                [[WTC, 128], [C, na], [1, C]]))
    return out


def qT_all_slice(qTall, og, q0, QTS, NQ):
    return qTall[:, og * NQ + q0: og * NQ + q0 + QTS]


def kernel(**inputs):
    cfg = full_cfg()
    S, NQ, D = cfg["S"], cfg["NQ"], cfg["D"]
    x = np.ascontiguousarray(np.asarray(inputs["L0_attn_x"], dtype=np.float32))
    B = x.shape[0]
    ncores = 8
    cpb = ncores // B

    f32c = lambda a: np.ascontiguousarray(np.asarray(a, dtype=np.float32))
    common = dict(
        q_w=f32c(inputs["L0_attn_q_w"]),
        q_b=f32c(inputs["L0_attn_q_b"]).reshape(D, 1),
        k_w=f32c(inputs["L0_attn_k_w"]),
        k_b=f32c(inputs["L0_attn_k_b"]).reshape(D, 1),
        v_w=f32c(inputs["L0_attn_v_w"]),
        v_b=f32c(inputs["L0_attn_v_b"]).reshape(D, 1),
        out_w=f32c(inputs["L0_attn_out_w"]),
        out_b=f32c(inputs["L0_attn_out_b"]).reshape(1, D),
        conv_w=f32c(inputs["L0_attn_conv_w"]).reshape(D, 3),
        conv_b=f32c(inputs["L0_attn_conv_b"]).reshape(D, 1),
        w_up=f32c(inputs["L0_ffn_w_up"]),
        w_down=f32c(inputs["L0_ffn_w_down"]),
        ident=np.eye(128, dtype=ml_dtypes.bfloat16),
    )
    in_maps = []
    for c in range(ncores):
        b, qc = divmod(c, cpb)
        m = dict(common)
        m["x_ctx"] = np.ascontiguousarray(x[b])
        m["x_own"] = np.ascontiguousarray(x[b, qc * NQ:(qc + 1) * NQ])
        in_maps.append(m)

    nc = build_nc(cfg)
    res = run_bass_kernel_spmd(nc, in_maps, core_ids=list(range(ncores)))
    out = np.empty((B, S, D), dtype=np.float32)
    for c in range(ncores):
        b, qc = divmod(c, cpb)
        out[b, qc * NQ:(qc + 1) * NQ] = res.results[c]["y"]
    return out


# version 45
# speedup vs baseline: 1.6881x; 1.2323x over previous
import numpy as np
import ml_dtypes
from contextlib import ExitStack

import concourse.bass as bass
import concourse.mybir as mybir
from concourse.bass_utils import run_bass_kernel_spmd

F32 = mybir.dt.float32
BF16 = mybir.dt.bfloat16
AF = mybir.ActivationFunctionType
ALU = mybir.AluOpType
AX = mybir.AxisListType

ENG_NAMES = ("sync", "gpsimd", "vector", "scalar", "tensor")


def full_cfg():
    return dict(S=8192, NQ=2048, D=1024, H=16, F=4096, CKV=512, QTS=512,
                WTC=1024)


class Prog:

    def __init__(self, nc, es):
        self.nc = nc
        self.es = es
        self.sems = {}
        self.cnt = {}
        self.stream = {e: [] for e in ENG_NAMES}
        self.lastw = {e: {} for e in ENG_NAMES}

    def sem(self, name):
        if name not in self.sems:
            self.sems[name] = self.es.enter_context(self.nc.semaphore(name))
            self.cnt[name] = 0
        return self.sems[name]

    def tok(self, name):
        self.sem(name)
        return self.cnt[name]

    def wait(self, engname, semname, val):
        if val is None or val <= 0:
            return
        s = self.sem(semname)
        lw = self.lastw[engname]
        if lw.get(semname, 0) >= val:
            return
        lw[semname] = val
        self.stream[engname].append(lambda eng, s=s, v=val: eng.wait_ge(s, v))

    def op(self, engname, build, inc=None, waits=()):
        for (sn, v) in waits:
            self.wait(engname, sn, v)
        incs = []
        tok = None
        if inc is not None:
            sn, n = inc
            self.sem(sn)
            self.cnt[sn] += n
            tok = self.cnt[sn]
            incs.append((self.sems[sn], n))

        def emit(eng, build=build, incs=incs):
            ins = build(eng)
            for (s, n) in incs:
                ins = ins.then_inc(s, n)

        self.stream[engname].append(emit)
        return tok

    def fence(self, exclude=("g", "gwa0", "gwa1")):
        for sn in list(self.cnt.keys()):
            if sn in exclude:
                continue
            v = self.cnt[sn]
            for e in ENG_NAMES:
                self.wait(e, sn, v)

    def emit_all(self, blk):
        for name in ENG_NAMES:
            fns = self.stream[name]

            def runner(eng, fns=fns):
                for fn in fns:
                    fn(eng)

            getattr(blk, name)(runner)


def build_nc(cfg):
    S, NQ, D, H, F = cfg["S"], cfg["NQ"], cfg["D"], cfg["H"], cfg["F"]
    CKV, QTS, WTC = cfg["CKV"], cfg["QTS"], cfg["WTC"]
    G = D // 128
    FG = F // 128
    KV = S // 2
    KVG = KV // 128
    NCH = KV // CKV
    NQT = NQ // QTS
    QSUB = QTS // 128
    DH = D // H
    assert DH == 64
    SCALE = 1.0 / float(np.sqrt(DH))
    XT = S // 128
    XOT = NQ // 128
    CS = 2 * CKV
    CW = CS + 128
    NJ = CKV // 128
    VROW = H * 65

    nc = bass.Bass("TRN2", target_bir_lowering=False)

    def din(name, shape):
        return nc.dram_tensor(name, shape, F32, kind="ExternalInput")

    x_own = din("x_own", [NQ, D])
    x_ctx = din("x_ctx", [S, D])
    q_w = din("q_w", [D, D]); q_b = din("q_b", [D, 1])
    k_w = din("k_w", [D, D]); k_b = din("k_b", [D, 1])
    v_w = din("v_w", [D, D]); v_b = din("v_b", [D, 1])
    out_w = din("out_w", [D, D]); out_b = din("out_b", [1, D])
    conv_w = din("conv_w", [D, 3]); conv_b = din("conv_b", [D, 1])
    w_up = din("w_up", [F, D])
    w_down = din("w_down", [D, F])
    y = nc.dram_tensor("y", [NQ, D], F32, kind="ExternalOutput")

    ident_in = nc.dram_tensor("ident", [128, 128], BF16, kind="ExternalInput")

    xn_fm_sc = nc.dram_tensor("xn_fm_sc", [D, S], BF16)
    xn_own_fm_sc = nc.dram_tensor("xn_own_fm_sc", [D, NQ], BF16)
    r1_sc = nc.dram_tensor("r1_sc", [NQ, 1], F32)
    wq_bf = nc.dram_tensor("wq_bf", [D, D], BF16)
    wk_bf = nc.dram_tensor("wk_bf", [D, D], BF16)
    wv_bf = nc.dram_tensor("wv_bf", [D, D], BF16)
    wo_bf = nc.dram_tensor("wo_bf", [D, D], BF16)
    kt_sc = nc.dram_tensor("kt_sc", [D, KV], BF16)
    vp_sc = nc.dram_tensor("vp_sc", [KV, VROW], BF16)
    wup_t_sc = nc.dram_tensor("wup_t_sc", [F, D], BF16)
    wdn_t_sc = nc.dram_tensor("wdn_t_sc", [D, F], BF16)
    wupT_sc = nc.dram_tensor("wupT_sc", [D, F], BF16)
    wdnT_sc = nc.dram_tensor("wdnT_sc", [F, D], BF16)

    es = ExitStack()
    P = Prog(nc, es)

    def sb(name, shape, dt, scope=es):
        return scope.enter_context(nc.sbuf_tensor(name, shape, dt))

    qb_sb = sb("qb_sb", [128, G], F32)
    kb_sb = sb("kb_sb", [128, G], F32)
    vb_sb = sb("vb_sb", [128, G], F32)
    cb_sb = sb("cb_sb", [128, G], F32)
    cw_sb = sb("cw_sb", [128, 3 * G], F32)
    ones_bf = sb("ones_bf", [128, 1], BF16)
    ones_f = sb("ones_f", [128, 1], F32)
    ones_row_f = sb("ones_row_f", [1, 128], F32)
    ones_row_b = sb("ones_row_b", [1, 128], BF16)
    obc = sb("obc", [128, D], F32)
    den = sb("den", [1, 1024], F32)
    recb = sb("recb", [1, 1024], BF16)
    bcf = sb("bcf", [128, QTS], F32)
    ubuf = [sb(f"ubuf{i}", [128, D], F32) for i in range(2)]
    ysb = [sb(f"ysb{i}", [128, D], F32) for i in range(2)]
    thr_t = [sb(f"thr{i}", [128, 1], F32) for i in range(2)]
    nthr_t = [sb(f"nthr{i}", [128, 1], F32) for i in range(2)]
    WTCA = 2048
    asum_p = [sb(f"asum_p{i}", [128, 1], F32) for i in range(2)]
    rsum = [sb(f"rsum{i}", [128, 1], F32) for i in range(2)]
    epsln = sb("epsln", [128, 1], F32)
    ident_sb = sb("ident_sb", [128, 128], BF16)
    scopeW = ExitStack()
    ob_row = sb("ob_row", [1, D], F32, scopeW)
    wbuf = [sb(f"wbuf{i}", [128, WTCA], F32, scopeW) for i in range(2)]
    absb2 = [sb(f"absb2{i}", [128, WTCA], BF16, scopeW) for i in range(3)]

    pW3 = es.enter_context(nc.psum_tensor("pw3", [128, 1024], F32))
    pG, pH = pW3[:, 0:512], pW3[:, 512:1024]
    pAL = [pG, pH]
    scopeT = ExitStack()
    pT = [scopeT.enter_context(nc.psum_tensor(f"pt{i}", [128, 1024], BF16))
          for i in range(2)]

    for (src, dst) in ((q_w, wq_bf), (k_w, wk_bf), (v_w, wv_bf), (out_w, wo_bf)):
        P.op("gpsimd", lambda g, s=src, d=dst:
             g.dma_start(out=d[:, :], in_=s[:, :]), inc=("gcast", 16))
    t_cast = P.tok("gcast")
    for (vec, tile) in ((q_b, qb_sb), (k_b, kb_sb), (v_b, vb_sb), (conv_b, cb_sb)):
        for g in range(G):
            P.op("sync", lambda e, v=vec, t=tile, g=g:
                 e.dma_start(out=t[:, g:g + 1], in_=v[g * 128:(g + 1) * 128, 0:1]),
                 inc=("dsm", 16))
    for g in range(G):
        P.op("sync", lambda e, g=g: e.dma_start(
            out=cw_sb[:, g * 3:(g + 1) * 3], in_=conv_w[g * 128:(g + 1) * 128, :]),
            inc=("dsm", 16))
    P.op("sync", lambda e: e.dma_start(out=ob_row[:, :], in_=out_b[0:1, :]),
         inc=("dsm", 16))
    P.op("sync", lambda e: e.dma_start(out=ident_sb[:, :], in_=ident_in[:, :]),
         inc=("dsm", 16))
    t_dsm = P.tok("dsm")

    P.op("vector", lambda v: v.memset(ones_bf[:, :], 1.0))
    P.op("vector", lambda v: v.memset(ones_f[:, :], 1.0))
    P.op("vector", lambda v: v.memset(ones_row_f[:, :], 1.0))
    P.op("vector", lambda v: v.memset(ones_row_b[:, :], 1.0))
    P.op("vector", lambda v: v.memset(epsln[:, :], 1e-5), inc=("v", 1))
    t_const = P.tok("v")

    t1 = P.op("tensor", lambda t: t.matmul(pG[:, 0:512], ones_row_f[:, :],
                                           ob_row[0:1, 0:512], start=True, stop=True),
              inc=("t", 1), waits=[("dsm", t_dsm), ("v", t_const)])
    t2 = P.op("tensor", lambda t: t.matmul(pH[:, 0:512], ones_row_f[:, :],
                                           ob_row[0:1, 512:1024], start=True, stop=True),
              inc=("t", 1))
    P.op("vector", lambda v: v.tensor_copy(obc[:, 0:512], pG[:, 0:512]),
         waits=[("t", t1)])
    t_obc = P.op("vector", lambda v: v.tensor_copy(obc[:, 512:1024], pH[:, 0:512]),
                 inc=("v", 1), waits=[("t", t2)])

    ga_sem = ["gwa0", "gwa1"]
    abs_tok = [0, 0]
    pe_alpha_tok = [0, 0, 0]
    alpha_jobs = []
    for wi, wsrc in enumerate((w_up, w_down)):
        nt = len(wtiles(wsrc, WTCA))
        for it in range(nt):
            alpha_jobs.append((wi, wsrc, it, it == 0, it == nt - 1))
    n_alpha = len(alpha_jobs)
    alpha_done_tok = []

    aq = []

    def emit_alpha_mm(item):
        j, ta = item
        wi, wsrc, it, first, last = alpha_jobs[j]
        ai = j % 3
        nsub = WTCA // 512
        for k in range(nsub):
            wm = [("s", ta)] if k == 0 else []
            if first and k == 0:
                wm.append(("v", t_obc))
            P.op("tensor", lambda t, a=ai, k=k, wi=wi: t.matmul(
                pAL[wi][0:1, 0:512], ones_bf[:, :],
                absb2[a][:, k * 512:(k + 1) * 512],
                start=(first and k == 0), stop=(last and k == nsub - 1)),
                waits=wm, inc=("t", 1) if k == nsub - 1 else None)
        pe_alpha_tok[ai] = P.tok("t")
        if last:
            P.op("vector", lambda v, wi=wi: v.reduce_sum(
                asum_p[wi][0:1, 0:1], pAL[wi][0:1, 0:512], axis=AX.X),
                inc=("v", 1), waits=[("t", pe_alpha_tok[ai])])
            alpha_done_tok.append(P.tok("v"))

    def emit_alpha_job(j):
        wi, wsrc, it, first, last = alpha_jobs[j]
        bi = j % 2
        ai = j % 3
        P.op("sync", lambda g, it=it, b=bi, w=wsrc:
             g.dma_start(out=bass.AP(wbuf[b], 0, wtiles(w, WTCA)[it][1]),
                         in_=wtiles(w, WTCA)[it][0]),
             inc=(ga_sem[bi], 16), waits=[("s", abs_tok[bi])])
        tld = P.tok(ga_sem[bi])
        abs_tok[bi] = P.op("scalar", lambda s, b=bi, a=ai: s.activation(
            absb2[a][:, :], wbuf[b][:, :], AF.Abs),
            inc=("s", 1), waits=[(ga_sem[bi], tld), ("t", pe_alpha_tok[ai])])
        aq.append((j, abs_tok[bi]))
        if len(aq) > 2:
            emit_alpha_mm(aq.pop(0))

    NB = 4
    with ExitStack() as scope1:
        xbuf4 = [sb(f"xbuf4{i}", [128, NB * D], F32, scope1) for i in range(3)]
        nbuf4 = [sb(f"nbuf4{i}", [128, NB * D], BF16, scope1) for i in range(3)]
        fmb = [sb(f"fmb{i}", [128, G * NB * 128], BF16, scope1) for i in range(2)]
        stat2 = [sb(f"stat{i}", [128, 12], F32, scope1) for i in range(3)]
        mv2 = [sb(f"mv{i}", [128, 2 * NB], F32, scope1) for i in range(3)]
        lnt2 = [sb(f"lnt{i}", [128, 3 * NB], F32, scope1) for i in range(3)]
        sqt2 = [sb(f"sqt{i}", [128, 3 * NB], F32, scope1) for i in range(3)]

        ln_jobs = [(x_ctx, xn_fm_sc, j * NB * 128) for j in range(XT // NB)] + \
                  [(x_own, xn_own_fm_sc, j * NB * 128) for j in range(XOT // NB)]
        aj = [0]
        norm_tok = [0, 0, 0]
        aggr2 = [0, 0, 0]
        tp_tok = [0, 0, 0]
        ev_tok = [0, 0]
        fst_tok = [0, 0]
        r1st_tok = [0, 0, 0]
        nmr_tok = 0
        xl_sem = ["dl0", "dl1", "dl2"]
        fm_sems = ["dfm0", "dfm1"]
        for i, (src_, dst, r0) in enumerate(ln_jobs):
            bi = i % 3
            fi = i % 2
            stat, mv, lnt = stat2[bi], mv2[bi], lnt2[bi]
            aggr_tok = aggr2[bi]
            wl = []
            if i >= 3:
                wl += [("s", norm_tok[bi])]
            P.op("sync", lambda e, s=src_, r0=r0, b=bi: e.dma_start(
                out=xbuf4[b][:, 0:NB * D],
                in_=s[r0:r0 + NB * 128, :].rearrange("(a p) c -> p a c", p=128)),
                inc=(xl_sem[bi], 16), waits=wl)
            tld = P.tok(xl_sem[bi])
            for sub in range(NB):
                P.op("vector", lambda v, b=bi, sub=sub, stat=stat: v.bn_stats(
                    stat[:, 0:6], xbuf4[b][:, sub * D: sub * D + 512]),
                    waits=[(xl_sem[bi], tld), ("v", aggr_tok),
                           ("s", norm_tok[bi])])
                P.op("vector", lambda v, b=bi, sub=sub, stat=stat: v.bn_stats(
                    stat[:, 6:12], xbuf4[b][:, sub * D + 512: sub * D + 1024]),
                    inc=("v", 1))
                ts_ = P.tok("v")
                aggr_tok = P.op("vector", lambda v, sub=sub, stat=stat, mv=mv:
                                v.bn_aggr(mv[:, sub * 2:sub * 2 + 2], stat[:, :]),
                                inc=("v", 1), waits=[("v", ts_)])
            aggr2[bi] = aggr_tok
            P.op("scalar", lambda s, mv=mv, lnt=lnt: s.activation(
                lnt[:, 0:NB], mv[:, 1:2 * NB:2], AF.Ln,
                bias=epsln[:, 0:1], scale=1.0),
                inc=("s", 1), waits=[("v", aggr_tok), ("v", t_const)])
            tl1 = P.tok("s")
            P.op("scalar", lambda s, lnt=lnt: s.activation(
                lnt[:, NB:2 * NB], lnt[:, 0:NB], AF.Exp, bias=0.0, scale=-0.5),
                inc=("s", 1), waits=[("s", tl1)])
            tl2 = P.tok("s")
            nmr_tok = P.op("vector", lambda v, mv=mv, lnt=lnt:
                           v.scalar_tensor_tensor(
                               lnt[:, 2 * NB:3 * NB], mv[:, 0:2 * NB:2], -1.0,
                               lnt[:, NB:2 * NB], ALU.mult, ALU.mult),
                           inc=("v", 1), waits=[("s", tl2)])
            if dst is xn_own_fm_sc:
                sq = sqt2[bi]
                P.op("scalar", lambda s, mv=mv, sq=sq: s.activation(
                    sq[:, 0:NB], mv[:, 1:2 * NB:2], AF.Sqrt),
                    inc=("s", 1), waits=[("s", tl2), ("dr1", r1st_tok[bi])])
                tsq = P.tok("s")
                P.op("vector", lambda v, lnt=lnt, sq=sq: v.tensor_mul(
                    sq[:, NB:2 * NB], sq[:, 0:NB], lnt[:, NB:2 * NB]),
                    waits=[("s", tsq)])
                P.op("vector", lambda v, sq=sq: v.tensor_scalar(
                    sq[:, NB:2 * NB], sq[:, NB:2 * NB], 1e-8, None, ALU.add))
                P.op("vector", lambda v, sq=sq: v.reciprocal(
                    sq[:, 2 * NB:3 * NB], sq[:, NB:2 * NB]), inc=("v", 1))
                tr1 = P.tok("v")
                P.op("gpsimd", lambda e, r0=r0, sq=sq: e.dma_start(
                    out=r1_sc[r0:r0 + NB * 128, 0:1].rearrange(
                        "(a p) c -> p a c", p=128),
                    in_=sq[:, 2 * NB:3 * NB],
                    allow_slow_non_contiguous=True),
                    inc=("dr1", 16), waits=[("v", tr1)])
                r1st_tok[bi] = P.tok("dr1")
            for sub in range(NB):
                P.op("scalar", lambda s, b=bi, sub=sub, lnt=lnt:
                     s.activation(
                         nbuf4[b][:, sub * D:(sub + 1) * D],
                         xbuf4[b][:, sub * D:(sub + 1) * D],
                         AF.Identity,
                         bias=lnt[:, 2 * NB + sub:2 * NB + sub + 1],
                         scale=lnt[:, NB + sub:NB + sub + 1]),
                     waits=[("v", nmr_tok), ("t", tp_tok[bi])],
                     inc=("s", 1) if sub == NB - 1 else None)
            norm_tok[bi] = P.tok("s")
            for sub in range(NB):
                tb = (i * NB + sub) % 2
                for g in range(G):
                    wt = []
                    if g == 0:
                        wt = [("s", norm_tok[bi]), ("v", ev_tok[tb]),
                              ("dsm", t_dsm)]
                    P.op("tensor", lambda t, b=bi, sub=sub, g=g, tb=tb:
                         t.transpose(
                             pT[tb][:, g * 128:(g + 1) * 128],
                             nbuf4[b][:, sub * D + g * 128: sub * D + g * 128 + 128],
                             ident_sb[:, :]),
                         waits=wt, inc=("t", 1) if g == G - 1 else None)
                ttp = P.tok("t")
                we = [("t", ttp)]
                if sub == 0:
                    we.append((fm_sems[fi], fst_tok[fi]))
                ev_tok[tb] = P.op("vector", lambda v, b=fi, sub=sub, tb=tb:
                                  v.tensor_copy(
                                      bass.AP(fmb[b], sub * 128,
                                              [[G * NB * 128, 128],
                                               [NB * 128, G], [1, 128]]),
                                      pT[tb][:, 0:G * 128]),
                                  inc=("v", 1), waits=we)
            tp_tok[bi] = P.tok("t")
            ev_all = P.tok("v")
            P.op("gpsimd", lambda e, d=dst, r0=r0, b=fi: e.dma_start(
                out=d[0:D, r0:r0 + NB * 128].rearrange("(g p) c -> p g c", p=128),
                in_=fmb[b][:, :]),
                inc=(fm_sems[fi], 16), waits=[("v", ev_all)])
            fst_tok[fi] = P.tok(fm_sems[fi])
            while aj[0] < min(n_alpha, (i + 1) * 2):
                emit_alpha_job(aj[0])
                aj[0] += 1
        while aj[0] < n_alpha:
            emit_alpha_job(aj[0])
            aj[0] += 1
        while aq:
            emit_alpha_mm(aq.pop(0))
        t_lnfm = [("dfm0", P.tok("dfm0")), ("dfm1", P.tok("dfm1"))]
        t_r1done = ("dr1", P.tok("dr1"))
    t_p1_s = P.tok("s")
    t_p1_v = P.tok("v")
    scopeT.close()
    pW = [es.enter_context(nc.psum_tensor(f"pw{i}", [128, 1024], F32))
          for i in range(3)] + [pW3]
    ps = [pW[i // 2][:, (i % 2) * 512:(i % 2) * 512 + 512] for i in range(8)]
    pE, pF = ps[4], ps[5]

    P.fence()
    with ExitStack() as scope3:
        kwT = sb("kwT", [128, G * D], BF16, scope3)
        vwT = sb("vwT", [128, G * D], BF16, scope3)
        xnc = [sb(f"xnc{i}", [128, G * CW], BF16, scope3) for i in range(2)]
        xdT = [sb(f"xdT{i}", [128, G * CKV], BF16, scope3) for i in range(3)]
        cacc = [sb(f"cacc{i}", [128, CKV], F32, scope3) for i in range(2)]
        ktb = [sb(f"ktb{i}", [128, CKV], BF16, scope3) for i in range(2)]
        vpb = [sb(f"vpb{i}", [128, VROW], BF16, scope3) for i in range(2)]

        P.wait("sync", "gcast", t_cast)
        for (wsc, tile) in ((wk_bf, kwT), (wv_bf, vwT)):
            for g in range(G):
                P.op("sync", lambda e, w=wsc, t=tile, g=g: e.dma_start(
                    out=t[:, g * D:(g + 1) * D],
                    in_=w[0:D, g * 128:(g + 1) * 128], transpose=True),
                    inc=("dwt", 16))
        t_kvw = P.tok("dwt")

        P.op("vector", lambda v: v.memset(
            bass.AP(xnc[0], 0, [[xnc[0].shape[1], 128], [CW, G], [1, 128]]), 0.0))
        P.op("vector", lambda v: v.memset(
            bass.AP(vpb[0], 64, [[vpb[0].shape[1], 128], [65, H], [1, 1]]), 1.0))
        t_zp = P.op("vector", lambda v: v.memset(
            bass.AP(vpb[1], 64, [[vpb[1].shape[1], 128], [65, H], [1, 1]]), 1.0),
            inc=("v", 1))

        conv_tok = [0, 0]
        kvmm_tok = [0, 0, 0]
        kevac = [0] * 8
        kst_tok = [0, 0]
        vst_tok = [0, 0]
        tr_sem = ["dc0", "dc1"]
        def emit_loads_conv(c):
            bi = c % 2
            xi = c % 3
            lo = c * CS - 128
            wl = list(t_lnfm)
            if c == 0:
                wl.append(("v", t_zp))
            if c >= 2:
                wl.append(("v", conv_tok[bi]))
            if c == 0:
                P.op("sync", lambda e: e.dma_start(
                    out=bass.AP(xnc[0], 128,
                                [[G * CW, 128], [CW, G], [1, CS]]),
                    in_=xn_fm_sc[0:D, 0:CS].rearrange(
                        "(g p) c -> p g c", p=128)),
                    inc=(tr_sem[bi], 16), waits=wl)
            else:
                P.op("sync", lambda e, lo=lo, b=bi: e.dma_start(
                    out=bass.AP(xnc[b], 0,
                                [[G * CW, 128], [CW, G], [1, CW]]),
                    in_=xn_fm_sc[0:D, lo:lo + CW].rearrange(
                        "(g p) c -> p g c", p=128)),
                    inc=(tr_sem[bi], 16), waits=wl)
            t_tr = P.tok(tr_sem[bi])
            wc = [(tr_sem[bi], t_tr), ("dsm", t_dsm)]
            if c >= 3:
                wc.append(("t", kvmm_tok[xi]))
            prev_g = P.tok("v")
            for g in range(G):
                def xg(off, g=g, b=bi):
                    return bass.AP(xnc[b], g * CW + off,
                                   [[xnc[b].shape[1], 128], [2, CKV]])
                prev_g = P.op("vector", lambda p, g=g, b=bi, xg=xg: p.tensor_scalar(
                    cacc[b][:, :], xg(127), cw_sb[:, g * 3:g * 3 + 1],
                    cb_sb[:, g:g + 1], ALU.mult, ALU.add),
                    inc=("v", 1),
                    waits=(wc if g == 0 else []) + [("v", prev_g)])
                prev_g = P.op("vector", lambda p, g=g, b=bi, xg=xg: p.scalar_tensor_tensor(
                    cacc[1 - b][:, :], xg(128), cw_sb[:, g * 3 + 1:g * 3 + 2],
                    cacc[b][:, :], ALU.mult, ALU.add),
                    inc=("v", 1), waits=[("v", prev_g)])
                prev_g = P.op("vector", lambda p, g=g, b=bi, x=xi, xg=xg: p.scalar_tensor_tensor(
                    xdT[x][:, g * CKV:(g + 1) * CKV], xg(129),
                    cw_sb[:, g * 3 + 2:g * 3 + 3], cacc[1 - b][:, :],
                    ALU.mult, ALU.add),
                    inc=("v", 1), waits=[("v", prev_g)])
            conv_tok[bi] = P.tok("v")

        def emit_kv(c):
            bi = c % 2
            xi = c % 3
            for og in range(G):
                pb = ps[og]
                wme = [("v", conv_tok[bi]), ("dwt", t_kvw), ("v", kevac[og])]
                for ic in range(G):
                    P.op("tensor", lambda t, og=og, ic=ic, x=xi, pb=pb: t.matmul(
                        pb[:, 0:CKV],
                        kwT[:, ic * D + og * 128: ic * D + og * 128 + 128],
                        xdT[x][:, ic * CKV:(ic + 1) * CKV],
                        start=(ic == 0), stop=(ic == G - 1)),
                        waits=wme if ic == 0 else (),
                        inc=("t", 1) if ic == G - 1 else None)
                tmm = P.tok("t")
                we = [("t", tmm), (["dkta", "dktb"][og % 2], kst_tok[og % 2])]
                tv = P.op("vector", lambda v, og=og, pb=pb: v.tensor_scalar(
                    ktb[og % 2][:, :], pb[:, 0:CKV], kb_sb[:, og:og + 1],
                    None, ALU.add), inc=("v", 1), waits=we)
                kevac[og] = tv
                ks = ["dkta", "dktb"][og % 2]
                P.op("gpsimd", lambda e, og=og, c=c: e.dma_start(
                    out=kt_sc[og * 128:(og + 1) * 128, c * CKV:(c + 1) * CKV],
                    in_=ktb[og % 2][:, :]), inc=(ks, 16), waits=[("v", tv)])
                kst_tok[og % 2] = P.tok(ks)

            for jg in range(NJ):
                for half in range(2):
                    pb = ps[jg * 2 + half]
                    wme = [("v", kevac[jg * 2 + half])]
                    for ic in range(G):
                        P.op("tensor", lambda t, jg=jg, half=half, ic=ic, x=xi, pb=pb:
                             t.matmul(
                                 pb[:, 0:512],
                                 xdT[x][:, ic * CKV + jg * 128: ic * CKV + jg * 128 + 128],
                                 vwT[:, ic * D + half * 512: ic * D + half * 512 + 512],
                                 start=(ic == 0), stop=(ic == G - 1)),
                             waits=wme if ic == 0 else (),
                             inc=("t", 1) if ic == G - 1 else None)
                    tmm = P.tok("t")
                    if jg == NJ - 1 and half == 1:
                        kvmm_tok[xi] = tmm
                    kvg = c * NJ + jg
                    we = [("t", tmm)]
                    if half == 0:
                        we.append((["dvpa", "dvpb"][kvg % 2], vst_tok[kvg % 2]))
                    tv = P.op("vector", lambda v, kvg=kvg, half=half, pb=pb:
                              v.tensor_copy(
                                  bass.AP(vpb[kvg % 2], half * 8 * 65,
                                          [[vpb[0].shape[1], 128], [65, 8], [1, 64]]),
                                  bass.AP(pb.tensor, pb.offset,
                                          [[pb.tensor.shape[1], 128], [64, 8], [1, 64]])),
                              inc=("v", 1), waits=we)
                    kevac[jg * 2 + half] = tv
                    if half == 1:
                        vs = ["dvpa", "dvpb"][kvg % 2]
                        P.op("gpsimd", lambda e, kvg=kvg: e.dma_start(
                            out=vp_sc[kvg * 128:(kvg + 1) * 128, :],
                            in_=vpb[kvg % 2][:, :]), inc=(vs, 16),
                            waits=[("v", tv)])
                        vst_tok[kvg % 2] = P.tok(vs)

        for c in range(NCH):
            emit_loads_conv(c)
            if c >= 1:
                emit_kv(c - 1)
        emit_kv(NCH - 1)
        t_kt_done = [("dkta", P.tok("dkta")), ("dktb", P.tok("dktb"))]
        t_vp_done = [("dvpa", P.tok("dvpa")), ("dvpb", P.tok("dvpb"))]
        t_p3_t = P.tok("t")
        t_p3_v = P.tok("v")
        t_p3_g = P.tok("g")

    thr_toks = []
    tv2_prev = 0
    for wi in range(2):
        tv1 = P.op("vector", lambda v, wi=wi: v.tensor_scalar(
            den[0:1, 0:1], asum_p[wi][0:1, 0:1], 0.5 / (F * D), None, ALU.mult),
            inc=("v", 1), waits=[("v", alpha_done_tok[wi]), ("t", t_p3_t),
                                 ("v", t_obc), ("v", tv2_prev)])
        tm2 = P.op("tensor", lambda t: t.matmul(
            pH[:, 0:1], ones_row_f[:, :], den[0:1, 0:1], start=True, stop=True),
            inc=("t", 1), waits=[("v", tv1)])
        tcp = P.op("vector", lambda v, wi=wi: v.tensor_copy(thr_t[wi][:, :], pH[:, 0:1]),
             inc=("v", 1), waits=[("t", tm2)])
        tv2 = P.op("vector", lambda v, wi=wi: v.tensor_scalar(
            nthr_t[wi][:, :], thr_t[wi][:, :], -1.0, None, ALU.mult),
            inc=("v", 1), waits=[("v", tcp)])
        thr_toks.append(tv2)
        tv2_prev = tv2
    scopeW.close()

    P.fence()
    scopeB = ExitStack()
    attnT = sb("attnT", [128, G * NQ], BF16, scopeB)
    scopeQ = ExitStack()
    qTall = sb("qTall", [128, G * NQ], BF16, scopeQ)
    scope4 = ExitStack()
    qwT = sb("qwT", [128, G * D], BF16, scope4)
    xnT = sb("xnT", [128, G * QTS], BF16, scope4)
    P.wait("sync", "t", t_p3_t)
    P.wait("sync", "g", t_p3_g)
    for g in range(G):
        P.op("sync", lambda e, g=g: e.dma_start(
            out=qwT[:, g * D:(g + 1) * D],
            in_=wq_bf[0:D, g * 128:(g + 1) * 128], transpose=True),
            inc=("dwt", 16), waits=[("gcast", t_cast)])
    t_qw = P.tok("dwt")

    xq_sem = ["dq0", "dq1"]
    qp_done = [0, 0]
    qevac = [0, 0, 0, 0]
    for qt in range(NQT):
        q0 = qt * QTS
        wl = list(t_lnfm)
        if qt >= 1:
            wl.append(("t", qp_done[(qt - 1) % 2]))
        P.op("sync", lambda e, q0=q0: e.dma_start(
            out=bass.AP(xnT, 0, [[G * QTS, 128], [QTS, G], [1, QTS]]),
            in_=xn_own_fm_sc[0:D, q0:q0 + QTS].rearrange(
                "(g p) c -> p g c", p=128)),
            inc=(xq_sem[qt % 2], 16), waits=wl)
        t_xq = P.tok(xq_sem[qt % 2])
        for og in range(G):
            pb = ps[og % 4]
            wme = [(xq_sem[qt % 2], t_xq), ("dwt", t_qw), ("s", qevac[og % 4]),
                   ("v", t_p3_v)]
            for ic in range(G):
                P.op("tensor", lambda t, og=og, ic=ic, pb=pb: t.matmul(
                    pb[:, 0:QTS],
                    qwT[:, ic * D + og * 128: ic * D + og * 128 + 128],
                    xnT[:, ic * QTS:(ic + 1) * QTS],
                    start=(ic == 0), stop=(ic == G - 1)),
                    waits=wme if ic == 0 else (),
                    inc=("t", 1) if ic == G - 1 else None)
            tmm = P.tok("t")
            qevac[og % 4] = P.op("scalar", lambda s, og=og, q0=q0, pb=pb: s.activation(
                qT_all_slice(qTall, og, q0, QTS, NQ), pb[:, 0:QTS], AF.Identity,
                bias=qb_sb[:, og:og + 1], scale=1.0),
                inc=("s", 1), waits=[("t", tmm), ("s", t_p1_s)])
        qp_done[qt % 2] = P.tok("t")
    t_qproj_s = P.tok("s")
    t_qproj_t = P.tok("t")
    scope4.close()

    P.fence()
    scope5 = ExitStack()
    tmpn = sb("tmpn", [64, 1024], F32, scope5)
    PTW = [sb(f"PTW{i}", [128, 1024], BF16, scope5) for i in range(2)]
    rawb = sb("rawb", [65, 1024], F32, scope5)
    bcsb = sb("bcsb", [64, 1024], F32, scope5)
    ktg = [sb(f"ktg{i}", [128, KV], BF16, scope5) for i in range(2)]
    vg = [sb(f"vg{i}", [128, KVG * 130], BF16, scope5) for i in range(2)]
    twb = [sb(f"twb{i}", [128, WTC], F32, scope5) for i in range(2)]
    tb = [sb(f"tb{i}", [128, WTC], BF16, scope5) for i in range(2)]
    tbf = [sb(f"tbf{i}", [128, WTC], BF16, scope5) for i in range(2)]

    tern_ld_sem = ["dwb0", "dwb1"]
    tst_tok = [0, 0]
    tern_jobs = []
    for wi, (wsrc, wtsc) in enumerate(((w_up, wup_t_sc), (w_down, wdn_t_sc))):
        for it in range(len(wtiles(wsrc, WTC))):
            tern_jobs.append((wi, wsrc, wtsc, it))

    def emit_tern_job(j):
        wi, wsrc, wtsc, it = tern_jobs[j]
        bi = j % 2
        wl = [("v", thr_toks[1])]
        if j >= 2:
            wl.append((["dwo0", "dwo1"][bi], tst_tok[bi]))
        P.op("sync", lambda e, it=it, b=bi, w=wsrc:
             e.dma_start(out=bass.AP(twb[b], 0, wtiles(w, WTC)[it][1]),
                         in_=wtiles(w, WTC)[it][0]),
             inc=(tern_ld_sem[bi], 16), waits=wl)
        tld = P.tok(tern_ld_sem[bi])
        P.op("vector", lambda p, b=bi, wi=wi: p.tensor_scalar(
            tb[b][:, :], twb[b][:, :], thr_t[wi][:, 0:1], None, ALU.is_gt),
            waits=[(tern_ld_sem[bi], tld)])
        P.op("vector", lambda p, b=bi, wi=wi: p.tensor_scalar(
            tbf[b][:, :], twb[b][:, :], nthr_t[wi][:, 0:1], None, ALU.is_lt))
        tsb_ = P.op("vector", lambda p, b=bi: p.tensor_sub(
            tbf[b][:, :], tb[b][:, :], tbf[b][:, :]), inc=("v", 1))
        P.op("gpsimd", lambda e, it=it, b=bi, wt=wtsc:
             e.dma_start(out=wtiles(wt, WTC)[it][0],
                         in_=bass.AP(tbf[b], 0, wtiles(wt, WTC)[it][1])),
             inc=(["dwo0", "dwo1"][bi], 16), waits=[("v", tsb_)])
        tst_tok[bi] = P.tok(["dwo0", "dwo1"][bi])

    rt_jobs = []
    for g in range(G):
        for fc in range(F // WTC):
            rt_jobs.append((wup_t_sc, wupT_sc, fc * WTC, g))
    for fg in range(FG):
        rt_jobs.append((wdn_t_sc, wdnT_sc, 0, fg))
    rtb = [sb(f"rtb{i}", [128, WTC], BF16, scope5) for i in range(2)]
    rt_ld_sem = ["drb0", "drb1"]
    rt_st_tok = [0, 0]
    t_tern_all = [None]

    def emit_rt_job(j):
        src_sc, dst_sc, r0, g = rt_jobs[j]
        bi = j % 2
        if t_tern_all[0] is None:
            t_tern_all[0] = [("dwo0", P.tok("dwo0")), ("dwo1", P.tok("dwo1"))]
        wl = list(t_tern_all[0])
        if j >= 2:
            wl.append((["dro0", "dro1"][bi], rt_st_tok[bi]))
        P.op("sync", lambda e, b=bi, s=src_sc, r0=r0, g=g:
             e.dma_start(out=rtb[b][:, :],
                         in_=s[r0:r0 + WTC, g * 128:(g + 1) * 128],
                         transpose=True),
             inc=(rt_ld_sem[bi], 16), waits=wl)
        tld = P.tok(rt_ld_sem[bi])
        P.op("gpsimd", lambda e, b=bi, d=dst_sc, r0=r0, g=g:
             e.dma_start(out=d[g * 128:(g + 1) * 128, r0:r0 + WTC],
                         in_=rtb[b][:, :]),
             inc=(["dro0", "dro1"][bi], 16), waits=[(rt_ld_sem[bi], tld)])
        rt_st_tok[bi] = P.tok(["dro0", "dro1"][bi])

    n_tern = len(tern_jobs)
    n_bg = n_tern + len(rt_jobs)
    tern_per_gqt = (n_bg + G * NQT - 1) // (G * NQT)

    P.wait("sync", "t", t_qproj_t)
    kv_sem = ["dk0", "dk1"]
    exp_tok = [0, 0]
    av_tok = [0, 0]
    lastexp = 0
    lastnorm = 0
    kvload_tok = [0, 0]
    pending_norm = [None]
    pEF, pGH = pW[2], pW[3]
    P.rawcopy_tok = 0
    P.lastnorm_new = 0
    ti = 0
    for g in range(G):
        gb = g % 2
        wl = t_kt_done + t_vp_done
        if g >= 2:
            wl.append(("t", kvload_tok[gb]))
        P.op("sync", lambda e, g=g, b=gb: e.dma_start(
            out=ktg[b][:, :], in_=kt_sc[g * 128:(g + 1) * 128, :]),
            inc=(kv_sem[gb], 16), waits=wl)
        P.op("sync", lambda e, g=g, b=gb: e.dma_start(
            out=vg[b][:, :],
            in_=bass.AP(vp_sc, 2 * g * 65,
                        [[VROW, 128], [128 * VROW, KVG], [1, 130]])),
            inc=(kv_sem[gb], 16))
        t_kv = P.tok(kv_sem[gb])

        for qt in range(NQT):
            q0 = qt * QTS

            def emit_qk(kc, g=g, qt=qt, q0=q0, gb=gb, t_kv=t_kv):
                pq = pW[kc % 2]
                wme = [("s", t_qproj_s), (kv_sem[gb], t_kv)]
                if kc >= 2:
                    wme.append(("s", exp_tok[kc % 2]))
                else:
                    wme.append(("s", lastexp_l[0]))
                P.op("tensor", lambda t, kc=kc, pq=pq: t.matmul(
                    pq[:, 0:QTS], ktg[gb][0:64, kc * 128:(kc + 1) * 128],
                    qTall[0:64, g * NQ + q0: g * NQ + q0 + QTS],
                    start=True, stop=True),
                    waits=wme)
                P.op("tensor", lambda t, kc=kc, pq=pq: t.matmul(
                    pq[:, 512:512 + QTS], ktg[gb][64:128, kc * 128:(kc + 1) * 128],
                    qTall[64:128, g * NQ + q0: g * NQ + q0 + QTS],
                    start=True, stop=True),
                    inc=("t", 1))
                tqk_l[kc % 2] = P.tok("t")

            def emit_exp_av(kc, g=g, gb=gb):
                pq = pW[kc % 2]
                we = [("t", tqk_l[kc % 2])]
                if kc >= 2:
                    we.append(("t", av_tok[kc % 2]))
                if QTS == 512:
                    te = P.op("scalar", lambda s, kc=kc, pq=pq: s.activation(
                        PTW[kc % 2][:, :], pq[:, :], AF.Exp,
                        bias=0.0, scale=SCALE), inc=("s", 1), waits=we)
                else:
                    P.op("scalar", lambda s, kc=kc, pq=pq: s.activation(
                        PTW[kc % 2][:, 0:QTS], pq[:, 0:QTS], AF.Exp,
                        bias=0.0, scale=SCALE), waits=we)
                    te = P.op("scalar", lambda s, kc=kc, pq=pq: s.activation(
                        PTW[kc % 2][:, 512:512 + QTS], pq[:, 512:512 + QTS],
                        AF.Exp, bias=0.0, scale=SCALE), inc=("s", 1))
                exp_tok[kc % 2] = te
                wa = [("s", te)]
                if kc == 0:
                    wa.append(("v", P.rawcopy_tok))
                P.op("tensor", lambda t, kc=kc: t.matmul(
                    pE[0:65, 0:QTS],
                    vg[gb][:, kc * 130: kc * 130 + 65],
                    PTW[kc % 2][:, 0:QTS],
                    start=(kc == 0), stop=(kc == KVG - 1)),
                    waits=wa)
                av_tok[kc % 2] = P.op("tensor", lambda t, kc=kc: t.matmul(
                    pF[0:65, 0:QTS],
                    vg[gb][:, kc * 130 + 65: kc * 130 + 130],
                    PTW[kc % 2][:, 512:512 + QTS],
                    start=(kc == 0), stop=(kc == KVG - 1)),
                    inc=("t", 1))

            tqk_l = [0, 0]
            lastexp_l = [lastexp]
            it_target = (g * NQT + qt + 1) * tern_per_gqt
            for kc in range(KVG):
                emit_qk(kc)
                if kc == 1 and pending_norm[0] is not None:
                    pending_norm[0]()
                    pending_norm[0] = None
                if kc == 3:
                    while ti < min(n_bg, it_target):
                        if ti < n_tern:
                            emit_tern_job(ti)
                        else:
                            emit_rt_job(ti - n_tern)
                        ti += 1
                if kc >= 1:
                    emit_exp_av(kc - 1)
            emit_exp_av(KVG - 1)
            lastexp = P.tok("s")
            t_av = P.tok("t")
            if qt == NQT - 1:
                kvload_tok[gb] = t_av

            def norm_block(g=g, q0=q0, t_av=t_av):
                def dual(x, p0, np_):
                    tn = x.tensor if isinstance(x, bass.AP) else x
                    off = x.offset if isinstance(x, bass.AP) else 0
                    pitch = tn.shape[1]
                    return bass.AP(tn, off + p0 * pitch,
                                   [[pitch, np_], [512, 2], [1, QTS]])
                trw = P.op("vector", lambda v: v.tensor_copy(
                    dual(rawb, 0, 65), dual(pEF, 0, 65)), inc=("v", 1),
                    waits=[("t", t_av), ("v", P.lastnorm_new)])
                P.rawcopy_tok = trw
                trec = P.op("vector", lambda v: v.tensor_copy(
                    dual(recb, 0, 1), dual(rawb, 64, 1)),
                    inc=("v", 1), waits=[("v", trw)])
                tb0 = P.op("tensor", lambda t: t.matmul(
                    pG[0:64, 0:QTS], ones_row_b[0:1, 0:64], recb[0:1, 0:QTS],
                    start=True, stop=True), waits=[("v", trec)])
                tb1 = P.op("tensor", lambda t: t.matmul(
                    pH[0:64, 0:QTS], ones_row_b[0:1, 0:64],
                    recb[0:1, 512:512 + QTS],
                    start=True, stop=True), inc=("t", 1))
                tbc = P.tok("t")
                tcp = P.op("vector", lambda v: v.tensor_copy(
                    dual(bcsb, 0, 64), dual(pGH, 0, 64)), inc=("v", 1),
                    waits=[("t", tbc)])
                trc2 = P.op("vector", lambda v: v.reciprocal(
                    dual(tmpn, 0, 64), dual(bcsb, 0, 64)),
                    inc=("v", 1), waits=[("v", tcp)])
                tml = P.op("vector", lambda v: v.tensor_tensor(
                    dual(bcsb, 0, 64), dual(rawb, 0, 64), dual(tmpn, 0, 64),
                    ALU.mult),
                    inc=("v", 1), waits=[("v", trc2)])
                P.op("vector", lambda v: v.tensor_scalar(
                    attnT[0:64, g * NQ + q0: g * NQ + q0 + QTS],
                    bcsb[0:64, 0:QTS], vb_sb[0:64, g:g + 1], None, ALU.add),
                    waits=[("v", tml)])
                tnm = P.op("vector", lambda v: v.tensor_scalar(
                    attnT[64:128, g * NQ + q0: g * NQ + q0 + QTS],
                    bcsb[0:64, 512:512 + QTS], vb_sb[64:128, g:g + 1],
                    None, ALU.add), inc=("v", 1))
                P.lastnorm_new = tnm

            pending_norm[0] = norm_block
    if pending_norm[0] is not None:
        pending_norm[0]()
        pending_norm[0] = None
    while ti < n_bg:
        if ti < n_tern:
            emit_tern_job(ti)
        else:
            emit_rt_job(ti - n_tern)
        ti += 1
    t_attn_v = P.tok("v")
    t_attn_t = P.tok("t")
    t_attn_s = P.tok("s")
    t_tern_done = [("dro0", P.tok("dro0")), ("dro1", P.tok("dro1"))]
    scope5.close()

    scopeQ.close()
    P.fence()
    scope7 = ExitStack()
    wupT = sb("wupT", [128, G * F], BF16, scope7)
    owT = sb("owT", [128, G * D], BF16, scope7)
    xnT7 = [sb(f"xnT7{i}", [128, G * QTS], BF16, scope7) for i in range(2)]
    hT = sb("hT", [128, FG * QTS], BF16, scope7)
    h2s = [sb(f"h2s{i}", [128, QTS], BF16, scope7) for i in range(3)]
    wdb = [sb(f"wdb{i}", [128, D], BF16, scope7) for i in range(2)]
    r1row = sb("r1row", [1, NQ], BF16, scope7)
    bcf1a = sb("bcf1a", [128, NQ], BF16, scope7)

    P.wait("sync", "t", t_attn_t)
    for g in range(G):
        P.op("sync", lambda e, g=g: e.dma_start(
            out=wupT[:, g * F:(g + 1) * F],
            in_=wupT_sc[g * 128:(g + 1) * 128, 0:F]),
            inc=("dft", 16), waits=list(t_tern_done))
        P.op("sync", lambda e, g=g: e.dma_start(
            out=owT[:, g * D:(g + 1) * D],
            in_=wo_bf[0:D, g * 128:(g + 1) * 128], transpose=True),
            inc=("dft", 16), waits=[("gcast", t_cast)])
    t_fw = P.tok("dft")

    xq2_sem = ["df0", "df1"]
    wd_sem = ["dd0", "dd1"]
    xo_sem = ["do0", "do1"]
    gelu_tok = [0, 0, 0, 0]
    h2_tok = [0, 0, 0]
    fevac = [0, 0]
    fyst_tok = [0, 0]
    xnorm_tok = 0
    wdld_tok = [0, 0]
    prev_l2 = 0

    P.op("gpsimd", lambda e: e.dma_start(
        out=r1row[0:1, :], in_=r1_sc[0:NQ, 0:1].rearrange("a c -> c a")),
        inc=("dr1r", 16), waits=[t_r1done])
    t_r1r = P.tok("dr1r")
    bc1_prev = 0
    for qt in range(NQT):
        tbm1 = P.op("tensor", lambda t, qt=qt: t.matmul(
            pH[:, 0:QTS], ones_row_b[0:1, :],
            r1row[0:1, qt * QTS:(qt + 1) * QTS], start=True, stop=True),
            inc=("t", 1), waits=[("dr1r", t_r1r), ("v", bc1_prev)])
        bc1_prev = P.op("vector", lambda v, qt=qt: v.tensor_copy(
            bcf1a[:, qt * QTS:(qt + 1) * QTS], pH[:, 0:QTS]),
            inc=("v", 1), waits=[("t", tbm1)])
    t_bc1a = bc1_prev

    upmm_done = [0, 0]
    for qt in range(NQT):
        q0 = qt * QTS
        x7 = xnT7[qt % 2]
        wl = list(t_lnfm)
        if qt >= 2:
            wl.append(("t", upmm_done[qt % 2]))
        P.op("sync", lambda e, q0=q0, x7=x7: e.dma_start(
            out=bass.AP(x7, 0, [[G * QTS, 128], [QTS, G], [1, QTS]]),
            in_=xn_own_fm_sc[0:D, q0:q0 + QTS].rearrange(
                "(g p) c -> p g c", p=128)),
            inc=(xq2_sem[qt % 2], 16), waits=wl)
        t_x7 = P.tok(xq2_sem[qt % 2])
        for g in range(G):
            wv = [(xq2_sem[qt % 2], t_x7)] if g == 0 else []
            if g == 0:
                wv.append(("v", t_bc1a))
                if qt >= 1:
                    wv.append(("v", xnorm_tok))
            P.op("vector", lambda v, g=g, q0=q0, x7=x7: v.tensor_mul(
                x7[:, g * QTS:(g + 1) * QTS], x7[:, g * QTS:(g + 1) * QTS],
                bcf1a[:, q0:q0 + QTS]),
                waits=wv, inc=("v", 1) if g == G - 1 else None)
        xnorm_tok = P.tok("v")

        h2q = []

        def emit_h2m(item, qt=qt):
            fc, tsq = item
            wm = [("v", tsq)]
            if fc == 0:
                wm.append(("s", prev_l2))
            h2_tok[fc % 3] = P.op("tensor", lambda t, fc=fc: t.matmul(
                pG[0:1, 0:QTS], ones_bf[:, :], h2s[fc % 3][:, :],
                start=(fc == 0), stop=(fc == FG - 1)),
                inc=("t", 1), waits=wm)

        for fc in range(FG):
            pb = ps[fc % 4]
            wme = [("v", xnorm_tok), ("dft", t_fw)]
            if fc >= 4 or qt > 0:
                wme.append(("s", gelu_tok[fc % 4]))
            if qt > 0 and fc < 4:
                wme += [("v", fevac[0]), ("v", fevac[1])]
            for dc in range(G):
                P.op("tensor", lambda t, fc=fc, dc=dc, pb=pb, x7=x7: t.matmul(
                    pb[:, 0:QTS],
                    wupT[:, dc * F + fc * 128: dc * F + fc * 128 + 128],
                    x7[:, dc * QTS:(dc + 1) * QTS],
                    start=(dc == 0), stop=(dc == G - 1)),
                    waits=wme if dc == 0 else (),
                    inc=("t", 1) if dc == G - 1 else None)
            tmm = P.tok("t")
            if fc == FG - 1:
                upmm_done[qt % 2] = tmm
            tg = P.op("scalar", lambda s, fc=fc, pb=pb: s.activation(
                hT[:, fc * QTS:(fc + 1) * QTS], pb[:, 0:QTS], AF.Gelu),
                inc=("s", 1), waits=[("t", tmm)])
            gelu_tok[fc % 4] = tg
            wv = [("s", tg)]
            if fc >= 3 or qt > 0:
                wv.append(("t", h2_tok[fc % 3]))
            P.op("vector", lambda v, fc=fc: v.tensor_mul(
                h2s[fc % 3][:, :], hT[:, fc * QTS:(fc + 1) * QTS],
                hT[:, fc * QTS:(fc + 1) * QTS]), inc=("v", 1), waits=wv)
            h2q.append((fc, P.tok("v")))
            if fc >= 2:
                emit_h2m(h2q.pop(0))
        while h2q:
            emit_h2m(h2q.pop(0))
        t_h = P.tok("s")
        t_s2 = P.tok("t")
        P.op("scalar", lambda s: s.activation(den[0:1, 0:QTS], pG[0:1, 0:QTS],
                                              AF.Ln, bias=0.0, scale=1.0),
             inc=("s", 1), waits=[("t", t_s2)])
        t_l2 = P.tok("s")
        prev_l2 = t_l2
        P.op("scalar", lambda s: s.activation(den[0:1, 512:512 + QTS],
                                              den[0:1, 0:QTS], AF.Exp,
                                              bias=0.0, scale=0.5),
             inc=("s", 1), waits=[("s", t_l2)])
        t_e2 = P.tok("s")
        tq1 = P.op("vector", lambda v: v.tensor_scalar(
            den[0:1, 0:QTS], den[0:1, 512:512 + QTS],
            1.0 / float(np.sqrt(F)), 1e-8, ALU.mult, ALU.add),
            inc=("v", 1), waits=[("s", t_e2)])
        tq2 = P.op("vector", lambda v: v.reciprocal(
            den[0:1, 512:512 + QTS], den[0:1, 0:QTS]),
            inc=("v", 1), waits=[("v", tq1)])
        P.op("vector", lambda v: v.tensor_copy(
            recb[0:1, 0:QTS], den[0:1, 512:512 + QTS]),
            inc=("v", 1), waits=[("v", tq2)])
        t_r2 = P.tok("v")
        tbm2 = P.op("tensor", lambda t: t.matmul(
            pH[:, 0:QTS], ones_row_b[0:1, :], recb[0:1, 0:QTS],
            start=True, stop=True), inc=("t", 1), waits=[("v", t_r2)])
        P.op("vector", lambda v: v.tensor_copy(bcf[:, 0:QTS], pH[:, 0:QTS]),
             inc=("v", 1), waits=[("t", tbm2)])
        t_bc2 = P.tok("v")
        wme = [("v", t_attn_v), ("dft", t_fw), ("s", t_h), ("v", t_bc2)]
        if qt > 0:
            wme += [("v", fevac[0]), ("v", fevac[1])]
        for qs in range(QSUB):
            for dc in range(G):
                P.op("tensor", lambda t, dc=dc, q0=q0, qs=qs: t.matmul(
                    ps[qs * 2][:, 0:512],
                    attnT[:, dc * NQ + q0 + qs * 128: dc * NQ + q0 + qs * 128 + 128],
                    owT[:, dc * D: dc * D + 512],
                    start=(dc == 0), stop=False),
                    waits=wme if (qs == 0 and dc == 0) else ())
                P.op("tensor", lambda t, dc=dc, q0=q0, qs=qs: t.matmul(
                    ps[qs * 2 + 1][:, 0:512],
                    attnT[:, dc * NQ + q0 + qs * 128: dc * NQ + q0 + qs * 128 + 128],
                    owT[:, dc * D + 512: dc * D + 1024],
                    start=(dc == 0), stop=False),
                    inc=("t", 1) if dc == G - 1 else None)
        for fc in range(FG):
            bi = fc % 2
            wl = list(t_tern_done)
            if fc >= 2 or qt > 0:
                wl.append(("t", wdld_tok[bi]))
            P.op("sync", lambda e, fc=fc, b=bi: e.dma_start(
                out=wdb[b][:, :],
                in_=wdnT_sc[fc * 128:(fc + 1) * 128, 0:D]),
                inc=(wd_sem[bi], 16), waits=wl)
            t_wd = P.tok(wd_sem[bi])
            ths = P.op("vector", lambda v, fc=fc: v.tensor_mul(
                hT[:, fc * QTS:(fc + 1) * QTS], hT[:, fc * QTS:(fc + 1) * QTS],
                bcf[:, 0:QTS]),
                inc=("v", 1), waits=[("v", t_bc2)])
            wme2 = [(wd_sem[bi], t_wd), ("v", ths)]
            for qs in range(QSUB):
                P.op("tensor", lambda t, fc=fc, qs=qs, b=bi: t.matmul(
                    ps[qs * 2][:, 0:512],
                    hT[:, fc * QTS + qs * 128: fc * QTS + qs * 128 + 128],
                    wdb[b][:, 0:512],
                    start=False, stop=(fc == FG - 1)),
                    waits=wme2 if qs == 0 else ())
                P.op("tensor", lambda t, fc=fc, qs=qs, b=bi: t.matmul(
                    ps[qs * 2 + 1][:, 0:512],
                    hT[:, fc * QTS + qs * 128: fc * QTS + qs * 128 + 128],
                    wdb[b][:, 512:1024],
                    start=False, stop=(fc == FG - 1)),
                    inc=("t", 1) if qs == QSUB - 1 else None)
            wdld_tok[bi] = P.tok("t")
        t_dn = P.tok("t")
        for qs in range(QSUB):
            bi = qs % 2
            P.op("sync", lambda e, q0=q0, qs=qs, b=bi: e.dma_start(
                out=ubuf[b][:, :],
                in_=x_own[q0 + qs * 128: q0 + qs * 128 + 128, :]),
                inc=(xo_sem[bi], 16),
                waits=[("v", fevac[bi])] if (qt > 0 or qs > 1) else [])
            t_yl = P.tok(xo_sem[bi])
            P.op("vector", lambda v, qs=qs, b=bi: v.tensor_add(
                ysb[b][:, 0:512], ps[qs * 2][:, 0:512], obc[:, 0:512]),
                waits=[("t", t_dn), (xo_sem[bi], t_yl), ("dy", fyst_tok[bi])])
            P.op("vector", lambda v, qs=qs, b=bi: v.tensor_add(
                ysb[b][:, 512:1024], ps[qs * 2 + 1][:, 0:512], obc[:, 512:1024]),
                inc=("v", 1))
            ta2 = P.tok("v")
            fevac[bi] = P.op("vector", lambda v, b=bi: v.tensor_add(
                ysb[b][:, :], ysb[b][:, :], ubuf[b][:, :]), inc=("v", 1),
                waits=[("v", ta2)])
            P.op("gpsimd", lambda e, q0=q0, qs=qs, b=bi: e.dma_start(
                out=y[q0 + qs * 128: q0 + qs * 128 + 128, :], in_=ysb[b][:, :]),
                inc=("dy", 16), waits=[("v", fevac[bi])])
            fyst_tok[bi] = P.tok("dy")
    scope7.close()
    scopeB.close()

    P.wait("sync", "dy", P.tok("dy"))
    P.wait("sync", "gcast", P.tok("gcast"))

    with nc.Block() as blk:
        P.emit_all(blk)
    es.close()
    return nc


def wtiles(w, WTC):
    R, C = w.shape
    out = []
    if C >= WTC:
        assert C % WTC == 0 and R % 128 == 0
        for r in range(R // 128):
            for cs in range(C // WTC):
                out.append((w[r * 128:(r + 1) * 128, cs * WTC:(cs + 1) * WTC],
                            [[WTC, 128], [1, WTC]]))
    else:
        rpt = 128 * (WTC // C)
        assert R % rpt == 0
        na = rpt // 128
        for r in range(R // rpt):
            out.append((w[r * rpt:(r + 1) * rpt, :].rearrange(
                "(a p) c -> p a c", p=128),
                [[WTC, 128], [C, na], [1, C]]))
    return out


def qT_all_slice(qTall, og, q0, QTS, NQ):
    return qTall[:, og * NQ + q0: og * NQ + q0 + QTS]


def kernel(**inputs):
    cfg = full_cfg()
    S, NQ, D = cfg["S"], cfg["NQ"], cfg["D"]
    x = np.ascontiguousarray(np.asarray(inputs["L0_attn_x"], dtype=np.float32))
    B = x.shape[0]
    ncores = 8
    cpb = ncores // B

    f32c = lambda a: np.ascontiguousarray(np.asarray(a, dtype=np.float32))
    common = dict(
        q_w=f32c(inputs["L0_attn_q_w"]),
        q_b=f32c(inputs["L0_attn_q_b"]).reshape(D, 1),
        k_w=f32c(inputs["L0_attn_k_w"]),
        k_b=f32c(inputs["L0_attn_k_b"]).reshape(D, 1),
        v_w=f32c(inputs["L0_attn_v_w"]),
        v_b=f32c(inputs["L0_attn_v_b"]).reshape(D, 1),
        out_w=f32c(inputs["L0_attn_out_w"]),
        out_b=f32c(inputs["L0_attn_out_b"]).reshape(1, D),
        conv_w=f32c(inputs["L0_attn_conv_w"]).reshape(D, 3),
        conv_b=f32c(inputs["L0_attn_conv_b"]).reshape(D, 1),
        w_up=f32c(inputs["L0_ffn_w_up"]),
        w_down=f32c(inputs["L0_ffn_w_down"]),
        ident=np.eye(128, dtype=ml_dtypes.bfloat16),
    )
    in_maps = []
    for c in range(ncores):
        b, qc = divmod(c, cpb)
        m = dict(common)
        m["x_ctx"] = np.ascontiguousarray(x[b])
        m["x_own"] = np.ascontiguousarray(x[b, qc * NQ:(qc + 1) * NQ])
        in_maps.append(m)

    nc = build_nc(cfg)
    res = run_bass_kernel_spmd(nc, in_maps, core_ids=list(range(ncores)))
    out = np.empty((B, S, D), dtype=np.float32)
    for c in range(ncores):
        b, qc = divmod(c, cpb)
        out[b, qc * NQ:(qc + 1) * NQ] = res.results[c]["y"]
    return out


# version 46
# speedup vs baseline: 1.7010x; 1.0076x over previous
import numpy as np
import ml_dtypes
from contextlib import ExitStack

import concourse.bass as bass
import concourse.mybir as mybir
from concourse.bass_utils import run_bass_kernel_spmd

F32 = mybir.dt.float32
BF16 = mybir.dt.bfloat16
AF = mybir.ActivationFunctionType
ALU = mybir.AluOpType
AX = mybir.AxisListType

ENG_NAMES = ("sync", "gpsimd", "vector", "scalar", "tensor")


def full_cfg():
    return dict(S=8192, NQ=2048, D=1024, H=16, F=4096, CKV=512, QTS=512,
                WTC=1024)


class Prog:

    def __init__(self, nc, es):
        self.nc = nc
        self.es = es
        self.sems = {}
        self.cnt = {}
        self.stream = {e: [] for e in ENG_NAMES}
        self.lastw = {e: {} for e in ENG_NAMES}

    def sem(self, name):
        if name not in self.sems:
            self.sems[name] = self.es.enter_context(self.nc.semaphore(name))
            self.cnt[name] = 0
        return self.sems[name]

    def tok(self, name):
        self.sem(name)
        return self.cnt[name]

    def wait(self, engname, semname, val):
        if val is None or val <= 0:
            return
        s = self.sem(semname)
        lw = self.lastw[engname]
        if lw.get(semname, 0) >= val:
            return
        lw[semname] = val
        self.stream[engname].append(lambda eng, s=s, v=val: eng.wait_ge(s, v))

    def op(self, engname, build, inc=None, waits=()):
        for (sn, v) in waits:
            self.wait(engname, sn, v)
        incs = []
        tok = None
        if inc is not None:
            sn, n = inc
            self.sem(sn)
            self.cnt[sn] += n
            tok = self.cnt[sn]
            incs.append((self.sems[sn], n))

        def emit(eng, build=build, incs=incs):
            ins = build(eng)
            for (s, n) in incs:
                ins = ins.then_inc(s, n)

        self.stream[engname].append(emit)
        return tok

    def fence(self, exclude=("g", "gwa0", "gwa1")):
        for sn in list(self.cnt.keys()):
            if sn in exclude:
                continue
            v = self.cnt[sn]
            for e in ENG_NAMES:
                self.wait(e, sn, v)

    def emit_all(self, blk):
        for name in ENG_NAMES:
            fns = self.stream[name]

            def runner(eng, fns=fns):
                for fn in fns:
                    fn(eng)

            getattr(blk, name)(runner)


def build_nc(cfg):
    S, NQ, D, H, F = cfg["S"], cfg["NQ"], cfg["D"], cfg["H"], cfg["F"]
    CKV, QTS, WTC = cfg["CKV"], cfg["QTS"], cfg["WTC"]
    G = D // 128
    FG = F // 128
    KV = S // 2
    KVG = KV // 128
    NCH = KV // CKV
    NQT = NQ // QTS
    QSUB = QTS // 128
    DH = D // H
    assert DH == 64
    SCALE = 1.0 / float(np.sqrt(DH))
    XT = S // 128
    XOT = NQ // 128
    CS = 2 * CKV
    CW = CS + 128
    NJ = CKV // 128
    VROW = H * 65

    nc = bass.Bass("TRN2", target_bir_lowering=False)

    def din(name, shape):
        return nc.dram_tensor(name, shape, F32, kind="ExternalInput")

    x_own = din("x_own", [NQ, D])
    x_ctx = din("x_ctx", [S, D])
    q_w = din("q_w", [D, D]); q_b = din("q_b", [D, 1])
    k_w = din("k_w", [D, D]); k_b = din("k_b", [D, 1])
    v_w = din("v_w", [D, D]); v_b = din("v_b", [D, 1])
    out_w = din("out_w", [D, D]); out_b = din("out_b", [1, D])
    conv_w = din("conv_w", [D, 3]); conv_b = din("conv_b", [D, 1])
    w_up = din("w_up", [F, D])
    w_down = din("w_down", [D, F])
    y = nc.dram_tensor("y", [NQ, D], F32, kind="ExternalOutput")

    ident_in = nc.dram_tensor("ident", [128, 128], BF16, kind="ExternalInput")

    xn_fm_sc = nc.dram_tensor("xn_fm_sc", [D, S], BF16)
    xn_own_fm_sc = nc.dram_tensor("xn_own_fm_sc", [D, NQ], BF16)
    r1_sc = nc.dram_tensor("r1_sc", [NQ, 1], F32)
    wq_bf = nc.dram_tensor("wq_bf", [D, D], BF16)
    wk_bf = nc.dram_tensor("wk_bf", [D, D], BF16)
    wv_bf = nc.dram_tensor("wv_bf", [D, D], BF16)
    wo_bf = nc.dram_tensor("wo_bf", [D, D], BF16)
    kt_sc = nc.dram_tensor("kt_sc", [D, KV], BF16)
    vp_sc = nc.dram_tensor("vp_sc", [KV, VROW], BF16)
    wup_t_sc = nc.dram_tensor("wup_t_sc", [F, D], BF16)
    wdn_t_sc = nc.dram_tensor("wdn_t_sc", [D, F], BF16)
    wupT_sc = nc.dram_tensor("wupT_sc", [D, F], BF16)
    wdnT_sc = nc.dram_tensor("wdnT_sc", [F, D], BF16)

    es = ExitStack()
    P = Prog(nc, es)

    def sb(name, shape, dt, scope=es):
        return scope.enter_context(nc.sbuf_tensor(name, shape, dt))

    qb_sb = sb("qb_sb", [128, G], F32)
    kb_sb = sb("kb_sb", [128, G], F32)
    vb_sb = sb("vb_sb", [128, G], F32)
    cb_sb = sb("cb_sb", [128, G], F32)
    cw_sb = sb("cw_sb", [128, 3 * G], F32)
    ones_bf = sb("ones_bf", [128, 1], BF16)
    ones_f = sb("ones_f", [128, 1], F32)
    ones_row_f = sb("ones_row_f", [1, 128], F32)
    ones_row_b = sb("ones_row_b", [1, 128], BF16)
    obc = sb("obc", [128, D], F32)
    den = sb("den", [1, 1024], F32)
    recb = sb("recb", [1, 1024], BF16)
    bcf = sb("bcf", [128, QTS], F32)
    ubuf = [sb(f"ubuf{i}", [128, D], F32) for i in range(2)]
    ysb = [sb(f"ysb{i}", [128, D], F32) for i in range(2)]
    thr_t = [sb(f"thr{i}", [128, 1], F32) for i in range(2)]
    nthr_t = [sb(f"nthr{i}", [128, 1], F32) for i in range(2)]
    WTCA = 2048
    asum_p = [sb(f"asum_p{i}", [128, 1], F32) for i in range(2)]
    rsum = [sb(f"rsum{i}", [128, 1], F32) for i in range(2)]
    epsln = sb("epsln", [128, 1], F32)
    ident_sb = sb("ident_sb", [128, 128], BF16)
    scopeW = ExitStack()
    ob_row = sb("ob_row", [1, D], F32, scopeW)
    wbuf = [sb(f"wbuf{i}", [128, WTCA], F32, scopeW) for i in range(2)]
    absb2 = [sb(f"absb2{i}", [128, WTCA], BF16, scopeW) for i in range(3)]

    pW3 = es.enter_context(nc.psum_tensor("pw3", [128, 1024], F32))
    pG, pH = pW3[:, 0:512], pW3[:, 512:1024]
    pAL = [pG, pH]
    scopeT = ExitStack()
    pT = [scopeT.enter_context(nc.psum_tensor(f"pt{i}", [128, 1024], BF16))
          for i in range(2)]

    for (src, dst) in ((q_w, wq_bf), (k_w, wk_bf), (v_w, wv_bf), (out_w, wo_bf)):
        P.op("gpsimd", lambda g, s=src, d=dst:
             g.dma_start(out=d[:, :], in_=s[:, :]), inc=("gcast", 16))
    t_cast = P.tok("gcast")
    for (vec, tile) in ((q_b, qb_sb), (k_b, kb_sb), (v_b, vb_sb), (conv_b, cb_sb)):
        for g in range(G):
            P.op("sync", lambda e, v=vec, t=tile, g=g:
                 e.dma_start(out=t[:, g:g + 1], in_=v[g * 128:(g + 1) * 128, 0:1]),
                 inc=("dsm", 16))
    for g in range(G):
        P.op("sync", lambda e, g=g: e.dma_start(
            out=cw_sb[:, g * 3:(g + 1) * 3], in_=conv_w[g * 128:(g + 1) * 128, :]),
            inc=("dsm", 16))
    P.op("sync", lambda e: e.dma_start(out=ob_row[:, :], in_=out_b[0:1, :]),
         inc=("dsm", 16))
    P.op("sync", lambda e: e.dma_start(out=ident_sb[:, :], in_=ident_in[:, :]),
         inc=("dsm", 16))
    t_dsm = P.tok("dsm")

    P.op("vector", lambda v: v.memset(ones_bf[:, :], 1.0))
    P.op("vector", lambda v: v.memset(ones_f[:, :], 1.0))
    P.op("vector", lambda v: v.memset(ones_row_f[:, :], 1.0))
    P.op("vector", lambda v: v.memset(ones_row_b[:, :], 1.0))
    P.op("vector", lambda v: v.memset(epsln[:, :], 1e-5), inc=("v", 1))
    t_const = P.tok("v")

    t1 = P.op("tensor", lambda t: t.matmul(pG[:, 0:512], ones_row_f[:, :],
                                           ob_row[0:1, 0:512], start=True, stop=True),
              inc=("t", 1), waits=[("dsm", t_dsm), ("v", t_const)])
    t2 = P.op("tensor", lambda t: t.matmul(pH[:, 0:512], ones_row_f[:, :],
                                           ob_row[0:1, 512:1024], start=True, stop=True),
              inc=("t", 1))
    P.op("vector", lambda v: v.tensor_copy(obc[:, 0:512], pG[:, 0:512]),
         waits=[("t", t1)])
    t_obc = P.op("vector", lambda v: v.tensor_copy(obc[:, 512:1024], pH[:, 0:512]),
                 inc=("v", 1), waits=[("t", t2)])

    ga_sem = ["gwa0", "gwa1"]
    abs_tok = [0, 0]
    pe_alpha_tok = [0, 0, 0]
    alpha_jobs = []
    for wi, wsrc in enumerate((w_up, w_down)):
        nt = len(wtiles(wsrc, WTCA))
        for it in range(nt):
            alpha_jobs.append((wi, wsrc, it, it == 0, it == nt - 1))
    n_alpha = len(alpha_jobs)
    alpha_done_tok = []

    aq = []

    def emit_alpha_mm(item):
        j, ta = item
        wi, wsrc, it, first, last = alpha_jobs[j]
        ai = j % 3
        nsub = WTCA // 512
        for k in range(nsub):
            wm = [("s", ta)] if k == 0 else []
            if first and k == 0:
                wm.append(("v", t_obc))
            P.op("tensor", lambda t, a=ai, k=k, wi=wi: t.matmul(
                pAL[wi][0:1, 0:512], ones_bf[:, :],
                absb2[a][:, k * 512:(k + 1) * 512],
                start=(first and k == 0), stop=(last and k == nsub - 1)),
                waits=wm, inc=("t", 1) if k == nsub - 1 else None)
        pe_alpha_tok[ai] = P.tok("t")
        if last:
            P.op("vector", lambda v, wi=wi: v.reduce_sum(
                asum_p[wi][0:1, 0:1], pAL[wi][0:1, 0:512], axis=AX.X),
                inc=("v", 1), waits=[("t", pe_alpha_tok[ai])])
            alpha_done_tok.append(P.tok("v"))

    def emit_alpha_job(j):
        wi, wsrc, it, first, last = alpha_jobs[j]
        bi = j % 2
        ai = j % 3
        P.op("sync", lambda g, it=it, b=bi, w=wsrc:
             g.dma_start(out=bass.AP(wbuf[b], 0, wtiles(w, WTCA)[it][1]),
                         in_=wtiles(w, WTCA)[it][0]),
             inc=(ga_sem[bi], 16), waits=[("s", abs_tok[bi])])
        tld = P.tok(ga_sem[bi])
        abs_tok[bi] = P.op("scalar", lambda s, b=bi, a=ai: s.activation(
            absb2[a][:, :], wbuf[b][:, :], AF.Abs),
            inc=("s", 1), waits=[(ga_sem[bi], tld), ("t", pe_alpha_tok[ai])])
        aq.append((j, abs_tok[bi]))
        if len(aq) > 2:
            emit_alpha_mm(aq.pop(0))

    NB = 4
    with ExitStack() as scope1:
        xbuf4 = [sb(f"xbuf4{i}", [128, NB * D], F32, scope1) for i in range(3)]
        nbuf4 = [sb(f"nbuf4{i}", [128, NB * D], BF16, scope1) for i in range(3)]
        fmb = [sb(f"fmb{i}", [128, G * NB * 128], BF16, scope1) for i in range(2)]
        stat2 = [sb(f"stat{i}", [128, 12], F32, scope1) for i in range(3)]
        mv2 = [sb(f"mv{i}", [128, 2 * NB], F32, scope1) for i in range(3)]
        lnt2 = [sb(f"lnt{i}", [128, 3 * NB], F32, scope1) for i in range(3)]
        sqt2 = [sb(f"sqt{i}", [128, 3 * NB], F32, scope1) for i in range(3)]

        ln_jobs = [(x_ctx, xn_fm_sc, j * NB * 128) for j in range(XT // NB)] + \
                  [(x_own, xn_own_fm_sc, j * NB * 128) for j in range(XOT // NB)]
        aj = [0]
        norm_tok = [0, 0, 0]
        aggr2 = [0, 0, 0]
        tp_tok = [0, 0, 0]
        ev_tok = [0, 0]
        fst_tok = [0, 0]
        r1st_tok = [0, 0, 0]
        nmr_tok = 0
        xl_sem = ["dl0", "dl1", "dl2"]
        fm_sems = ["dfm0", "dfm1"]
        for i, (src_, dst, r0) in enumerate(ln_jobs):
            bi = i % 3
            fi = i % 2
            stat, mv, lnt = stat2[bi], mv2[bi], lnt2[bi]
            aggr_tok = aggr2[bi]
            wl = []
            if i >= 3:
                wl += [("s", norm_tok[bi])]
            P.op("sync", lambda e, s=src_, r0=r0, b=bi: e.dma_start(
                out=xbuf4[b][:, 0:NB * D],
                in_=s[r0:r0 + NB * 128, :].rearrange("(a p) c -> p a c", p=128)),
                inc=(xl_sem[bi], 16), waits=wl)
            tld = P.tok(xl_sem[bi])
            for sub in range(NB):
                P.op("vector", lambda v, b=bi, sub=sub, stat=stat: v.bn_stats(
                    stat[:, 0:6], xbuf4[b][:, sub * D: sub * D + 512]),
                    waits=[(xl_sem[bi], tld), ("v", aggr_tok),
                           ("s", norm_tok[bi])])
                P.op("vector", lambda v, b=bi, sub=sub, stat=stat: v.bn_stats(
                    stat[:, 6:12], xbuf4[b][:, sub * D + 512: sub * D + 1024]),
                    inc=("v", 1))
                ts_ = P.tok("v")
                aggr_tok = P.op("vector", lambda v, sub=sub, stat=stat, mv=mv:
                                v.bn_aggr(mv[:, sub * 2:sub * 2 + 2], stat[:, :]),
                                inc=("v", 1), waits=[("v", ts_)])
            aggr2[bi] = aggr_tok
            P.op("scalar", lambda s, mv=mv, lnt=lnt: s.activation(
                lnt[:, 0:NB], mv[:, 1:2 * NB:2], AF.Ln,
                bias=epsln[:, 0:1], scale=1.0),
                inc=("s", 1), waits=[("v", aggr_tok), ("v", t_const)])
            tl1 = P.tok("s")
            P.op("scalar", lambda s, lnt=lnt: s.activation(
                lnt[:, NB:2 * NB], lnt[:, 0:NB], AF.Exp, bias=0.0, scale=-0.5),
                inc=("s", 1), waits=[("s", tl1)])
            tl2 = P.tok("s")
            nmr_tok = P.op("vector", lambda v, mv=mv, lnt=lnt:
                           v.scalar_tensor_tensor(
                               lnt[:, 2 * NB:3 * NB], mv[:, 0:2 * NB:2], -1.0,
                               lnt[:, NB:2 * NB], ALU.mult, ALU.mult),
                           inc=("v", 1), waits=[("s", tl2)])
            if dst is xn_own_fm_sc:
                sq = sqt2[bi]
                P.op("scalar", lambda s, mv=mv, sq=sq: s.activation(
                    sq[:, 0:NB], mv[:, 1:2 * NB:2], AF.Sqrt),
                    inc=("s", 1), waits=[("s", tl2), ("dr1", r1st_tok[bi])])
                tsq = P.tok("s")
                P.op("vector", lambda v, lnt=lnt, sq=sq: v.tensor_mul(
                    sq[:, NB:2 * NB], sq[:, 0:NB], lnt[:, NB:2 * NB]),
                    waits=[("s", tsq)])
                P.op("vector", lambda v, sq=sq: v.tensor_scalar(
                    sq[:, NB:2 * NB], sq[:, NB:2 * NB], 1e-8, None, ALU.add))
                P.op("vector", lambda v, sq=sq: v.reciprocal(
                    sq[:, 2 * NB:3 * NB], sq[:, NB:2 * NB]), inc=("v", 1))
                tr1 = P.tok("v")
                P.op("gpsimd", lambda e, r0=r0, sq=sq: e.dma_start(
                    out=r1_sc[r0:r0 + NB * 128, 0:1].rearrange(
                        "(a p) c -> p a c", p=128),
                    in_=sq[:, 2 * NB:3 * NB],
                    allow_slow_non_contiguous=True),
                    inc=("dr1", 16), waits=[("v", tr1)])
                r1st_tok[bi] = P.tok("dr1")
            for sub in range(NB):
                P.op("scalar", lambda s, b=bi, sub=sub, lnt=lnt:
                     s.activation(
                         nbuf4[b][:, sub * D:(sub + 1) * D],
                         xbuf4[b][:, sub * D:(sub + 1) * D],
                         AF.Identity,
                         bias=lnt[:, 2 * NB + sub:2 * NB + sub + 1],
                         scale=lnt[:, NB + sub:NB + sub + 1]),
                     waits=[("v", nmr_tok), ("t", tp_tok[bi])],
                     inc=("s", 1) if sub == NB - 1 else None)
            norm_tok[bi] = P.tok("s")
            for sub in range(NB):
                tb = (i * NB + sub) % 2
                for g in range(G):
                    wt = []
                    if g == 0:
                        wt = [("s", norm_tok[bi]), ("v", ev_tok[tb]),
                              ("dsm", t_dsm)]
                    P.op("tensor", lambda t, b=bi, sub=sub, g=g, tb=tb:
                         t.transpose(
                             pT[tb][:, g * 128:(g + 1) * 128],
                             nbuf4[b][:, sub * D + g * 128: sub * D + g * 128 + 128],
                             ident_sb[:, :]),
                         waits=wt, inc=("t", 1) if g == G - 1 else None)
                ttp = P.tok("t")
                we = [("t", ttp)]
                if sub == 0:
                    we.append((fm_sems[fi], fst_tok[fi]))
                ev_tok[tb] = P.op("vector", lambda v, b=fi, sub=sub, tb=tb:
                                  v.tensor_copy(
                                      bass.AP(fmb[b], sub * 128,
                                              [[G * NB * 128, 128],
                                               [NB * 128, G], [1, 128]]),
                                      pT[tb][:, 0:G * 128]),
                                  inc=("v", 1), waits=we)
            tp_tok[bi] = P.tok("t")
            ev_all = P.tok("v")
            P.op("gpsimd", lambda e, d=dst, r0=r0, b=fi: e.dma_start(
                out=d[0:D, r0:r0 + NB * 128].rearrange("(g p) c -> p g c", p=128),
                in_=fmb[b][:, :]),
                inc=(fm_sems[fi], 16), waits=[("v", ev_all)])
            fst_tok[fi] = P.tok(fm_sems[fi])
            while aj[0] < min(n_alpha, (i + 1) * 2):
                emit_alpha_job(aj[0])
                aj[0] += 1
        while aj[0] < n_alpha:
            emit_alpha_job(aj[0])
            aj[0] += 1
        while aq:
            emit_alpha_mm(aq.pop(0))
        t_lnfm = [("dfm0", P.tok("dfm0")), ("dfm1", P.tok("dfm1"))]
        t_r1done = ("dr1", P.tok("dr1"))
    t_p1_s = P.tok("s")
    t_p1_v = P.tok("v")
    scopeT.close()
    pW = [es.enter_context(nc.psum_tensor(f"pw{i}", [128, 1024], F32))
          for i in range(3)] + [pW3]
    ps = [pW[i // 2][:, (i % 2) * 512:(i % 2) * 512 + 512] for i in range(8)]
    pE, pF = ps[4], ps[5]

    P.fence()
    with ExitStack() as scope3:
        kwT = sb("kwT", [128, G * D], BF16, scope3)
        vwT = sb("vwT", [128, G * D], BF16, scope3)
        xnc = [sb(f"xnc{i}", [128, G * CW], BF16, scope3) for i in range(2)]
        xdT = [sb(f"xdT{i}", [128, G * CKV], BF16, scope3) for i in range(3)]
        cacc = [sb(f"cacc{i}", [128, CKV], F32, scope3) for i in range(2)]
        ktb = [sb(f"ktb{i}", [128, CKV], BF16, scope3) for i in range(2)]
        vpb = [sb(f"vpb{i}", [128, VROW], BF16, scope3) for i in range(2)]

        P.wait("sync", "gcast", t_cast)
        for (wsc, tile) in ((wk_bf, kwT), (wv_bf, vwT)):
            for g in range(G):
                P.op("sync", lambda e, w=wsc, t=tile, g=g: e.dma_start(
                    out=t[:, g * D:(g + 1) * D],
                    in_=w[0:D, g * 128:(g + 1) * 128], transpose=True),
                    inc=("dwt", 16))
        t_kvw = P.tok("dwt")

        P.op("vector", lambda v: v.memset(
            bass.AP(xnc[0], 0, [[xnc[0].shape[1], 128], [CW, G], [1, 128]]), 0.0))
        P.op("vector", lambda v: v.memset(
            bass.AP(vpb[0], 64, [[vpb[0].shape[1], 128], [65, H], [1, 1]]), 1.0))
        t_zp = P.op("vector", lambda v: v.memset(
            bass.AP(vpb[1], 64, [[vpb[1].shape[1], 128], [65, H], [1, 1]]), 1.0),
            inc=("v", 1))

        conv_tok = [0, 0]
        kvmm_tok = [0, 0, 0]
        kevac = [0] * 8
        kst_tok = [0, 0]
        vst_tok = [0, 0]
        tr_sem = ["dc0", "dc1"]
        def emit_loads_conv(c):
            bi = c % 2
            xi = c % 3
            lo = c * CS - 128
            wl = list(t_lnfm)
            if c == 0:
                wl.append(("v", t_zp))
            if c >= 2:
                wl.append(("v", conv_tok[bi]))
            if c == 0:
                P.op("sync", lambda e: e.dma_start(
                    out=bass.AP(xnc[0], 128,
                                [[G * CW, 128], [CW, G], [1, CS]]),
                    in_=xn_fm_sc[0:D, 0:CS].rearrange(
                        "(g p) c -> p g c", p=128)),
                    inc=(tr_sem[bi], 16), waits=wl)
            else:
                P.op("sync", lambda e, lo=lo, b=bi: e.dma_start(
                    out=bass.AP(xnc[b], 0,
                                [[G * CW, 128], [CW, G], [1, CW]]),
                    in_=xn_fm_sc[0:D, lo:lo + CW].rearrange(
                        "(g p) c -> p g c", p=128)),
                    inc=(tr_sem[bi], 16), waits=wl)
            t_tr = P.tok(tr_sem[bi])
            wc = [(tr_sem[bi], t_tr), ("dsm", t_dsm)]
            if c >= 3:
                wc.append(("t", kvmm_tok[xi]))
            prev_g = P.tok("v")
            for g in range(G):
                def xg(off, g=g, b=bi):
                    return bass.AP(xnc[b], g * CW + off,
                                   [[xnc[b].shape[1], 128], [2, CKV]])
                prev_g = P.op("vector", lambda p, g=g, b=bi, xg=xg: p.tensor_scalar(
                    cacc[b][:, :], xg(127), cw_sb[:, g * 3:g * 3 + 1],
                    cb_sb[:, g:g + 1], ALU.mult, ALU.add),
                    inc=("v", 1),
                    waits=(wc if g == 0 else []) + [("v", prev_g)])
                prev_g = P.op("vector", lambda p, g=g, b=bi, xg=xg: p.scalar_tensor_tensor(
                    cacc[1 - b][:, :], xg(128), cw_sb[:, g * 3 + 1:g * 3 + 2],
                    cacc[b][:, :], ALU.mult, ALU.add),
                    inc=("v", 1), waits=[("v", prev_g)])
                prev_g = P.op("vector", lambda p, g=g, b=bi, x=xi, xg=xg: p.scalar_tensor_tensor(
                    xdT[x][:, g * CKV:(g + 1) * CKV], xg(129),
                    cw_sb[:, g * 3 + 2:g * 3 + 3], cacc[1 - b][:, :],
                    ALU.mult, ALU.add),
                    inc=("v", 1), waits=[("v", prev_g)])
            conv_tok[bi] = P.tok("v")

        def emit_kv(c):
            bi = c % 2
            xi = c % 3
            for og in range(G):
                pb = ps[og]
                wme = [("v", conv_tok[bi]), ("dwt", t_kvw), ("v", kevac[og])]
                for ic in range(G):
                    P.op("tensor", lambda t, og=og, ic=ic, x=xi, pb=pb: t.matmul(
                        pb[:, 0:CKV],
                        kwT[:, ic * D + og * 128: ic * D + og * 128 + 128],
                        xdT[x][:, ic * CKV:(ic + 1) * CKV],
                        start=(ic == 0), stop=(ic == G - 1)),
                        waits=wme if ic == 0 else (),
                        inc=("t", 1) if ic == G - 1 else None)
                tmm = P.tok("t")
                we = [("t", tmm), (["dkta", "dktb"][og % 2], kst_tok[og % 2])]
                tv = P.op("vector", lambda v, og=og, pb=pb: v.tensor_scalar(
                    ktb[og % 2][:, :], pb[:, 0:CKV], kb_sb[:, og:og + 1],
                    None, ALU.add), inc=("v", 1), waits=we)
                kevac[og] = tv
                ks = ["dkta", "dktb"][og % 2]
                P.op("gpsimd", lambda e, og=og, c=c: e.dma_start(
                    out=kt_sc[og * 128:(og + 1) * 128, c * CKV:(c + 1) * CKV],
                    in_=ktb[og % 2][:, :]), inc=(ks, 16), waits=[("v", tv)])
                kst_tok[og % 2] = P.tok(ks)

            for jg in range(NJ):
                for half in range(2):
                    pb = ps[jg * 2 + half]
                    wme = [("v", kevac[jg * 2 + half])]
                    for ic in range(G):
                        P.op("tensor", lambda t, jg=jg, half=half, ic=ic, x=xi, pb=pb:
                             t.matmul(
                                 pb[:, 0:512],
                                 xdT[x][:, ic * CKV + jg * 128: ic * CKV + jg * 128 + 128],
                                 vwT[:, ic * D + half * 512: ic * D + half * 512 + 512],
                                 start=(ic == 0), stop=(ic == G - 1)),
                             waits=wme if ic == 0 else (),
                             inc=("t", 1) if ic == G - 1 else None)
                    tmm = P.tok("t")
                    if jg == NJ - 1 and half == 1:
                        kvmm_tok[xi] = tmm
                    kvg = c * NJ + jg
                    we = [("t", tmm)]
                    if half == 0:
                        we.append((["dvpa", "dvpb"][kvg % 2], vst_tok[kvg % 2]))
                    tv = P.op("vector", lambda v, kvg=kvg, half=half, pb=pb:
                              v.tensor_copy(
                                  bass.AP(vpb[kvg % 2], half * 8 * 65,
                                          [[vpb[0].shape[1], 128], [65, 8], [1, 64]]),
                                  bass.AP(pb.tensor, pb.offset,
                                          [[pb.tensor.shape[1], 128], [64, 8], [1, 64]])),
                              inc=("v", 1), waits=we)
                    kevac[jg * 2 + half] = tv
                    if half == 1:
                        vs = ["dvpa", "dvpb"][kvg % 2]
                        P.op("gpsimd", lambda e, kvg=kvg: e.dma_start(
                            out=vp_sc[kvg * 128:(kvg + 1) * 128, :],
                            in_=vpb[kvg % 2][:, :]), inc=(vs, 16),
                            waits=[("v", tv)])
                        vst_tok[kvg % 2] = P.tok(vs)

        for c in range(NCH):
            emit_loads_conv(c)
            if c >= 1:
                emit_kv(c - 1)
        emit_kv(NCH - 1)
        t_kt_done = [("dkta", P.tok("dkta")), ("dktb", P.tok("dktb"))]
        t_vp_done = [("dvpa", P.tok("dvpa")), ("dvpb", P.tok("dvpb"))]
        t_p3_t = P.tok("t")
        t_p3_v = P.tok("v")
        t_p3_g = P.tok("g")

    thr_toks = []
    tv2_prev = 0
    for wi in range(2):
        tv1 = P.op("vector", lambda v, wi=wi: v.tensor_scalar(
            den[0:1, 0:1], asum_p[wi][0:1, 0:1], 0.5 / (F * D), None, ALU.mult),
            inc=("v", 1), waits=[("v", alpha_done_tok[wi]), ("t", t_p3_t),
                                 ("v", t_obc), ("v", tv2_prev)])
        tm2 = P.op("tensor", lambda t: t.matmul(
            pH[:, 0:1], ones_row_f[:, :], den[0:1, 0:1], start=True, stop=True),
            inc=("t", 1), waits=[("v", tv1)])
        tcp = P.op("vector", lambda v, wi=wi: v.tensor_copy(thr_t[wi][:, :], pH[:, 0:1]),
             inc=("v", 1), waits=[("t", tm2)])
        tv2 = P.op("vector", lambda v, wi=wi: v.tensor_scalar(
            nthr_t[wi][:, :], thr_t[wi][:, :], -1.0, None, ALU.mult),
            inc=("v", 1), waits=[("v", tcp)])
        thr_toks.append(tv2)
        tv2_prev = tv2
    scopeW.close()

    P.fence()
    scopeB = ExitStack()
    attnT = sb("attnT", [128, G * NQ], BF16, scopeB)
    scopeQ = ExitStack()
    qTall = sb("qTall", [128, G * NQ], BF16, scopeQ)
    scope4 = ExitStack()
    qwT = sb("qwT", [128, G * D], BF16, scope4)
    xnT = sb("xnT", [128, G * QTS], BF16, scope4)
    P.wait("sync", "t", t_p3_t)
    P.wait("sync", "g", t_p3_g)
    for g in range(G):
        P.op("sync", lambda e, g=g: e.dma_start(
            out=qwT[:, g * D:(g + 1) * D],
            in_=wq_bf[0:D, g * 128:(g + 1) * 128], transpose=True),
            inc=("dwt", 16), waits=[("gcast", t_cast)])
    t_qw = P.tok("dwt")

    xq_sem = ["dq0", "dq1"]
    qp_done = [0, 0]
    qevac = [0, 0, 0, 0]
    for qt in range(NQT):
        q0 = qt * QTS
        wl = list(t_lnfm)
        if qt >= 1:
            wl.append(("t", qp_done[(qt - 1) % 2]))
        P.op("sync", lambda e, q0=q0: e.dma_start(
            out=bass.AP(xnT, 0, [[G * QTS, 128], [QTS, G], [1, QTS]]),
            in_=xn_own_fm_sc[0:D, q0:q0 + QTS].rearrange(
                "(g p) c -> p g c", p=128)),
            inc=(xq_sem[qt % 2], 16), waits=wl)
        t_xq = P.tok(xq_sem[qt % 2])
        for og in range(G):
            pb = ps[og % 4]
            wme = [(xq_sem[qt % 2], t_xq), ("dwt", t_qw), ("s", qevac[og % 4]),
                   ("v", t_p3_v)]
            for ic in range(G):
                P.op("tensor", lambda t, og=og, ic=ic, pb=pb: t.matmul(
                    pb[:, 0:QTS],
                    qwT[:, ic * D + og * 128: ic * D + og * 128 + 128],
                    xnT[:, ic * QTS:(ic + 1) * QTS],
                    start=(ic == 0), stop=(ic == G - 1)),
                    waits=wme if ic == 0 else (),
                    inc=("t", 1) if ic == G - 1 else None)
            tmm = P.tok("t")
            qevac[og % 4] = P.op("scalar", lambda s, og=og, q0=q0, pb=pb: s.activation(
                qT_all_slice(qTall, og, q0, QTS, NQ), pb[:, 0:QTS], AF.Identity,
                bias=qb_sb[:, og:og + 1], scale=1.0),
                inc=("s", 1), waits=[("t", tmm), ("s", t_p1_s)])
        qp_done[qt % 2] = P.tok("t")
    t_qproj_s = P.tok("s")
    t_qproj_t = P.tok("t")
    scope4.close()

    P.fence()
    scope5 = ExitStack()
    tmpn = sb("tmpn", [64, 1024], F32, scope5)
    PTW = [sb(f"PTW{i}", [128, 1024], BF16, scope5) for i in range(2)]
    rawb = sb("rawb", [65, 1024], F32, scope5)
    bcsb = sb("bcsb", [64, 1024], F32, scope5)
    ktg = [sb(f"ktg{i}", [128, KV], BF16, scope5) for i in range(2)]
    vg = [sb(f"vg{i}", [128, KVG * 130], BF16, scope5) for i in range(2)]
    twb = [sb(f"twb{i}", [128, WTC], F32, scope5) for i in range(2)]
    tb = [sb(f"tb{i}", [128, WTC], BF16, scope5) for i in range(2)]
    tbf = [sb(f"tbf{i}", [128, WTC], BF16, scope5) for i in range(2)]

    tern_ld_sem = ["dwb0", "dwb1"]
    tst_tok = [0, 0]
    tern_jobs = []
    for wi, (wsrc, wtsc) in enumerate(((w_up, wup_t_sc), (w_down, wdn_t_sc))):
        for it in range(len(wtiles(wsrc, WTC))):
            tern_jobs.append((wi, wsrc, wtsc, it))

    def emit_tern_job(j):
        wi, wsrc, wtsc, it = tern_jobs[j]
        bi = j % 2
        wl = [("v", thr_toks[1])]
        if j >= 2:
            wl.append((["dwo0", "dwo1"][bi], tst_tok[bi]))
        P.op("sync", lambda e, it=it, b=bi, w=wsrc:
             e.dma_start(out=bass.AP(twb[b], 0, wtiles(w, WTC)[it][1]),
                         in_=wtiles(w, WTC)[it][0]),
             inc=(tern_ld_sem[bi], 16), waits=wl)
        tld = P.tok(tern_ld_sem[bi])
        P.op("vector", lambda p, b=bi, wi=wi: p.tensor_scalar(
            tb[b][:, :], twb[b][:, :], thr_t[wi][:, 0:1], None, ALU.is_gt),
            waits=[(tern_ld_sem[bi], tld)])
        P.op("vector", lambda p, b=bi, wi=wi: p.tensor_scalar(
            tbf[b][:, :], twb[b][:, :], nthr_t[wi][:, 0:1], None, ALU.is_lt))
        tsb_ = P.op("vector", lambda p, b=bi: p.tensor_sub(
            tbf[b][:, :], tb[b][:, :], tbf[b][:, :]), inc=("v", 1))
        P.op("gpsimd", lambda e, it=it, b=bi, wt=wtsc:
             e.dma_start(out=wtiles(wt, WTC)[it][0],
                         in_=bass.AP(tbf[b], 0, wtiles(wt, WTC)[it][1])),
             inc=(["dwo0", "dwo1"][bi], 16), waits=[("v", tsb_)])
        tst_tok[bi] = P.tok(["dwo0", "dwo1"][bi])

    rt_jobs = []
    for g in range(G):
        for fc in range(F // WTC):
            rt_jobs.append((wup_t_sc, wupT_sc, fc * WTC, g))
    for fg in range(FG):
        rt_jobs.append((wdn_t_sc, wdnT_sc, 0, fg))
    rtb = [sb(f"rtb{i}", [128, WTC], BF16, scope5) for i in range(2)]
    rt_ld_sem = ["drb0", "drb1"]
    rt_st_tok = [0, 0]
    t_tern_all = [None]

    def emit_rt_job(j):
        src_sc, dst_sc, r0, g = rt_jobs[j]
        bi = j % 2
        if t_tern_all[0] is None:
            t_tern_all[0] = [("dwo0", P.tok("dwo0")), ("dwo1", P.tok("dwo1"))]
        wl = list(t_tern_all[0])
        if j >= 2:
            wl.append((["dro0", "dro1"][bi], rt_st_tok[bi]))
        P.op("sync", lambda e, b=bi, s=src_sc, r0=r0, g=g:
             e.dma_start(out=rtb[b][:, :],
                         in_=s[r0:r0 + WTC, g * 128:(g + 1) * 128],
                         transpose=True),
             inc=(rt_ld_sem[bi], 16), waits=wl)
        tld = P.tok(rt_ld_sem[bi])
        P.op("gpsimd", lambda e, b=bi, d=dst_sc, r0=r0, g=g:
             e.dma_start(out=d[g * 128:(g + 1) * 128, r0:r0 + WTC],
                         in_=rtb[b][:, :]),
             inc=(["dro0", "dro1"][bi], 16), waits=[(rt_ld_sem[bi], tld)])
        rt_st_tok[bi] = P.tok(["dro0", "dro1"][bi])

    n_tern = len(tern_jobs)
    n_bg = n_tern + len(rt_jobs)
    tern_per_gqt = (n_bg + G * NQT - 1) // (G * NQT)

    P.wait("sync", "t", t_qproj_t)
    kv_sem = ["dk0", "dk1"]
    exp_tok = [0, 0]
    av_tok = [0, 0]
    lastexp = 0
    lastnorm = 0
    kvload_tok = [0, 0]
    pending_norm = [None]
    pEF, pGH = pW[2], pW[3]
    P.rawcopy_tok = 0
    P.lastnorm_new = 0
    ti = 0
    for g in range(G):
        gb = g % 2
        wl = t_kt_done + t_vp_done
        if g >= 2:
            wl.append(("t", kvload_tok[gb]))
        P.op("sync", lambda e, g=g, b=gb: e.dma_start(
            out=ktg[b][:, :], in_=kt_sc[g * 128:(g + 1) * 128, :]),
            inc=(kv_sem[gb], 16), waits=wl)
        P.op("sync", lambda e, g=g, b=gb: e.dma_start(
            out=vg[b][:, :],
            in_=bass.AP(vp_sc, 2 * g * 65,
                        [[VROW, 128], [128 * VROW, KVG], [1, 130]])),
            inc=(kv_sem[gb], 16))
        t_kv = P.tok(kv_sem[gb])

        for qt in range(NQT):
            q0 = qt * QTS

            def emit_qk(kc, g=g, qt=qt, q0=q0, gb=gb, t_kv=t_kv):
                pq = pW[kc % 2]
                wme = [("s", t_qproj_s), (kv_sem[gb], t_kv)]
                if kc >= 2:
                    wme.append(("s", exp_tok[kc % 2]))
                else:
                    wme.append(("s", lastexp_l[0]))
                P.op("tensor", lambda t, kc=kc, pq=pq: t.matmul(
                    pq[:, 0:QTS], ktg[gb][0:64, kc * 128:(kc + 1) * 128],
                    qTall[0:64, g * NQ + q0: g * NQ + q0 + QTS],
                    start=True, stop=True),
                    waits=wme)
                P.op("tensor", lambda t, kc=kc, pq=pq: t.matmul(
                    pq[:, 512:512 + QTS], ktg[gb][64:128, kc * 128:(kc + 1) * 128],
                    qTall[64:128, g * NQ + q0: g * NQ + q0 + QTS],
                    start=True, stop=True),
                    inc=("t", 1))
                tqk_l[kc % 2] = P.tok("t")

            def emit_exp_av(kc, g=g, gb=gb):
                pq = pW[kc % 2]
                we = [("t", tqk_l[kc % 2])]
                if kc >= 2:
                    we.append(("t", av_tok[kc % 2]))
                if QTS == 512:
                    te = P.op("scalar", lambda s, kc=kc, pq=pq: s.activation(
                        PTW[kc % 2][:, :], pq[:, :], AF.Exp,
                        bias=0.0, scale=SCALE), inc=("s", 1), waits=we)
                else:
                    P.op("scalar", lambda s, kc=kc, pq=pq: s.activation(
                        PTW[kc % 2][:, 0:QTS], pq[:, 0:QTS], AF.Exp,
                        bias=0.0, scale=SCALE), waits=we)
                    te = P.op("scalar", lambda s, kc=kc, pq=pq: s.activation(
                        PTW[kc % 2][:, 512:512 + QTS], pq[:, 512:512 + QTS],
                        AF.Exp, bias=0.0, scale=SCALE), inc=("s", 1))
                exp_tok[kc % 2] = te
                wa = [("s", te)]
                if kc == 0:
                    wa.append(("v", P.rawcopy_tok))
                P.op("tensor", lambda t, kc=kc: t.matmul(
                    pE[0:65, 0:QTS],
                    vg[gb][:, kc * 130: kc * 130 + 65],
                    PTW[kc % 2][:, 0:QTS],
                    start=(kc == 0), stop=(kc == KVG - 1)),
                    waits=wa)
                av_tok[kc % 2] = P.op("tensor", lambda t, kc=kc: t.matmul(
                    pF[0:65, 0:QTS],
                    vg[gb][:, kc * 130 + 65: kc * 130 + 130],
                    PTW[kc % 2][:, 512:512 + QTS],
                    start=(kc == 0), stop=(kc == KVG - 1)),
                    inc=("t", 1))

            tqk_l = [0, 0]
            lastexp_l = [lastexp]
            it_target = (g * NQT + qt + 1) * tern_per_gqt
            for kc in range(KVG):
                emit_qk(kc)
                if kc == 1 and pending_norm[0] is not None:
                    pending_norm[0]()
                    pending_norm[0] = None
                if kc == 3:
                    while ti < min(n_bg, it_target):
                        if ti < n_tern:
                            emit_tern_job(ti)
                        else:
                            emit_rt_job(ti - n_tern)
                        ti += 1
                if kc >= 1:
                    emit_exp_av(kc - 1)
            emit_exp_av(KVG - 1)
            lastexp = P.tok("s")
            t_av = P.tok("t")
            if qt == NQT - 1:
                kvload_tok[gb] = t_av

            def norm_block(g=g, q0=q0, t_av=t_av):
                def dual(x, p0, np_):
                    tn = x.tensor if isinstance(x, bass.AP) else x
                    off = x.offset if isinstance(x, bass.AP) else 0
                    pitch = tn.shape[1]
                    return bass.AP(tn, off + p0 * pitch,
                                   [[pitch, np_], [512, 2], [1, QTS]])
                trw = P.op("vector", lambda v: v.tensor_copy(
                    dual(rawb, 0, 65), dual(pEF, 0, 65)), inc=("v", 1),
                    waits=[("t", t_av), ("v", P.lastnorm_new)])
                P.rawcopy_tok = trw
                trec = P.op("vector", lambda v: v.tensor_copy(
                    dual(recb, 0, 1), dual(rawb, 64, 1)),
                    inc=("v", 1), waits=[("v", trw)])
                tb0 = P.op("tensor", lambda t: t.matmul(
                    pG[0:64, 0:QTS], ones_row_b[0:1, 0:64], recb[0:1, 0:QTS],
                    start=True, stop=True), waits=[("v", trec)])
                tb1 = P.op("tensor", lambda t: t.matmul(
                    pH[0:64, 0:QTS], ones_row_b[0:1, 0:64],
                    recb[0:1, 512:512 + QTS],
                    start=True, stop=True), inc=("t", 1))
                tbc = P.tok("t")
                tcp = P.op("vector", lambda v: v.tensor_copy(
                    dual(bcsb, 0, 64), dual(pGH, 0, 64)), inc=("v", 1),
                    waits=[("t", tbc)])
                trc2 = P.op("vector", lambda v: v.reciprocal(
                    dual(tmpn, 0, 64), dual(bcsb, 0, 64)),
                    inc=("v", 1), waits=[("v", tcp)])
                tml = P.op("vector", lambda v: v.tensor_tensor(
                    dual(bcsb, 0, 64), dual(rawb, 0, 64), dual(tmpn, 0, 64),
                    ALU.mult),
                    inc=("v", 1), waits=[("v", trc2)])
                P.op("vector", lambda v: v.tensor_scalar(
                    attnT[0:64, g * NQ + q0: g * NQ + q0 + QTS],
                    bcsb[0:64, 0:QTS], vb_sb[0:64, g:g + 1], None, ALU.add),
                    waits=[("v", tml)])
                tnm = P.op("vector", lambda v: v.tensor_scalar(
                    attnT[64:128, g * NQ + q0: g * NQ + q0 + QTS],
                    bcsb[0:64, 512:512 + QTS], vb_sb[64:128, g:g + 1],
                    None, ALU.add), inc=("v", 1))
                P.lastnorm_new = tnm

            pending_norm[0] = norm_block
    if pending_norm[0] is not None:
        pending_norm[0]()
        pending_norm[0] = None
    while ti < n_bg:
        if ti < n_tern:
            emit_tern_job(ti)
        else:
            emit_rt_job(ti - n_tern)
        ti += 1
    t_attn_v = P.tok("v")
    t_attn_t = P.tok("t")
    t_attn_s = P.tok("s")
    t_tern_done = [("dro0", P.tok("dro0")), ("dro1", P.tok("dro1"))]
    scope5.close()

    scopeQ.close()
    P.fence()
    scope7 = ExitStack()
    wupT = sb("wupT", [128, G * F], BF16, scope7)
    owT = sb("owT", [128, G * D], BF16, scope7)
    xnT7 = [sb(f"xnT7{i}", [128, G * QTS], BF16, scope7) for i in range(2)]
    hT = sb("hT", [128, FG * QTS], BF16, scope7)
    h2s = [sb(f"h2s{i}", [128, QTS], BF16, scope7) for i in range(3)]
    wdb = [sb(f"wdb{i}", [128, D], BF16, scope7) for i in range(2)]
    r1row = sb("r1row", [1, NQ], BF16, scope7)
    bcf1a = sb("bcf1a", [128, NQ], BF16, scope7)

    P.wait("sync", "t", t_attn_t)
    for g in range(G):
        P.op("sync", lambda e, g=g: e.dma_start(
            out=wupT[:, g * F:(g + 1) * F],
            in_=wupT_sc[g * 128:(g + 1) * 128, 0:F]),
            inc=("dft", 16), waits=list(t_tern_done))
        P.op("sync", lambda e, g=g: e.dma_start(
            out=owT[:, g * D:(g + 1) * D],
            in_=wo_bf[0:D, g * 128:(g + 1) * 128], transpose=True),
            inc=("dft", 16), waits=[("gcast", t_cast)])
    t_fw = P.tok("dft")

    xq2_sem = ["df0", "df1"]
    wd_sem = ["dd0", "dd1"]
    xo_sem = ["do0", "do1"]
    gelu_tok = [0, 0, 0, 0]
    h2_tok = [0, 0, 0]
    fevac = [0, 0]
    fyst_tok = [0, 0]
    xnorm_tok = 0
    wdld_tok = [0, 0]
    prev_l2 = 0
    fev01 = [(0, 0)]

    P.op("gpsimd", lambda e: e.dma_start(
        out=r1row[0:1, :], in_=r1_sc[0:NQ, 0:1].rearrange("a c -> c a")),
        inc=("dr1r", 16), waits=[t_r1done])
    t_r1r = P.tok("dr1r")
    bc1_prev = 0
    for qt in range(NQT):
        tbm1 = P.op("tensor", lambda t, qt=qt: t.matmul(
            pH[:, 0:QTS], ones_row_b[0:1, :],
            r1row[0:1, qt * QTS:(qt + 1) * QTS], start=True, stop=True),
            inc=("t", 1), waits=[("dr1r", t_r1r), ("v", bc1_prev)])
        bc1_prev = P.op("vector", lambda v, qt=qt: v.tensor_copy(
            bcf1a[:, qt * QTS:(qt + 1) * QTS], pH[:, 0:QTS]),
            inc=("v", 1), waits=[("t", tbm1)])
    t_bc1a = bc1_prev

    upmm_done = [0, 0]
    for qt in range(NQT):
        q0 = qt * QTS
        x7 = xnT7[qt % 2]
        wl = list(t_lnfm)
        if qt >= 2:
            wl.append(("t", upmm_done[qt % 2]))
        P.op("sync", lambda e, q0=q0, x7=x7: e.dma_start(
            out=bass.AP(x7, 0, [[G * QTS, 128], [QTS, G], [1, QTS]]),
            in_=xn_own_fm_sc[0:D, q0:q0 + QTS].rearrange(
                "(g p) c -> p g c", p=128)),
            inc=(xq2_sem[qt % 2], 16), waits=wl)
        t_x7 = P.tok(xq2_sem[qt % 2])
        for g in range(G):
            wv = [(xq2_sem[qt % 2], t_x7)] if g == 0 else []
            if g == 0:
                wv.append(("v", t_bc1a))
                if qt >= 1:
                    wv.append(("v", xnorm_tok))
            P.op("vector", lambda v, g=g, q0=q0, x7=x7: v.tensor_mul(
                x7[:, g * QTS:(g + 1) * QTS], x7[:, g * QTS:(g + 1) * QTS],
                bcf1a[:, q0:q0 + QTS]),
                waits=wv, inc=("v", 1) if g == G - 1 else None)
        xnorm_tok = P.tok("v")

        h2q = []

        def emit_h2m(item, qt=qt):
            fc, tsq = item
            wm = [("v", tsq)]
            if fc == 0:
                wm.append(("s", prev_l2))
            h2_tok[fc % 3] = P.op("tensor", lambda t, fc=fc: t.matmul(
                pG[0:1, 0:QTS], ones_bf[:, :], h2s[fc % 3][:, :],
                start=(fc == 0), stop=(fc == FG - 1)),
                inc=("t", 1), waits=wm)

        for fc in range(FG):
            pb = ps[fc % 4]
            wme = [("v", xnorm_tok), ("dft", t_fw)]
            if fc >= 4 or qt > 0:
                wme.append(("s", gelu_tok[fc % 4]))
            if qt > 0 and fc < 4:
                wme += [("v", fev01[0][0]), ("v", fev01[0][1])]
            for dc in range(G):
                P.op("tensor", lambda t, fc=fc, dc=dc, pb=pb, x7=x7: t.matmul(
                    pb[:, 0:QTS],
                    wupT[:, dc * F + fc * 128: dc * F + fc * 128 + 128],
                    x7[:, dc * QTS:(dc + 1) * QTS],
                    start=(dc == 0), stop=(dc == G - 1)),
                    waits=wme if dc == 0 else (),
                    inc=("t", 1) if dc == G - 1 else None)
            tmm = P.tok("t")
            if fc == FG - 1:
                upmm_done[qt % 2] = tmm
            tg = P.op("scalar", lambda s, fc=fc, pb=pb: s.activation(
                hT[:, fc * QTS:(fc + 1) * QTS], pb[:, 0:QTS], AF.Gelu),
                inc=("s", 1), waits=[("t", tmm)])
            gelu_tok[fc % 4] = tg
            wv = [("s", tg)]
            if fc >= 3 or qt > 0:
                wv.append(("t", h2_tok[fc % 3]))
            P.op("vector", lambda v, fc=fc: v.tensor_mul(
                h2s[fc % 3][:, :], hT[:, fc * QTS:(fc + 1) * QTS],
                hT[:, fc * QTS:(fc + 1) * QTS]), inc=("v", 1), waits=wv)
            h2q.append((fc, P.tok("v")))
            if fc >= 2:
                emit_h2m(h2q.pop(0))
        while h2q:
            emit_h2m(h2q.pop(0))
        t_h = P.tok("s")
        t_s2 = P.tok("t")
        P.op("scalar", lambda s: s.activation(den[0:1, 0:QTS], pG[0:1, 0:QTS],
                                              AF.Ln, bias=0.0, scale=1.0),
             inc=("s", 1), waits=[("t", t_s2)])
        t_l2 = P.tok("s")
        prev_l2 = t_l2
        P.op("scalar", lambda s: s.activation(den[0:1, 512:512 + QTS],
                                              den[0:1, 0:QTS], AF.Exp,
                                              bias=0.0, scale=0.5),
             inc=("s", 1), waits=[("s", t_l2)])
        t_e2 = P.tok("s")
        tq1 = P.op("vector", lambda v: v.tensor_scalar(
            den[0:1, 0:QTS], den[0:1, 512:512 + QTS],
            1.0 / float(np.sqrt(F)), 1e-8, ALU.mult, ALU.add),
            inc=("v", 1), waits=[("s", t_e2)])
        tq2 = P.op("vector", lambda v: v.reciprocal(
            den[0:1, 512:512 + QTS], den[0:1, 0:QTS]),
            inc=("v", 1), waits=[("v", tq1)])
        P.op("vector", lambda v: v.tensor_copy(
            recb[0:1, 0:QTS], den[0:1, 512:512 + QTS]),
            inc=("v", 1), waits=[("v", tq2)])
        t_r2 = P.tok("v")
        tbm2 = P.op("tensor", lambda t: t.matmul(
            pH[:, 0:QTS], ones_row_b[0:1, :], recb[0:1, 0:QTS],
            start=True, stop=True), inc=("t", 1), waits=[("v", t_r2)])
        P.op("vector", lambda v: v.tensor_copy(bcf[:, 0:QTS], pH[:, 0:QTS]),
             inc=("v", 1), waits=[("t", tbm2)])
        t_bc2 = P.tok("v")
        wme = [("v", t_attn_v), ("dft", t_fw), ("s", t_h)]
        if qt > 0:
            wme += [("v", fevac[0]), ("v", fevac[1])]
        for qs in range(QSUB):
            wq = list(wme) if qs == 0 else []
            if qs == 3:
                wq.append(("v", t_bc2))
            for dc in range(G):
                P.op("tensor", lambda t, dc=dc, q0=q0, qs=qs: t.matmul(
                    ps[qs * 2][:, 0:512],
                    attnT[:, dc * NQ + q0 + qs * 128: dc * NQ + q0 + qs * 128 + 128],
                    owT[:, dc * D: dc * D + 512],
                    start=(dc == 0), stop=False),
                    waits=wq if dc == 0 else ())
                P.op("tensor", lambda t, dc=dc, q0=q0, qs=qs: t.matmul(
                    ps[qs * 2 + 1][:, 0:512],
                    attnT[:, dc * NQ + q0 + qs * 128: dc * NQ + q0 + qs * 128 + 128],
                    owT[:, dc * D + 512: dc * D + 1024],
                    start=(dc == 0), stop=False),
                    inc=("t", 1) if dc == G - 1 else None)
        for fc in range(FG):
            bi = fc % 2
            wl = list(t_tern_done)
            if fc >= 2 or qt > 0:
                wl.append(("t", wdld_tok[bi]))
            P.op("sync", lambda e, fc=fc, b=bi: e.dma_start(
                out=wdb[b][:, :],
                in_=wdnT_sc[fc * 128:(fc + 1) * 128, 0:D]),
                inc=(wd_sem[bi], 16), waits=wl)
            t_wd = P.tok(wd_sem[bi])
            ths = P.op("vector", lambda v, fc=fc: v.tensor_mul(
                hT[:, fc * QTS:(fc + 1) * QTS], hT[:, fc * QTS:(fc + 1) * QTS],
                bcf[:, 0:QTS]),
                inc=("v", 1), waits=[("v", t_bc2)])
            wme2 = [(wd_sem[bi], t_wd), ("v", ths)]
            for qs in range(QSUB):
                P.op("tensor", lambda t, fc=fc, qs=qs, b=bi: t.matmul(
                    ps[qs * 2][:, 0:512],
                    hT[:, fc * QTS + qs * 128: fc * QTS + qs * 128 + 128],
                    wdb[b][:, 0:512],
                    start=False, stop=(fc == FG - 1)),
                    waits=wme2 if qs == 0 else ())
                P.op("tensor", lambda t, fc=fc, qs=qs, b=bi: t.matmul(
                    ps[qs * 2 + 1][:, 0:512],
                    hT[:, fc * QTS + qs * 128: fc * QTS + qs * 128 + 128],
                    wdb[b][:, 512:1024],
                    start=False, stop=(fc == FG - 1)),
                    inc=("t", 1) if qs == QSUB - 1 else None)
            wdld_tok[bi] = P.tok("t")
        t_dn = P.tok("t")
        for qs in range(QSUB):
            bi = qs % 2
            P.op("sync", lambda e, q0=q0, qs=qs, b=bi: e.dma_start(
                out=ubuf[b][:, :],
                in_=x_own[q0 + qs * 128: q0 + qs * 128 + 128, :]),
                inc=(xo_sem[bi], 16),
                waits=[("v", fevac[bi])] if (qt > 0 or qs > 1) else [])
            t_yl = P.tok(xo_sem[bi])
            P.op("vector", lambda v, qs=qs, b=bi: v.tensor_add(
                ysb[b][:, 0:512], ps[qs * 2][:, 0:512], obc[:, 0:512]),
                waits=[("t", t_dn), (xo_sem[bi], t_yl), ("dy", fyst_tok[bi])])
            P.op("vector", lambda v, qs=qs, b=bi: v.tensor_add(
                ysb[b][:, 512:1024], ps[qs * 2 + 1][:, 0:512], obc[:, 512:1024]),
                inc=("v", 1))
            ta2 = P.tok("v")
            fevac[bi] = P.op("vector", lambda v, b=bi: v.tensor_add(
                ysb[b][:, :], ysb[b][:, :], ubuf[b][:, :]), inc=("v", 1),
                waits=[("v", ta2)])
            if qs == 1:
                fev01[0] = (fevac[0], fevac[1])
            P.op("gpsimd", lambda e, q0=q0, qs=qs, b=bi: e.dma_start(
                out=y[q0 + qs * 128: q0 + qs * 128 + 128, :], in_=ysb[b][:, :]),
                inc=("dy", 16), waits=[("v", fevac[bi])])
            fyst_tok[bi] = P.tok("dy")
    scope7.close()
    scopeB.close()

    P.wait("sync", "dy", P.tok("dy"))
    P.wait("sync", "gcast", P.tok("gcast"))

    with nc.Block() as blk:
        P.emit_all(blk)
    es.close()
    return nc


def wtiles(w, WTC):
    R, C = w.shape
    out = []
    if C >= WTC:
        assert C % WTC == 0 and R % 128 == 0
        for r in range(R // 128):
            for cs in range(C // WTC):
                out.append((w[r * 128:(r + 1) * 128, cs * WTC:(cs + 1) * WTC],
                            [[WTC, 128], [1, WTC]]))
    else:
        rpt = 128 * (WTC // C)
        assert R % rpt == 0
        na = rpt // 128
        for r in range(R // rpt):
            out.append((w[r * rpt:(r + 1) * rpt, :].rearrange(
                "(a p) c -> p a c", p=128),
                [[WTC, 128], [C, na], [1, C]]))
    return out


def qT_all_slice(qTall, og, q0, QTS, NQ):
    return qTall[:, og * NQ + q0: og * NQ + q0 + QTS]


def kernel(**inputs):
    cfg = full_cfg()
    S, NQ, D = cfg["S"], cfg["NQ"], cfg["D"]
    x = np.ascontiguousarray(np.asarray(inputs["L0_attn_x"], dtype=np.float32))
    B = x.shape[0]
    ncores = 8
    cpb = ncores // B

    f32c = lambda a: np.ascontiguousarray(np.asarray(a, dtype=np.float32))
    common = dict(
        q_w=f32c(inputs["L0_attn_q_w"]),
        q_b=f32c(inputs["L0_attn_q_b"]).reshape(D, 1),
        k_w=f32c(inputs["L0_attn_k_w"]),
        k_b=f32c(inputs["L0_attn_k_b"]).reshape(D, 1),
        v_w=f32c(inputs["L0_attn_v_w"]),
        v_b=f32c(inputs["L0_attn_v_b"]).reshape(D, 1),
        out_w=f32c(inputs["L0_attn_out_w"]),
        out_b=f32c(inputs["L0_attn_out_b"]).reshape(1, D),
        conv_w=f32c(inputs["L0_attn_conv_w"]).reshape(D, 3),
        conv_b=f32c(inputs["L0_attn_conv_b"]).reshape(D, 1),
        w_up=f32c(inputs["L0_ffn_w_up"]),
        w_down=f32c(inputs["L0_ffn_w_down"]),
        ident=np.eye(128, dtype=ml_dtypes.bfloat16),
    )
    in_maps = []
    for c in range(ncores):
        b, qc = divmod(c, cpb)
        m = dict(common)
        m["x_ctx"] = np.ascontiguousarray(x[b])
        m["x_own"] = np.ascontiguousarray(x[b, qc * NQ:(qc + 1) * NQ])
        in_maps.append(m)

    nc = build_nc(cfg)
    res = run_bass_kernel_spmd(nc, in_maps, core_ids=list(range(ncores)))
    out = np.empty((B, S, D), dtype=np.float32)
    for c in range(ncores):
        b, qc = divmod(c, cpb)
        out[b, qc * NQ:(qc + 1) * NQ] = res.results[c]["y"]
    return out


# version 47
# speedup vs baseline: 1.7185x; 1.0103x over previous
import numpy as np
import ml_dtypes
from contextlib import ExitStack

import concourse.bass as bass
import concourse.mybir as mybir
from concourse.bass_utils import run_bass_kernel_spmd

F32 = mybir.dt.float32
BF16 = mybir.dt.bfloat16
AF = mybir.ActivationFunctionType
ALU = mybir.AluOpType
AX = mybir.AxisListType

ENG_NAMES = ("sync", "gpsimd", "vector", "scalar", "tensor")


def full_cfg():
    return dict(S=8192, NQ=2048, D=1024, H=16, F=4096, CKV=512, QTS=512,
                WTC=1024)


class Prog:

    def __init__(self, nc, es):
        self.nc = nc
        self.es = es
        self.sems = {}
        self.cnt = {}
        self.stream = {e: [] for e in ENG_NAMES}
        self.lastw = {e: {} for e in ENG_NAMES}

    def sem(self, name):
        if name not in self.sems:
            self.sems[name] = self.es.enter_context(self.nc.semaphore(name))
            self.cnt[name] = 0
        return self.sems[name]

    def tok(self, name):
        self.sem(name)
        return self.cnt[name]

    def wait(self, engname, semname, val):
        if val is None or val <= 0:
            return
        s = self.sem(semname)
        lw = self.lastw[engname]
        if lw.get(semname, 0) >= val:
            return
        lw[semname] = val
        self.stream[engname].append(lambda eng, s=s, v=val: eng.wait_ge(s, v))

    def op(self, engname, build, inc=None, waits=()):
        for (sn, v) in waits:
            self.wait(engname, sn, v)
        incs = []
        tok = None
        if inc is not None:
            sn, n = inc
            self.sem(sn)
            self.cnt[sn] += n
            tok = self.cnt[sn]
            incs.append((self.sems[sn], n))

        def emit(eng, build=build, incs=incs):
            ins = build(eng)
            for (s, n) in incs:
                ins = ins.then_inc(s, n)

        self.stream[engname].append(emit)
        return tok

    def fence(self, exclude=("g", "gwa0", "gwa1")):
        for sn in list(self.cnt.keys()):
            if sn in exclude:
                continue
            v = self.cnt[sn]
            for e in ENG_NAMES:
                self.wait(e, sn, v)

    def emit_all(self, blk):
        for name in ENG_NAMES:
            fns = self.stream[name]

            def runner(eng, fns=fns):
                for fn in fns:
                    fn(eng)

            getattr(blk, name)(runner)


def build_nc(cfg):
    S, NQ, D, H, F = cfg["S"], cfg["NQ"], cfg["D"], cfg["H"], cfg["F"]
    CKV, QTS, WTC = cfg["CKV"], cfg["QTS"], cfg["WTC"]
    G = D // 128
    FG = F // 128
    KV = S // 2
    KVG = KV // 128
    NCH = KV // CKV
    NQT = NQ // QTS
    QSUB = QTS // 128
    DH = D // H
    assert DH == 64
    SCALE = 1.0 / float(np.sqrt(DH))
    XT = S // 128
    XOT = NQ // 128
    CS = 2 * CKV
    CW = CS + 128
    NJ = CKV // 128
    VROW = H * 65

    nc = bass.Bass("TRN2", target_bir_lowering=False)

    def din(name, shape):
        return nc.dram_tensor(name, shape, F32, kind="ExternalInput")

    x_own = din("x_own", [NQ, D])
    x_ctx = din("x_ctx", [S, D])
    q_w = din("q_w", [D, D]); q_b = din("q_b", [D, 1])
    k_w = din("k_w", [D, D]); k_b = din("k_b", [D, 1])
    v_w = din("v_w", [D, D]); v_b = din("v_b", [D, 1])
    out_w = din("out_w", [D, D]); out_b = din("out_b", [1, D])
    conv_w = din("conv_w", [D, 3]); conv_b = din("conv_b", [D, 1])
    w_up = din("w_up", [F, D])
    w_down = din("w_down", [D, F])
    y = nc.dram_tensor("y", [NQ, D], F32, kind="ExternalOutput")

    ident_in = nc.dram_tensor("ident", [128, 128], BF16, kind="ExternalInput")

    xn_fm_sc = nc.dram_tensor("xn_fm_sc", [D, S], BF16)
    xn_own_fm_sc = nc.dram_tensor("xn_own_fm_sc", [D, NQ], BF16)
    r1_sc = nc.dram_tensor("r1_sc", [NQ, 1], F32)
    wq_bf = nc.dram_tensor("wq_bf", [D, D], BF16)
    wk_bf = nc.dram_tensor("wk_bf", [D, D], BF16)
    wv_bf = nc.dram_tensor("wv_bf", [D, D], BF16)
    wo_bf = nc.dram_tensor("wo_bf", [D, D], BF16)
    kt_sc = nc.dram_tensor("kt_sc", [D, KV], BF16)
    vp_sc = nc.dram_tensor("vp_sc", [KV, VROW], BF16)
    wup_t_sc = nc.dram_tensor("wup_t_sc", [F, D], BF16)
    wdn_t_sc = nc.dram_tensor("wdn_t_sc", [D, F], BF16)
    wupT_sc = nc.dram_tensor("wupT_sc", [D, F], BF16)
    wdnT_sc = nc.dram_tensor("wdnT_sc", [F, D], BF16)

    es = ExitStack()
    P = Prog(nc, es)

    def sb(name, shape, dt, scope=es):
        return scope.enter_context(nc.sbuf_tensor(name, shape, dt))

    qb_sb = sb("qb_sb", [128, G], F32)
    kb_sb = sb("kb_sb", [128, G], F32)
    vb_sb = sb("vb_sb", [128, G], F32)
    cb_sb = sb("cb_sb", [128, G], F32)
    cw_sb = sb("cw_sb", [128, 3 * G], F32)
    ones_bf = sb("ones_bf", [128, 1], BF16)
    ones_f = sb("ones_f", [128, 1], F32)
    ones_row_f = sb("ones_row_f", [1, 128], F32)
    ones_row_b = sb("ones_row_b", [1, 128], BF16)
    ob_row = sb("ob_row", [1, D], F32)
    obc = sb("obc", [128, D], F32)
    den = sb("den", [1, 1024], F32)
    rec = sb("rec", [1, 1024], F32)
    recb = sb("recb", [1, 1024], BF16)
    bcf = sb("bcf", [128, QTS], F32)
    ubuf = [sb(f"ubuf{i}", [128, D], F32) for i in range(2)]
    ysb = [sb(f"ysb{i}", [128, D], F32) for i in range(2)]
    r2t = sb("r2t", [128, 2 * QSUB], F32)
    thr_t = [sb(f"thr{i}", [128, 1], F32) for i in range(2)]
    nthr_t = [sb(f"nthr{i}", [128, 1], F32) for i in range(2)]
    WTCA = 2048
    asum_p = [sb(f"asum_p{i}", [128, 1], F32) for i in range(2)]
    rsum = [sb(f"rsum{i}", [128, 1], F32) for i in range(2)]
    epsln = sb("epsln", [128, 1], F32)
    ident_sb = sb("ident_sb", [128, 128], BF16)
    scopeW = ExitStack()
    wbuf = [sb(f"wbuf{i}", [128, WTCA], F32, scopeW) for i in range(2)]
    absb2 = [sb(f"absb2{i}", [128, WTCA], BF16, scopeW) for i in range(3)]

    pW3 = es.enter_context(nc.psum_tensor("pw3", [128, 1024], F32))
    pG, pH = pW3[:, 0:512], pW3[:, 512:1024]
    pAL = [pG, pH]
    scopeT = ExitStack()
    pT = [scopeT.enter_context(nc.psum_tensor(f"pt{i}", [128, 1024], BF16))
          for i in range(2)]

    gcast_jobs = [(q_w, wq_bf), (k_w, wk_bf), (v_w, wv_bf), (out_w, wo_bf)]

    def emit_gcast(j):
        s_, d_ = gcast_jobs[j]
        P.op("gpsimd", lambda g, s=s_, d=d_:
             g.dma_start(out=d[:, :], in_=s[:, :]), inc=("gcast", 16))
    for (vec, tile) in ((q_b, qb_sb), (k_b, kb_sb), (v_b, vb_sb), (conv_b, cb_sb)):
        for g in range(G):
            P.op("sync", lambda e, v=vec, t=tile, g=g:
                 e.dma_start(out=t[:, g:g + 1], in_=v[g * 128:(g + 1) * 128, 0:1]),
                 inc=("dsm", 16))
    for g in range(G):
        P.op("sync", lambda e, g=g: e.dma_start(
            out=cw_sb[:, g * 3:(g + 1) * 3], in_=conv_w[g * 128:(g + 1) * 128, :]),
            inc=("dsm", 16))
    P.op("sync", lambda e: e.dma_start(out=ob_row[:, :], in_=out_b[0:1, :]),
         inc=("dsm", 16))
    P.op("sync", lambda e: e.dma_start(out=ident_sb[:, :], in_=ident_in[:, :]),
         inc=("dsm", 16))
    t_dsm = P.tok("dsm")

    P.op("vector", lambda v: v.memset(ones_bf[:, :], 1.0))
    P.op("vector", lambda v: v.memset(ones_f[:, :], 1.0))
    P.op("vector", lambda v: v.memset(ones_row_f[:, :], 1.0))
    P.op("vector", lambda v: v.memset(ones_row_b[:, :], 1.0))
    P.op("vector", lambda v: v.memset(epsln[:, :], 1e-5), inc=("v", 1))
    t_const = P.tok("v")

    t1 = P.op("tensor", lambda t: t.matmul(pG[:, 0:512], ones_row_f[:, :],
                                           ob_row[0:1, 0:512], start=True, stop=True),
              inc=("t", 1), waits=[("dsm", t_dsm), ("v", t_const)])
    t2 = P.op("tensor", lambda t: t.matmul(pH[:, 0:512], ones_row_f[:, :],
                                           ob_row[0:1, 512:1024], start=True, stop=True),
              inc=("t", 1))
    P.op("vector", lambda v: v.tensor_copy(obc[:, 0:512], pG[:, 0:512]),
         waits=[("t", t1)])
    t_obc = P.op("vector", lambda v: v.tensor_copy(obc[:, 512:1024], pH[:, 0:512]),
                 inc=("v", 1), waits=[("t", t2)])

    ga_sem = ["gwa0", "gwa1"]
    abs_tok = [0, 0]
    pe_alpha_tok = [0, 0, 0]
    alpha_jobs = []
    for wi, wsrc in enumerate((w_up, w_down)):
        nt = len(wtiles(wsrc, WTCA))
        for it in range(nt):
            alpha_jobs.append((wi, wsrc, it, it == 0, it == nt - 1))
    n_alpha = len(alpha_jobs)
    alpha_done_tok = []

    aq = []

    def emit_alpha_mm(item):
        j, ta = item
        wi, wsrc, it, first, last = alpha_jobs[j]
        ai = j % 3
        nsub = WTCA // 512
        for k in range(nsub):
            wm = [("s", ta)] if k == 0 else []
            if first and k == 0:
                wm.append(("v", t_obc))
            P.op("tensor", lambda t, a=ai, k=k, wi=wi: t.matmul(
                pAL[wi][0:1, 0:512], ones_bf[:, :],
                absb2[a][:, k * 512:(k + 1) * 512],
                start=(first and k == 0), stop=(last and k == nsub - 1)),
                waits=wm, inc=("t", 1) if k == nsub - 1 else None)
        pe_alpha_tok[ai] = P.tok("t")
        if last:
            P.op("vector", lambda v, wi=wi: v.reduce_sum(
                asum_p[wi][0:1, 0:1], pAL[wi][0:1, 0:512], axis=AX.X),
                inc=("v", 1), waits=[("t", pe_alpha_tok[ai])])
            alpha_done_tok.append(P.tok("v"))

    def emit_alpha_job(j):
        wi, wsrc, it, first, last = alpha_jobs[j]
        bi = j % 2
        ai = j % 3
        P.op("sync", lambda g, it=it, b=bi, w=wsrc:
             g.dma_start(out=bass.AP(wbuf[b], 0, wtiles(w, WTCA)[it][1]),
                         in_=wtiles(w, WTCA)[it][0]),
             inc=(ga_sem[bi], 16), waits=[("s", abs_tok[bi])])
        tld = P.tok(ga_sem[bi])
        abs_tok[bi] = P.op("scalar", lambda s, b=bi, a=ai: s.activation(
            absb2[a][:, :], wbuf[b][:, :], AF.Abs),
            inc=("s", 1), waits=[(ga_sem[bi], tld), ("t", pe_alpha_tok[ai])])
        aq.append((j, abs_tok[bi]))
        if len(aq) > 2:
            emit_alpha_mm(aq.pop(0))

    NB = 4
    with ExitStack() as scope1:
        xbuf4 = [sb(f"xbuf4{i}", [128, NB * D], F32, scope1) for i in range(3)]
        nbuf4 = [sb(f"nbuf4{i}", [128, NB * D], BF16, scope1) for i in range(3)]
        fmb = [sb(f"fmb{i}", [128, G * NB * 128], BF16, scope1) for i in range(2)]
        stat2 = [sb(f"stat{i}", [128, 12], F32, scope1) for i in range(3)]
        mv2 = [sb(f"mv{i}", [128, 2 * NB], F32, scope1) for i in range(3)]
        lnt2 = [sb(f"lnt{i}", [128, 3 * NB], F32, scope1) for i in range(3)]
        sqt2 = [sb(f"sqt{i}", [128, 3 * NB], F32, scope1) for i in range(3)]

        ln_jobs = [(x_ctx, xn_fm_sc, j * NB * 128) for j in range(XT // NB)] + \
                  [(x_own, xn_own_fm_sc, j * NB * 128) for j in range(XOT // NB)]
        aj = [0]
        norm_tok = [0, 0, 0]
        aggr2 = [0, 0, 0]
        tp_tok = [0, 0, 0]
        ev_tok = [0, 0]
        fst_tok = [0, 0]
        r1st_tok = [0, 0, 0]
        nmr_tok = 0
        xl_sem = ["dl0", "dl1", "dl2"]
        fm_sems = ["dfm0", "dfm1"]
        for i, (src_, dst, r0) in enumerate(ln_jobs):
            bi = i % 3
            fi = i % 2
            stat, mv, lnt = stat2[bi], mv2[bi], lnt2[bi]
            aggr_tok = aggr2[bi]
            wl = []
            if i >= 3:
                wl += [("s", norm_tok[bi])]
            P.op("sync", lambda e, s=src_, r0=r0, b=bi: e.dma_start(
                out=xbuf4[b][:, 0:NB * D],
                in_=s[r0:r0 + NB * 128, :].rearrange("(a p) c -> p a c", p=128)),
                inc=(xl_sem[bi], 16), waits=wl)
            tld = P.tok(xl_sem[bi])
            for sub in range(NB):
                P.op("vector", lambda v, b=bi, sub=sub, stat=stat: v.bn_stats(
                    stat[:, 0:6], xbuf4[b][:, sub * D: sub * D + 512]),
                    waits=[(xl_sem[bi], tld), ("v", aggr_tok),
                           ("s", norm_tok[bi])])
                P.op("vector", lambda v, b=bi, sub=sub, stat=stat: v.bn_stats(
                    stat[:, 6:12], xbuf4[b][:, sub * D + 512: sub * D + 1024]),
                    inc=("v", 1))
                ts_ = P.tok("v")
                aggr_tok = P.op("vector", lambda v, sub=sub, stat=stat, mv=mv:
                                v.bn_aggr(mv[:, sub * 2:sub * 2 + 2], stat[:, :]),
                                inc=("v", 1), waits=[("v", ts_)])
            aggr2[bi] = aggr_tok
            P.op("scalar", lambda s, mv=mv, lnt=lnt: s.activation(
                lnt[:, 0:NB], mv[:, 1:2 * NB:2], AF.Ln,
                bias=epsln[:, 0:1], scale=1.0),
                inc=("s", 1), waits=[("v", aggr_tok), ("v", t_const)])
            tl1 = P.tok("s")
            P.op("scalar", lambda s, lnt=lnt: s.activation(
                lnt[:, NB:2 * NB], lnt[:, 0:NB], AF.Exp, bias=0.0, scale=-0.5),
                inc=("s", 1), waits=[("s", tl1)])
            tl2 = P.tok("s")
            nmr_tok = P.op("vector", lambda v, mv=mv, lnt=lnt:
                           v.scalar_tensor_tensor(
                               lnt[:, 2 * NB:3 * NB], mv[:, 0:2 * NB:2], -1.0,
                               lnt[:, NB:2 * NB], ALU.mult, ALU.mult),
                           inc=("v", 1), waits=[("s", tl2)])
            if dst is xn_own_fm_sc:
                sq = sqt2[bi]
                P.op("scalar", lambda s, mv=mv, sq=sq: s.activation(
                    sq[:, 0:NB], mv[:, 1:2 * NB:2], AF.Sqrt),
                    inc=("s", 1), waits=[("s", tl2), ("dr1", r1st_tok[bi])])
                tsq = P.tok("s")
                P.op("vector", lambda v, lnt=lnt, sq=sq: v.tensor_mul(
                    sq[:, NB:2 * NB], sq[:, 0:NB], lnt[:, NB:2 * NB]),
                    waits=[("s", tsq)])
                P.op("vector", lambda v, sq=sq: v.tensor_scalar(
                    sq[:, NB:2 * NB], sq[:, NB:2 * NB], 1e-8, None, ALU.add))
                P.op("vector", lambda v, sq=sq: v.reciprocal(
                    sq[:, 2 * NB:3 * NB], sq[:, NB:2 * NB]), inc=("v", 1))
                tr1 = P.tok("v")
                P.op("gpsimd", lambda e, r0=r0, sq=sq: e.dma_start(
                    out=r1_sc[r0:r0 + NB * 128, 0:1].rearrange(
                        "(a p) c -> p a c", p=128),
                    in_=sq[:, 2 * NB:3 * NB],
                    allow_slow_non_contiguous=True),
                    inc=("dr1", 16), waits=[("v", tr1)])
                r1st_tok[bi] = P.tok("dr1")
            for sub in range(NB):
                P.op("scalar", lambda s, b=bi, sub=sub, lnt=lnt:
                     s.activation(
                         nbuf4[b][:, sub * D:(sub + 1) * D],
                         xbuf4[b][:, sub * D:(sub + 1) * D],
                         AF.Identity,
                         bias=lnt[:, 2 * NB + sub:2 * NB + sub + 1],
                         scale=lnt[:, NB + sub:NB + sub + 1]),
                     waits=[("v", nmr_tok), ("t", tp_tok[bi])],
                     inc=("s", 1) if sub == NB - 1 else None)
            norm_tok[bi] = P.tok("s")
            for sub in range(NB):
                tb = (i * NB + sub) % 2
                for g in range(G):
                    wt = []
                    if g == 0:
                        wt = [("s", norm_tok[bi]), ("v", ev_tok[tb]),
                              ("dsm", t_dsm)]
                    P.op("tensor", lambda t, b=bi, sub=sub, g=g, tb=tb:
                         t.transpose(
                             pT[tb][:, g * 128:(g + 1) * 128],
                             nbuf4[b][:, sub * D + g * 128: sub * D + g * 128 + 128],
                             ident_sb[:, :]),
                         waits=wt, inc=("t", 1) if g == G - 1 else None)
                ttp = P.tok("t")
                we = [("t", ttp)]
                if sub == 0:
                    we.append((fm_sems[fi], fst_tok[fi]))
                ev_tok[tb] = P.op("vector", lambda v, b=fi, sub=sub, tb=tb:
                                  v.tensor_copy(
                                      bass.AP(fmb[b], sub * 128,
                                              [[G * NB * 128, 128],
                                               [NB * 128, G], [1, 128]]),
                                      pT[tb][:, 0:G * 128]),
                                  inc=("v", 1), waits=we)
            tp_tok[bi] = P.tok("t")
            ev_all = P.tok("v")
            P.op("gpsimd", lambda e, d=dst, r0=r0, b=fi: e.dma_start(
                out=d[0:D, r0:r0 + NB * 128].rearrange("(g p) c -> p g c", p=128),
                in_=fmb[b][:, :]),
                inc=(fm_sems[fi], 16), waits=[("v", ev_all)])
            fst_tok[fi] = P.tok(fm_sems[fi])
            while aj[0] < min(n_alpha, (i + 1) * 2):
                emit_alpha_job(aj[0])
                aj[0] += 1
            if 4 <= i < 8:
                emit_gcast(i - 4)
        while aj[0] < n_alpha:
            emit_alpha_job(aj[0])
            aj[0] += 1
        while aq:
            emit_alpha_mm(aq.pop(0))
        t_lnfm = [("dfm0", P.tok("dfm0")), ("dfm1", P.tok("dfm1"))]
        t_r1done = ("dr1", P.tok("dr1"))
        t_cast = P.tok("gcast")
    t_p1_s = P.tok("s")
    t_p1_v = P.tok("v")
    scopeT.close()
    pW = [es.enter_context(nc.psum_tensor(f"pw{i}", [128, 1024], F32))
          for i in range(3)] + [pW3]
    ps = [pW[i // 2][:, (i % 2) * 512:(i % 2) * 512 + 512] for i in range(8)]
    pE, pF = ps[4], ps[5]

    P.fence()
    with ExitStack() as scope3:
        kwT = sb("kwT", [128, G * D], BF16, scope3)
        vwT = sb("vwT", [128, G * D], BF16, scope3)
        xnc = [sb(f"xnc{i}", [128, G * CW], BF16, scope3) for i in range(2)]
        xdT = [sb(f"xdT{i}", [128, G * CKV], BF16, scope3) for i in range(2)]
        cacc = [sb(f"cacc{i}", [128, CKV], F32, scope3) for i in range(2)]
        ktb = [sb(f"ktb{i}", [128, CKV], BF16, scope3) for i in range(2)]
        vpb = [sb(f"vpb{i}", [128, VROW], BF16, scope3) for i in range(2)]

        P.wait("sync", "gcast", t_cast)
        for (wsc, tile) in ((wk_bf, kwT), (wv_bf, vwT)):
            for g in range(G):
                P.op("sync", lambda e, w=wsc, t=tile, g=g: e.dma_start(
                    out=t[:, g * D:(g + 1) * D],
                    in_=w[0:D, g * 128:(g + 1) * 128], transpose=True),
                    inc=("dwt", 16))
        t_kvw = P.tok("dwt")

        P.op("vector", lambda v: v.memset(
            bass.AP(xnc[0], 0, [[xnc[0].shape[1], 128], [CW, G], [1, 128]]), 0.0))
        P.op("vector", lambda v: v.memset(
            bass.AP(vpb[0], 64, [[vpb[0].shape[1], 128], [65, H], [1, 1]]), 1.0))
        t_zp = P.op("vector", lambda v: v.memset(
            bass.AP(vpb[1], 64, [[vpb[1].shape[1], 128], [65, H], [1, 1]]), 1.0),
            inc=("v", 1))

        conv_tok = [0, 0]
        kvmm_tok = [0, 0]
        kevac = [0] * 8
        kst_tok = [0, 0]
        vst_tok = [0, 0]
        tr_sem = ["dc0", "dc1"]
        def emit_loads_conv(c):
            bi = c % 2
            lo = c * CS - 128
            wl = list(t_lnfm)
            if c == 0:
                wl.append(("v", t_zp))
            if c >= 2:
                wl.append(("v", conv_tok[bi]))
            if c == 0:
                P.op("sync", lambda e: e.dma_start(
                    out=bass.AP(xnc[0], 128,
                                [[G * CW, 128], [CW, G], [1, CS]]),
                    in_=xn_fm_sc[0:D, 0:CS].rearrange(
                        "(g p) c -> p g c", p=128)),
                    inc=(tr_sem[bi], 16), waits=wl)
            else:
                P.op("sync", lambda e, lo=lo, b=bi: e.dma_start(
                    out=bass.AP(xnc[b], 0,
                                [[G * CW, 128], [CW, G], [1, CW]]),
                    in_=xn_fm_sc[0:D, lo:lo + CW].rearrange(
                        "(g p) c -> p g c", p=128)),
                    inc=(tr_sem[bi], 16), waits=wl)
            t_tr = P.tok(tr_sem[bi])
            wc = [(tr_sem[bi], t_tr), ("dsm", t_dsm)]
            if c >= 2:
                wc.append(("t", kvmm_tok[bi]))
            prev_g = P.tok("v")
            for g in range(G):
                def xg(off, g=g, b=bi):
                    return bass.AP(xnc[b], g * CW + off,
                                   [[xnc[b].shape[1], 128], [2, CKV]])
                prev_g = P.op("vector", lambda p, g=g, b=bi, xg=xg: p.tensor_scalar(
                    cacc[b][:, :], xg(127), cw_sb[:, g * 3:g * 3 + 1],
                    cb_sb[:, g:g + 1], ALU.mult, ALU.add),
                    inc=("v", 1),
                    waits=(wc if g == 0 else []) + [("v", prev_g)])
                prev_g = P.op("vector", lambda p, g=g, b=bi, xg=xg: p.scalar_tensor_tensor(
                    cacc[1 - b][:, :], xg(128), cw_sb[:, g * 3 + 1:g * 3 + 2],
                    cacc[b][:, :], ALU.mult, ALU.add),
                    inc=("v", 1), waits=[("v", prev_g)])
                prev_g = P.op("vector", lambda p, g=g, b=bi, xg=xg: p.scalar_tensor_tensor(
                    xdT[b][:, g * CKV:(g + 1) * CKV], xg(129),
                    cw_sb[:, g * 3 + 2:g * 3 + 3], cacc[1 - b][:, :],
                    ALU.mult, ALU.add),
                    inc=("v", 1), waits=[("v", prev_g)])
            conv_tok[bi] = P.tok("v")

        def emit_kv(c):
            bi = c % 2
            for og in range(G):
                pb = ps[og]
                wme = [("v", conv_tok[bi]), ("dwt", t_kvw), ("v", kevac[og])]
                for ic in range(G):
                    P.op("tensor", lambda t, og=og, ic=ic, b=bi, pb=pb: t.matmul(
                        pb[:, 0:CKV],
                        kwT[:, ic * D + og * 128: ic * D + og * 128 + 128],
                        xdT[b][:, ic * CKV:(ic + 1) * CKV],
                        start=(ic == 0), stop=(ic == G - 1)),
                        waits=wme if ic == 0 else (),
                        inc=("t", 1) if ic == G - 1 else None)
                tmm = P.tok("t")
                we = [("t", tmm), (["dkta", "dktb"][og % 2], kst_tok[og % 2])]
                tv = P.op("vector", lambda v, og=og, pb=pb: v.tensor_scalar(
                    ktb[og % 2][:, :], pb[:, 0:CKV], kb_sb[:, og:og + 1],
                    None, ALU.add), inc=("v", 1), waits=we)
                kevac[og] = tv
                ks = ["dkta", "dktb"][og % 2]
                P.op("gpsimd", lambda e, og=og, c=c: e.dma_start(
                    out=kt_sc[og * 128:(og + 1) * 128, c * CKV:(c + 1) * CKV],
                    in_=ktb[og % 2][:, :]), inc=(ks, 16), waits=[("v", tv)])
                kst_tok[og % 2] = P.tok(ks)

            for jg in range(NJ):
                for half in range(2):
                    pb = ps[jg * 2 + half]
                    wme = [("v", kevac[jg * 2 + half])]
                    for ic in range(G):
                        P.op("tensor", lambda t, jg=jg, half=half, ic=ic, b=bi, pb=pb:
                             t.matmul(
                                 pb[:, 0:512],
                                 xdT[b][:, ic * CKV + jg * 128: ic * CKV + jg * 128 + 128],
                                 vwT[:, ic * D + half * 512: ic * D + half * 512 + 512],
                                 start=(ic == 0), stop=(ic == G - 1)),
                             waits=wme if ic == 0 else (),
                             inc=("t", 1) if ic == G - 1 else None)
                    tmm = P.tok("t")
                    if jg == NJ - 1 and half == 1:
                        kvmm_tok[bi] = tmm
                    kvg = c * NJ + jg
                    we = [("t", tmm)]
                    if half == 0:
                        we.append((["dvpa", "dvpb"][kvg % 2], vst_tok[kvg % 2]))
                    tv = P.op("vector", lambda v, kvg=kvg, half=half, pb=pb:
                              v.tensor_copy(
                                  bass.AP(vpb[kvg % 2], half * 8 * 65,
                                          [[vpb[0].shape[1], 128], [65, 8], [1, 64]]),
                                  bass.AP(pb.tensor, pb.offset,
                                          [[pb.tensor.shape[1], 128], [64, 8], [1, 64]])),
                              inc=("v", 1), waits=we)
                    kevac[jg * 2 + half] = tv
                    if half == 1:
                        vs = ["dvpa", "dvpb"][kvg % 2]
                        P.op("gpsimd", lambda e, kvg=kvg: e.dma_start(
                            out=vp_sc[kvg * 128:(kvg + 1) * 128, :],
                            in_=vpb[kvg % 2][:, :]), inc=(vs, 16),
                            waits=[("v", tv)])
                        vst_tok[kvg % 2] = P.tok(vs)

        for c in range(NCH):
            emit_loads_conv(c)
            if c >= 1:
                emit_kv(c - 1)
        emit_kv(NCH - 1)
        t_kt_done = [("dkta", P.tok("dkta")), ("dktb", P.tok("dktb"))]
        t_vp_done = [("dvpa", P.tok("dvpa")), ("dvpb", P.tok("dvpb"))]
        t_p3_t = P.tok("t")
        t_p3_v = P.tok("v")
        t_p3_g = P.tok("g")

    thr_toks = []
    tv2_prev = 0
    for wi in range(2):
        tv1 = P.op("vector", lambda v, wi=wi: v.tensor_scalar(
            den[0:1, 0:1], asum_p[wi][0:1, 0:1], 0.5 / (F * D), None, ALU.mult),
            inc=("v", 1), waits=[("v", alpha_done_tok[wi]), ("t", t_p3_t),
                                 ("v", t_obc), ("v", tv2_prev)])
        tm2 = P.op("tensor", lambda t: t.matmul(
            pH[:, 0:1], ones_row_f[:, :], den[0:1, 0:1], start=True, stop=True),
            inc=("t", 1), waits=[("v", tv1)])
        tcp = P.op("vector", lambda v, wi=wi: v.tensor_copy(thr_t[wi][:, :], pH[:, 0:1]),
             inc=("v", 1), waits=[("t", tm2)])
        tv2 = P.op("vector", lambda v, wi=wi: v.tensor_scalar(
            nthr_t[wi][:, :], thr_t[wi][:, :], -1.0, None, ALU.mult),
            inc=("v", 1), waits=[("v", tcp)])
        thr_toks.append(tv2)
        tv2_prev = tv2
    scopeW.close()

    P.fence()
    scopeB = ExitStack()
    attnT = sb("attnT", [128, G * NQ], BF16, scopeB)
    scopeQ = ExitStack()
    qTall = sb("qTall", [128, G * NQ], BF16, scopeQ)
    scope4 = ExitStack()
    qwT = sb("qwT", [128, G * D], BF16, scope4)
    xnT = sb("xnT", [128, G * QTS], BF16, scope4)
    P.wait("sync", "t", t_p3_t)
    P.wait("sync", "g", t_p3_g)
    for g in range(G):
        P.op("sync", lambda e, g=g: e.dma_start(
            out=qwT[:, g * D:(g + 1) * D],
            in_=wq_bf[0:D, g * 128:(g + 1) * 128], transpose=True),
            inc=("dwt", 16), waits=[("gcast", t_cast)])
    t_qw = P.tok("dwt")

    xq_sem = ["dq0", "dq1"]
    qp_done = [0, 0]
    qevac = [0, 0, 0, 0]
    for qt in range(NQT):
        q0 = qt * QTS
        wl = list(t_lnfm)
        if qt >= 1:
            wl.append(("t", qp_done[(qt - 1) % 2]))
        P.op("sync", lambda e, q0=q0: e.dma_start(
            out=bass.AP(xnT, 0, [[G * QTS, 128], [QTS, G], [1, QTS]]),
            in_=xn_own_fm_sc[0:D, q0:q0 + QTS].rearrange(
                "(g p) c -> p g c", p=128)),
            inc=(xq_sem[qt % 2], 16), waits=wl)
        t_xq = P.tok(xq_sem[qt % 2])
        for og in range(G):
            pb = ps[og % 4]
            wme = [(xq_sem[qt % 2], t_xq), ("dwt", t_qw), ("s", qevac[og % 4]),
                   ("v", t_p3_v)]
            for ic in range(G):
                P.op("tensor", lambda t, og=og, ic=ic, pb=pb: t.matmul(
                    pb[:, 0:QTS],
                    qwT[:, ic * D + og * 128: ic * D + og * 128 + 128],
                    xnT[:, ic * QTS:(ic + 1) * QTS],
                    start=(ic == 0), stop=(ic == G - 1)),
                    waits=wme if ic == 0 else (),
                    inc=("t", 1) if ic == G - 1 else None)
            tmm = P.tok("t")
            qevac[og % 4] = P.op("scalar", lambda s, og=og, q0=q0, pb=pb: s.activation(
                qT_all_slice(qTall, og, q0, QTS, NQ), pb[:, 0:QTS], AF.Identity,
                bias=qb_sb[:, og:og + 1], scale=1.0),
                inc=("s", 1), waits=[("t", tmm), ("s", t_p1_s)])
        qp_done[qt % 2] = P.tok("t")
    t_qproj_s = P.tok("s")
    t_qproj_t = P.tok("t")
    scope4.close()

    P.fence()
    scope5 = ExitStack()
    tmpn = sb("tmpn", [64, 1024], F32, scope5)
    PTW = [sb(f"PTW{i}", [128, 1024], BF16, scope5) for i in range(2)]
    rawb = sb("rawb", [65, 1024], F32, scope5)
    bcsb = sb("bcsb", [64, 1024], F32, scope5)
    ktg = [sb(f"ktg{i}", [128, KV], BF16, scope5) for i in range(2)]
    vg = [sb(f"vg{i}", [128, KVG * 130], BF16, scope5) for i in range(2)]
    twb = [sb(f"twb{i}", [128, WTC], F32, scope5) for i in range(2)]
    tb = [sb(f"tb{i}", [128, WTC], BF16, scope5) for i in range(2)]
    tbf = [sb(f"tbf{i}", [128, WTC], BF16, scope5) for i in range(2)]

    tern_ld_sem = ["dwb0", "dwb1"]
    tst_tok = [0, 0]
    tern_jobs = []
    for wi, (wsrc, wtsc) in enumerate(((w_up, wup_t_sc), (w_down, wdn_t_sc))):
        for it in range(len(wtiles(wsrc, WTC))):
            tern_jobs.append((wi, wsrc, wtsc, it))

    def emit_tern_job(j):
        wi, wsrc, wtsc, it = tern_jobs[j]
        bi = j % 2
        wl = [("v", thr_toks[1])]
        if j >= 2:
            wl.append((["dwo0", "dwo1"][bi], tst_tok[bi]))
        P.op("sync", lambda e, it=it, b=bi, w=wsrc:
             e.dma_start(out=bass.AP(twb[b], 0, wtiles(w, WTC)[it][1]),
                         in_=wtiles(w, WTC)[it][0]),
             inc=(tern_ld_sem[bi], 16), waits=wl)
        tld = P.tok(tern_ld_sem[bi])
        P.op("vector", lambda p, b=bi, wi=wi: p.tensor_scalar(
            tb[b][:, :], twb[b][:, :], thr_t[wi][:, 0:1], None, ALU.is_gt),
            waits=[(tern_ld_sem[bi], tld)])
        P.op("vector", lambda p, b=bi, wi=wi: p.tensor_scalar(
            tbf[b][:, :], twb[b][:, :], nthr_t[wi][:, 0:1], None, ALU.is_lt))
        tsb_ = P.op("vector", lambda p, b=bi: p.tensor_sub(
            tbf[b][:, :], tb[b][:, :], tbf[b][:, :]), inc=("v", 1))
        P.op("gpsimd", lambda e, it=it, b=bi, wt=wtsc:
             e.dma_start(out=wtiles(wt, WTC)[it][0],
                         in_=bass.AP(tbf[b], 0, wtiles(wt, WTC)[it][1])),
             inc=(["dwo0", "dwo1"][bi], 16), waits=[("v", tsb_)])
        tst_tok[bi] = P.tok(["dwo0", "dwo1"][bi])

    rt_jobs = []
    for g in range(G):
        for fc in range(F // WTC):
            rt_jobs.append((wup_t_sc, wupT_sc, fc * WTC, g))
    for fg in range(FG):
        rt_jobs.append((wdn_t_sc, wdnT_sc, 0, fg))
    rtb = [sb(f"rtb{i}", [128, WTC], BF16, scope5) for i in range(2)]
    rt_ld_sem = ["drb0", "drb1"]
    rt_st_tok = [0, 0]
    t_tern_all = [None]

    def emit_rt_job(j):
        src_sc, dst_sc, r0, g = rt_jobs[j]
        bi = j % 2
        if t_tern_all[0] is None:
            t_tern_all[0] = [("dwo0", P.tok("dwo0")), ("dwo1", P.tok("dwo1"))]
        wl = list(t_tern_all[0])
        if j >= 2:
            wl.append((["dro0", "dro1"][bi], rt_st_tok[bi]))
        P.op("sync", lambda e, b=bi, s=src_sc, r0=r0, g=g:
             e.dma_start(out=rtb[b][:, :],
                         in_=s[r0:r0 + WTC, g * 128:(g + 1) * 128],
                         transpose=True),
             inc=(rt_ld_sem[bi], 16), waits=wl)
        tld = P.tok(rt_ld_sem[bi])
        P.op("gpsimd", lambda e, b=bi, d=dst_sc, r0=r0, g=g:
             e.dma_start(out=d[g * 128:(g + 1) * 128, r0:r0 + WTC],
                         in_=rtb[b][:, :]),
             inc=(["dro0", "dro1"][bi], 16), waits=[(rt_ld_sem[bi], tld)])
        rt_st_tok[bi] = P.tok(["dro0", "dro1"][bi])

    n_tern = len(tern_jobs)
    n_bg = n_tern + len(rt_jobs)
    tern_per_gqt = (n_bg + G * NQT - 1) // (G * NQT)

    P.wait("sync", "t", t_qproj_t)
    kv_sem = ["dk0", "dk1"]
    exp_tok = [0, 0]
    av_tok = [0, 0]
    lastexp = 0
    lastnorm = 0
    kvload_tok = [0, 0]
    pending_norm = [None]
    pEF, pGH = pW[2], pW[3]
    P.rawcopy_tok = 0
    P.lastnorm_new = 0
    ti = 0
    for g in range(G):
        gb = g % 2
        wl = t_kt_done + t_vp_done
        if g >= 2:
            wl.append(("t", kvload_tok[gb]))
        P.op("sync", lambda e, g=g, b=gb: e.dma_start(
            out=ktg[b][:, :], in_=kt_sc[g * 128:(g + 1) * 128, :]),
            inc=(kv_sem[gb], 16), waits=wl)
        P.op("sync", lambda e, g=g, b=gb: e.dma_start(
            out=vg[b][:, :],
            in_=bass.AP(vp_sc, 2 * g * 65,
                        [[VROW, 128], [128 * VROW, KVG], [1, 130]])),
            inc=(kv_sem[gb], 16))
        t_kv = P.tok(kv_sem[gb])

        for qt in range(NQT):
            q0 = qt * QTS

            def emit_qk(kc, g=g, qt=qt, q0=q0, gb=gb, t_kv=t_kv):
                pq = pW[kc % 2]
                wme = [("s", t_qproj_s), (kv_sem[gb], t_kv)]
                if kc >= 2:
                    wme.append(("s", exp_tok[kc % 2]))
                else:
                    wme.append(("s", lastexp_l[0]))
                P.op("tensor", lambda t, kc=kc, pq=pq: t.matmul(
                    pq[:, 0:QTS], ktg[gb][0:64, kc * 128:(kc + 1) * 128],
                    qTall[0:64, g * NQ + q0: g * NQ + q0 + QTS],
                    start=True, stop=True),
                    waits=wme)
                P.op("tensor", lambda t, kc=kc, pq=pq: t.matmul(
                    pq[:, 512:512 + QTS], ktg[gb][64:128, kc * 128:(kc + 1) * 128],
                    qTall[64:128, g * NQ + q0: g * NQ + q0 + QTS],
                    start=True, stop=True),
                    inc=("t", 1))
                tqk_l[kc % 2] = P.tok("t")

            def emit_exp_av(kc, g=g, gb=gb):
                pq = pW[kc % 2]
                we = [("t", tqk_l[kc % 2])]
                if kc >= 2:
                    we.append(("t", av_tok[kc % 2]))
                if QTS == 512:
                    te = P.op("scalar", lambda s, kc=kc, pq=pq: s.activation(
                        PTW[kc % 2][:, :], pq[:, :], AF.Exp,
                        bias=0.0, scale=SCALE), inc=("s", 1), waits=we)
                else:
                    P.op("scalar", lambda s, kc=kc, pq=pq: s.activation(
                        PTW[kc % 2][:, 0:QTS], pq[:, 0:QTS], AF.Exp,
                        bias=0.0, scale=SCALE), waits=we)
                    te = P.op("scalar", lambda s, kc=kc, pq=pq: s.activation(
                        PTW[kc % 2][:, 512:512 + QTS], pq[:, 512:512 + QTS],
                        AF.Exp, bias=0.0, scale=SCALE), inc=("s", 1))
                exp_tok[kc % 2] = te
                wa = [("s", te)]
                if kc == 0:
                    wa.append(("v", P.rawcopy_tok))
                P.op("tensor", lambda t, kc=kc: t.matmul(
                    pE[0:65, 0:QTS],
                    vg[gb][:, kc * 130: kc * 130 + 65],
                    PTW[kc % 2][:, 0:QTS],
                    start=(kc == 0), stop=(kc == KVG - 1)),
                    waits=wa)
                av_tok[kc % 2] = P.op("tensor", lambda t, kc=kc: t.matmul(
                    pF[0:65, 0:QTS],
                    vg[gb][:, kc * 130 + 65: kc * 130 + 130],
                    PTW[kc % 2][:, 512:512 + QTS],
                    start=(kc == 0), stop=(kc == KVG - 1)),
                    inc=("t", 1))

            tqk_l = [0, 0]
            lastexp_l = [lastexp]
            it_target = (g * NQT + qt + 1) * tern_per_gqt
            for kc in range(KVG):
                emit_qk(kc)
                if kc == 1 and pending_norm[0] is not None:
                    pending_norm[0]()
                    pending_norm[0] = None
                if kc == 3:
                    while ti < min(n_bg, it_target):
                        if ti < n_tern:
                            emit_tern_job(ti)
                        else:
                            emit_rt_job(ti - n_tern)
                        ti += 1
                if kc >= 1:
                    emit_exp_av(kc - 1)
            emit_exp_av(KVG - 1)
            lastexp = P.tok("s")
            t_av = P.tok("t")
            if qt == NQT - 1:
                kvload_tok[gb] = t_av

            def norm_block(g=g, q0=q0, t_av=t_av):
                def dual(x, p0, np_):
                    tn = x.tensor if isinstance(x, bass.AP) else x
                    off = x.offset if isinstance(x, bass.AP) else 0
                    pitch = tn.shape[1]
                    return bass.AP(tn, off + p0 * pitch,
                                   [[pitch, np_], [512, 2], [1, QTS]])
                trw = P.op("vector", lambda v: v.tensor_copy(
                    dual(rawb, 0, 65), dual(pEF, 0, 65)), inc=("v", 1),
                    waits=[("t", t_av), ("v", P.lastnorm_new)])
                P.rawcopy_tok = trw
                trec = P.op("vector", lambda v: v.tensor_copy(
                    dual(recb, 0, 1), dual(rawb, 64, 1)),
                    inc=("v", 1), waits=[("v", trw)])
                tb0 = P.op("tensor", lambda t: t.matmul(
                    pG[0:64, 0:QTS], ones_row_b[0:1, 0:64], recb[0:1, 0:QTS],
                    start=True, stop=True), waits=[("v", trec)])
                tb1 = P.op("tensor", lambda t: t.matmul(
                    pH[0:64, 0:QTS], ones_row_b[0:1, 0:64],
                    recb[0:1, 512:512 + QTS],
                    start=True, stop=True), inc=("t", 1))
                tbc = P.tok("t")
                tcp = P.op("vector", lambda v: v.tensor_copy(
                    dual(bcsb, 0, 64), dual(pGH, 0, 64)), inc=("v", 1),
                    waits=[("t", tbc)])
                trc2 = P.op("vector", lambda v: v.reciprocal(
                    dual(tmpn, 0, 64), dual(bcsb, 0, 64)),
                    inc=("v", 1), waits=[("v", tcp)])
                tml = P.op("vector", lambda v: v.tensor_tensor(
                    dual(bcsb, 0, 64), dual(rawb, 0, 64), dual(tmpn, 0, 64),
                    ALU.mult),
                    inc=("v", 1), waits=[("v", trc2)])
                P.op("vector", lambda v: v.tensor_scalar(
                    attnT[0:64, g * NQ + q0: g * NQ + q0 + QTS],
                    bcsb[0:64, 0:QTS], vb_sb[0:64, g:g + 1], None, ALU.add),
                    waits=[("v", tml)])
                tnm = P.op("vector", lambda v: v.tensor_scalar(
                    attnT[64:128, g * NQ + q0: g * NQ + q0 + QTS],
                    bcsb[0:64, 512:512 + QTS], vb_sb[64:128, g:g + 1],
                    None, ALU.add), inc=("v", 1))
                P.lastnorm_new = tnm

            pending_norm[0] = norm_block
    if pending_norm[0] is not None:
        pending_norm[0]()
        pending_norm[0] = None
    while ti < n_bg:
        if ti < n_tern:
            emit_tern_job(ti)
        else:
            emit_rt_job(ti - n_tern)
        ti += 1
    t_attn_v = P.tok("v")
    t_attn_t = P.tok("t")
    t_attn_s = P.tok("s")
    t_tern_done = [("dro0", P.tok("dro0")), ("dro1", P.tok("dro1"))]
    scope5.close()

    scopeQ.close()
    P.fence()
    scope7 = ExitStack()
    wupT = sb("wupT", [128, G * F], BF16, scope7)
    owT = sb("owT", [128, G * D], BF16, scope7)
    xnT7 = sb("xnT7", [128, G * QTS], BF16, scope7)
    hT = sb("hT", [128, FG * QTS], BF16, scope7)
    h2s = [sb(f"h2s{i}", [128, QTS], BF16, scope7) for i in range(4)]
    wdb = [sb(f"wdb{i}", [128, D], BF16, scope7) for i in range(2)]
    r1row = sb("r1row", [1, NQ], BF16, scope7)
    bcf1a = sb("bcf1a", [128, NQ], BF16, scope7)

    P.wait("sync", "t", t_attn_t)
    for g in range(G):
        P.op("sync", lambda e, g=g: e.dma_start(
            out=wupT[:, g * F:(g + 1) * F],
            in_=wupT_sc[g * 128:(g + 1) * 128, 0:F]),
            inc=("dft", 16), waits=list(t_tern_done))
        P.op("sync", lambda e, g=g: e.dma_start(
            out=owT[:, g * D:(g + 1) * D],
            in_=wo_bf[0:D, g * 128:(g + 1) * 128], transpose=True),
            inc=("dft", 16), waits=[("gcast", t_cast)])
    t_fw = P.tok("dft")

    xq2_sem = ["df0", "df1"]
    wd_sem = ["dd0", "dd1"]
    xo_sem = ["do0", "do1"]
    gelu_tok = [0, 0, 0, 0]
    h2_tok = [0, 0, 0, 0]
    fevac = [0, 0]
    fyst_tok = [0, 0]
    xnorm_tok = 0
    wdld_tok = [0, 0]
    prev_l2 = 0

    P.op("gpsimd", lambda e: e.dma_start(
        out=r1row[0:1, :], in_=r1_sc[0:NQ, 0:1].rearrange("a c -> c a")),
        inc=("dr1r", 16), waits=[t_r1done])
    t_r1r = P.tok("dr1r")
    bc1_prev = 0
    for qt in range(NQT):
        tbm1 = P.op("tensor", lambda t, qt=qt: t.matmul(
            pH[:, 0:QTS], ones_row_b[0:1, :],
            r1row[0:1, qt * QTS:(qt + 1) * QTS], start=True, stop=True),
            inc=("t", 1), waits=[("dr1r", t_r1r), ("v", bc1_prev)])
        bc1_prev = P.op("vector", lambda v, qt=qt: v.tensor_copy(
            bcf1a[:, qt * QTS:(qt + 1) * QTS], pH[:, 0:QTS]),
            inc=("v", 1), waits=[("t", tbm1)])
    t_bc1a = bc1_prev

    for qt in range(NQT):
        q0 = qt * QTS
        wl = list(t_lnfm)
        if qt >= 1:
            wl.append(("v", xnorm_tok))
        P.op("sync", lambda e, q0=q0: e.dma_start(
            out=bass.AP(xnT7, 0, [[G * QTS, 128], [QTS, G], [1, QTS]]),
            in_=xn_own_fm_sc[0:D, q0:q0 + QTS].rearrange(
                "(g p) c -> p g c", p=128)),
            inc=(xq2_sem[qt % 2], 16), waits=wl)
        t_x7 = P.tok(xq2_sem[qt % 2])
        for g in range(G):
            wv = [(xq2_sem[qt % 2], t_x7)] if g == 0 else []
            if g == 0:
                wv.append(("v", t_bc1a))
            P.op("vector", lambda v, g=g, q0=q0: v.tensor_mul(
                xnT7[:, g * QTS:(g + 1) * QTS], xnT7[:, g * QTS:(g + 1) * QTS],
                bcf1a[:, q0:q0 + QTS]),
                waits=wv, inc=("v", 1) if g == G - 1 else None)
        xnorm_tok = P.tok("v")

        h2q = []

        def emit_h2m(item, qt=qt):
            fc, tsq = item
            wm = [("v", tsq)]
            if fc == 0:
                wm.append(("s", prev_l2))
            h2_tok[fc % 4] = P.op("tensor", lambda t, fc=fc: t.matmul(
                pG[0:1, 0:QTS], ones_bf[:, :], h2s[fc % 4][:, :],
                start=(fc == 0), stop=(fc == FG - 1)),
                inc=("t", 1), waits=wm)

        for fc in range(FG):
            pb = ps[fc % 4]
            wme = [("v", xnorm_tok), ("dft", t_fw)]
            if fc >= 4 or qt > 0:
                wme.append(("s", gelu_tok[fc % 4]))
            if qt > 0 and fc < 4:
                wme += [("v", fevac[0]), ("v", fevac[1])]
            for dc in range(G):
                P.op("tensor", lambda t, fc=fc, dc=dc, pb=pb: t.matmul(
                    pb[:, 0:QTS],
                    wupT[:, dc * F + fc * 128: dc * F + fc * 128 + 128],
                    xnT7[:, dc * QTS:(dc + 1) * QTS],
                    start=(dc == 0), stop=(dc == G - 1)),
                    waits=wme if dc == 0 else (),
                    inc=("t", 1) if dc == G - 1 else None)
            tmm = P.tok("t")
            tg = P.op("scalar", lambda s, fc=fc, pb=pb: s.activation(
                hT[:, fc * QTS:(fc + 1) * QTS], pb[:, 0:QTS], AF.Gelu),
                inc=("s", 1), waits=[("t", tmm)])
            gelu_tok[fc % 4] = tg
            wv = [("s", tg)]
            if fc >= 4 or qt > 0:
                wv.append(("t", h2_tok[fc % 4]))
            P.op("vector", lambda v, fc=fc: v.tensor_mul(
                h2s[fc % 4][:, :], hT[:, fc * QTS:(fc + 1) * QTS],
                hT[:, fc * QTS:(fc + 1) * QTS]), inc=("v", 1), waits=wv)
            h2q.append((fc, P.tok("v")))
            if fc >= 2:
                emit_h2m(h2q.pop(0))
        while h2q:
            emit_h2m(h2q.pop(0))
        t_h = P.tok("s")
        t_s2 = P.tok("t")
        P.op("scalar", lambda s: s.activation(den[0:1, 0:QTS], pG[0:1, 0:QTS],
                                              AF.Ln, bias=0.0, scale=1.0),
             inc=("s", 1), waits=[("t", t_s2)])
        t_l2 = P.tok("s")
        prev_l2 = t_l2
        P.op("scalar", lambda s: s.activation(den[0:1, 512:512 + QTS],
                                              den[0:1, 0:QTS], AF.Exp,
                                              bias=0.0, scale=0.5),
             inc=("s", 1), waits=[("s", t_l2)])
        t_e2 = P.tok("s")
        tq1 = P.op("vector", lambda v: v.tensor_scalar(
            den[0:1, 0:QTS], den[0:1, 512:512 + QTS],
            1.0 / float(np.sqrt(F)), 1e-8, ALU.mult, ALU.add),
            inc=("v", 1), waits=[("s", t_e2)])
        tq2 = P.op("vector", lambda v: v.reciprocal(rec[0:1, 0:QTS],
                                                    den[0:1, 0:QTS]),
                   inc=("v", 1), waits=[("v", tq1)])
        P.op("vector", lambda v: v.tensor_copy(recb[0:1, 0:QTS], rec[0:1, 0:QTS]),
             inc=("v", 1), waits=[("v", tq2)])
        t_r2 = P.tok("v")
        tbm2 = P.op("tensor", lambda t: t.matmul(
            pH[:, 0:QTS], ones_row_b[0:1, :], recb[0:1, 0:QTS],
            start=True, stop=True), inc=("t", 1), waits=[("v", t_r2)])
        P.op("vector", lambda v: v.tensor_copy(bcf[:, 0:QTS], pH[:, 0:QTS]),
             inc=("v", 1), waits=[("t", tbm2)])
        t_bc2 = P.tok("v")
        wme = [("v", t_attn_v), ("dft", t_fw), ("s", t_h), ("v", t_bc2)]
        if qt > 0:
            wme += [("v", fevac[0]), ("v", fevac[1])]
        for qs in range(QSUB):
            for dc in range(G):
                P.op("tensor", lambda t, dc=dc, q0=q0, qs=qs: t.matmul(
                    ps[qs * 2][:, 0:512],
                    attnT[:, dc * NQ + q0 + qs * 128: dc * NQ + q0 + qs * 128 + 128],
                    owT[:, dc * D: dc * D + 512],
                    start=(dc == 0), stop=False),
                    waits=wme if (qs == 0 and dc == 0) else ())
                P.op("tensor", lambda t, dc=dc, q0=q0, qs=qs: t.matmul(
                    ps[qs * 2 + 1][:, 0:512],
                    attnT[:, dc * NQ + q0 + qs * 128: dc * NQ + q0 + qs * 128 + 128],
                    owT[:, dc * D + 512: dc * D + 1024],
                    start=(dc == 0), stop=False),
                    inc=("t", 1) if dc == G - 1 else None)
        for fc in range(FG):
            bi = fc % 2
            wl = list(t_tern_done)
            if fc >= 2 or qt > 0:
                wl.append(("t", wdld_tok[bi]))
            P.op("sync", lambda e, fc=fc, b=bi: e.dma_start(
                out=wdb[b][:, :],
                in_=wdnT_sc[fc * 128:(fc + 1) * 128, 0:D]),
                inc=(wd_sem[bi], 16), waits=wl)
            t_wd = P.tok(wd_sem[bi])
            ths = P.op("vector", lambda v, fc=fc: v.tensor_mul(
                hT[:, fc * QTS:(fc + 1) * QTS], hT[:, fc * QTS:(fc + 1) * QTS],
                bcf[:, 0:QTS]),
                inc=("v", 1), waits=[("v", t_bc2)])
            wme2 = [(wd_sem[bi], t_wd), ("v", ths)]
            for qs in range(QSUB):
                P.op("tensor", lambda t, fc=fc, qs=qs, b=bi: t.matmul(
                    ps[qs * 2][:, 0:512],
                    hT[:, fc * QTS + qs * 128: fc * QTS + qs * 128 + 128],
                    wdb[b][:, 0:512],
                    start=False, stop=(fc == FG - 1)),
                    waits=wme2 if qs == 0 else ())
                P.op("tensor", lambda t, fc=fc, qs=qs, b=bi: t.matmul(
                    ps[qs * 2 + 1][:, 0:512],
                    hT[:, fc * QTS + qs * 128: fc * QTS + qs * 128 + 128],
                    wdb[b][:, 512:1024],
                    start=False, stop=(fc == FG - 1)),
                    inc=("t", 1) if qs == QSUB - 1 else None)
            wdld_tok[bi] = P.tok("t")
        t_dn = P.tok("t")
        for qs in range(QSUB):
            bi = qs % 2
            P.op("sync", lambda e, q0=q0, qs=qs, b=bi: e.dma_start(
                out=ubuf[b][:, :],
                in_=x_own[q0 + qs * 128: q0 + qs * 128 + 128, :]),
                inc=(xo_sem[bi], 16),
                waits=[("v", fevac[bi])] if (qt > 0 or qs > 1) else [])
            t_yl = P.tok(xo_sem[bi])
            P.op("vector", lambda v, qs=qs, b=bi: v.tensor_add(
                ysb[b][:, 0:512], ps[qs * 2][:, 0:512], obc[:, 0:512]),
                waits=[("t", t_dn), (xo_sem[bi], t_yl), ("dy", fyst_tok[bi])])
            P.op("vector", lambda v, qs=qs, b=bi: v.tensor_add(
                ysb[b][:, 512:1024], ps[qs * 2 + 1][:, 0:512], obc[:, 512:1024]),
                inc=("v", 1))
            ta2 = P.tok("v")
            fevac[bi] = P.op("vector", lambda v, b=bi: v.tensor_add(
                ysb[b][:, :], ysb[b][:, :], ubuf[b][:, :]), inc=("v", 1),
                waits=[("v", ta2)])
            P.op("gpsimd", lambda e, q0=q0, qs=qs, b=bi: e.dma_start(
                out=y[q0 + qs * 128: q0 + qs * 128 + 128, :], in_=ysb[b][:, :]),
                inc=("dy", 16), waits=[("v", fevac[bi])])
            fyst_tok[bi] = P.tok("dy")
    scope7.close()
    scopeB.close()

    P.wait("sync", "dy", P.tok("dy"))
    P.wait("sync", "gcast", P.tok("gcast"))

    with nc.Block() as blk:
        P.emit_all(blk)
    es.close()
    return nc


def wtiles(w, WTC):
    R, C = w.shape
    out = []
    if C >= WTC:
        assert C % WTC == 0 and R % 128 == 0
        for r in range(R // 128):
            for cs in range(C // WTC):
                out.append((w[r * 128:(r + 1) * 128, cs * WTC:(cs + 1) * WTC],
                            [[WTC, 128], [1, WTC]]))
    else:
        rpt = 128 * (WTC // C)
        assert R % rpt == 0
        na = rpt // 128
        for r in range(R // rpt):
            out.append((w[r * rpt:(r + 1) * rpt, :].rearrange(
                "(a p) c -> p a c", p=128),
                [[WTC, 128], [C, na], [1, C]]))
    return out


def qT_all_slice(qTall, og, q0, QTS, NQ):
    return qTall[:, og * NQ + q0: og * NQ + q0 + QTS]


def kernel(**inputs):
    cfg = full_cfg()
    S, NQ, D = cfg["S"], cfg["NQ"], cfg["D"]
    x = np.ascontiguousarray(np.asarray(inputs["L0_attn_x"], dtype=np.float32))
    B = x.shape[0]
    ncores = 8
    cpb = ncores // B

    f32c = lambda a: np.ascontiguousarray(np.asarray(a, dtype=np.float32))
    common = dict(
        q_w=f32c(inputs["L0_attn_q_w"]),
        q_b=f32c(inputs["L0_attn_q_b"]).reshape(D, 1),
        k_w=f32c(inputs["L0_attn_k_w"]),
        k_b=f32c(inputs["L0_attn_k_b"]).reshape(D, 1),
        v_w=f32c(inputs["L0_attn_v_w"]),
        v_b=f32c(inputs["L0_attn_v_b"]).reshape(D, 1),
        out_w=f32c(inputs["L0_attn_out_w"]),
        out_b=f32c(inputs["L0_attn_out_b"]).reshape(1, D),
        conv_w=f32c(inputs["L0_attn_conv_w"]).reshape(D, 3),
        conv_b=f32c(inputs["L0_attn_conv_b"]).reshape(D, 1),
        w_up=f32c(inputs["L0_ffn_w_up"]),
        w_down=f32c(inputs["L0_ffn_w_down"]),
        ident=np.eye(128, dtype=ml_dtypes.bfloat16),
    )
    in_maps = []
    for c in range(ncores):
        b, qc = divmod(c, cpb)
        m = dict(common)
        m["x_ctx"] = np.ascontiguousarray(x[b])
        m["x_own"] = np.ascontiguousarray(x[b, qc * NQ:(qc + 1) * NQ])
        in_maps.append(m)

    nc = build_nc(cfg)
    res = run_bass_kernel_spmd(nc, in_maps, core_ids=list(range(ncores)))
    out = np.empty((B, S, D), dtype=np.float32)
    for c in range(ncores):
        b, qc = divmod(c, cpb)
        out[b, qc * NQ:(qc + 1) * NQ] = res.results[c]["y"]
    return out
